# version 11
# baseline (speedup 1.0000x reference)
from contextlib import ExitStack

import numpy as np
import concourse.bass as bass
import concourse.mybir as mybir
from concourse.bass_isa import ReduceOp
from concourse.bass_utils import run_bass_kernel_spmd

F32 = mybir.dt.float32
BF16 = mybir.dt.bfloat16
ALU = mybir.AluOpType
ACT = mybir.ActivationFunctionType
AXX = mybir.AxisListType.X

D = 1024
NS = 8
TC = 512
T = 4096
NCH = T // TC
QB = 128.0
THR_C = (QB - 1e-5) / QB
INV_D = 1.0 / D
FFN_A = 1.001 / D
FFN_B = 1e-8
FFN_M = 1.001
SSM_A = 1.0 / D
SSM_B = 1e-5
SSM_M = 1.0


class Ctr:
    def __init__(self, labels, strict):
        self.labels = labels
        self.strict = strict
        self.cnt = {}
        self.out = {}

    def inc(self, instr, semname, sems, n=1, label=None):
        c = self.cnt.get(semname, 0) + n
        self.cnt[semname] = c
        if label is not None:
            self.out[label] = (semname, c)
        if instr is not None and n > 0:
            instr.then_inc(sems[semname], n)
        return c

    def mark(self, semname, label):
        self.out[label] = (semname, self.cnt.get(semname, 0))

    def wait(self, eng, label, sems):
        if label not in self.labels:
            if self.strict:
                raise KeyError(f"unknown label {label!r}")
            return
        semname, val = self.labels[label]
        if val > 0:
            eng.wait_ge(sems[semname], val)


DEBUG = False


def _emit(nc, labels, strict):
    C = Ctr(labels, strict)

    xT_h = nc.dram_tensor("xT", [D, T], F32, kind="ExternalInput")
    wffn_h = nc.dram_tensor("wffn", [D, D], F32, kind="ExternalInput")
    win_h = nc.dram_tensor("win", [D, D], F32, kind="ExternalInput")
    cmat_h = nc.dram_tensor("cmat", [D, D], F32, kind="ExternalInput")
    vecs_h = nc.dram_tensor("vecs", [128, 6 * NS], F32, kind="ExternalInput")
    outT_h = nc.dram_tensor("outT", [D, T], F32, kind="ExternalOutput")
    wqs_h = nc.dram_tensor("wq_scr", [D, D], BF16, kind="Internal")
    xT, wffn, win, cmat = xT_h.ap(), wffn_h.ap(), win_h.ap(), cmat_h.ap()
    outT, wq_scr = outT_h.ap(), wqs_h.ap()
    if DEBUG:
        dbg1 = nc.dram_tensor("dbg1", [D, T], F32, kind="ExternalOutput").ap()
        dbg2 = nc.dram_tensor("dbg2", [D, T], F32, kind="ExternalOutput").ap()
        dbgs = nc.dram_tensor("dbgs", [128, 7 * NCH * 4 + 16 + NS], F32,
                              kind="ExternalOutput").ap()
        dbga = nc.dram_tensor("dbga", [128, 8 * TC], F32, kind="ExternalOutput").ap()
        dbgr = nc.dram_tensor("dbgr", [128, NCH * 4], F32, kind="ExternalOutput").ap()

    with ExitStack() as st:
        def sb(name, shape, dt=F32):
            return st.enter_context(nc.sbuf_tensor(name, shape, dt))

        x_sb = [sb(f"x{s}", [128, T]) for s in range(NS)]
        w1 = [sb(f"w1_{s}", [128, D], BF16) for s in range(NS)]
        w2 = [sb(f"w2_{s}", [128, D], BF16) for s in range(NS)]
        bufX = [[sb(f"bx{p}_{s}", [128, TC], BF16) for s in range(NS)]
                for p in range(2)]
        bufH = [sb(f"bh{s}", [128, TC], BF16) for s in range(NS)]
        sqt_t = sb("sqt", [128, TC], BF16)
        sqt = [sqt_t, sqt_t]
        tn1_t = sb("tn1", [128, TC], BF16)
        tn1 = [tn1_t, tn1_t]

        scr1 = sb("scr1", [128, TC]); scr2 = sb("scr2", [128, TC])
        scr3 = sb("scr3", [128, TC])
        b_sb = sb("b_sb", [128, TC]); h_sb = b_sb
        s_rep = sb("s_rep", [128, TC])
        srow = sb("srow", [1, TC]); m2row = sb("m2row", [1, TC])
        mrow = m2row[0:1, :]
        srow_s = srow[0:1, :]
        vrow = scr2[0:1, :]
        qrow = scr1[0:1, :]
        mslots = sb("mslots", [128, NCH * NS])
        sqv = [sqt_t] + [bufH[i] for i in range(1, NS)]
        p2rb = sb("p2rb", [33, TC], BF16)
        murow_b = p2rb[0:1, :]
        m2row_b = p2rb[32:33, :]
        cmp_s = {k: sb(f"cs_{k}", [128, NCH * 4]) for k in ("x", "x1", "x2")}
        cmp_m2 = {k: sb(f"cm_{k}", [128, NCH * 4]) for k in ("x", "x1", "x2")}
        cmp_mu = sb("cmu_1", [128, NCH * 4])
        wstage = sb("wst", [128, TC])
        wsb = bufH[0]
        vct = sb("vct", [128, 6 * NS])
        vc = {k: vct[:, j * NS:(j + 1) * NS] for j, k in
              enumerate(("bin", "la", "bvec", "dd", "gam", "bet"))}
        a_cols = sb("a_cols", [128, NS]); bb_cols = sb("bb_cols", [128, NS])
        nd_colsb = sb("nd_colsb", [128, NS], BF16)
        ws_cols = sb("ws_cols", [128, NS], BF16)
        vcc1 = sb("vcc1", [128, 8]); vcc2 = sb("vcc2", [128, 4])
        dvin = vcc1[:, 0:4]; dvout = vcc1[:, 4:8]; dvt = vcc2[:, 0:4]
        gsl1 = sb("gsl1", [1, NCH * NS])
        gsl2 = gsl1
        p2r = sb("p2r", [33, D], BF16)
        nwrow = [p2r[0:1, e * 128:(e + 1) * 128] for e in range(NS)]
        ndrow = [p2r[32:33, e * 128:(e + 1) * 128] for e in range(NS)]
        ones_bf = sb("ones_bf", [128, 128], BF16)
        ones1_t = sb("ones1", [1, 128])
        ones1 = ones1_t[:]
        onescol_b = ones_bf[:, 0:1]
        carry = sb("carry", [128, NS])
        wsum_t = sb("wsum_t", [128, 1])
        cpk = sb("cpk", [128, 16])
        (mx1, mx2, gred, _unused_wsum, wred, beta_c, inv_t, inv_c, hbeta_c,
         thr1, nthr1, thr2, nthr2) = [cpk[:, j:j + 1] for j in range(13)]
        wsum = wsum_t[:]
        stg_a = cpk[0:1, 13:14]
        stg_b = cpk[0:1, 14:15]

        PB = [st.enter_context(nc.psum_tensor(f"pb{i}", [128, TC], F32))
              for i in range(8)]

        semnames = ["LX", "LW", "GQ", "RQ", "RS", "ST", "V", "S", "T", "G", "F", "RM"]
        sems = {n: st.enter_context(nc.semaphore(n)) for n in semnames}

        def col(tile, j):
            try:
                return tile[:, j:j + 1]
            except Exception:
                return tile

        def bcast(tile, j, n=TC):
            free = int(tile.shape[1])
            return bass.AP(tile, j, [[free, 128], [0, n]])

        def chunk(tile, c):
            return tile[:, c * TC:(c + 1) * TC]

        blk = nc.Block(); blk.__enter__()

        @blk.sync
        def _(sy):
            for c in range(NCH):
                for s in range(NS):
                    sy.dma_start(out=chunk(x_sb[s], c),
                                 in_=xT[s * 128:(s + 1) * 128, c * TC:(c + 1) * TC]
                                 ).then_inc(sems["LX"], 16)
                    C.inc(None, "LX", sems, 16)
                if c >= 1:
                    sy.wait_ge(sems["LX"], 16 * NS * c)
                    i = sy.nop()
                    C.inc(i, "F", sems, 1, label=f"lxf:{c - 1}")
            sy.wait_ge(sems["LX"], 16 * NS * NCH)
            i = sy.nop()
            C.inc(i, "F", sems, 1, label=f"lxf:{NCH - 1}")
            def _wload(k):
                s, hf = k // 2, k % 2
                if k == 0:
                    C.wait(sy, f"p:wr1:{2 * NS - 1}", sems)
                else:
                    C.wait(sy, f"p:wq:{k - 1}", sems)
                sy.dma_start(out=wstage[:],
                             in_=wffn[s * 128:(s + 1) * 128,
                                      hf * TC:(hf + 1) * TC]
                             ).then_inc(sems["LW"], 16)
                C.inc(None, "LW", sems, 16, label=f"p:w2:{k}")

            wl_sched = {4: [0, 1, 2, 3, 4], 5: [5, 6, 7, 8],
                        6: [9, 10, 11, 12], 7: [13, 14, 15]}
            for c in range(NCH):
                C.wait(sy, f"a1:math1:{c}", sems)
                sy.dma_start(out=cmp_m2["x"][:, 4 * c:4 * c + 4],
                             in_=mrow).then_inc(sems["RM"], 16)
                C.inc(None, "RM", sems, 16, label=f"a1:mst:{c}")
                C.wait(sy, f"a1:sr:{c}", sems)
                sy.dma_start(out=cmp_s["x"][:, 4 * c:4 * c + 4],
                             in_=srow_s).then_inc(sems["RQ"], 16)
                C.inc(None, "RQ", sems, 16, label=f"a1:sst:{c}")
                for k in wl_sched.get(c, []):
                    _wload(k)

        @blk.gpsimd
        def _(gp):
            gp.dma_start(out=vct[:], in_=vecs_h.ap()[:]).then_inc(sems["GQ"], 16)
            C.inc(None, "GQ", sems, 16)
            gp.wait_ge(sems["GQ"], C.cnt["GQ"])
            i = gp.nop()
            C.inc(i, "F", sems, 1, label="vecf")
            for k in range(2 * NS):
                s, hf = k // 2, k % 2
                if k >= 1:
                    C.wait(gp, f"p:wr1:{k - 1}", sems)
                gp.dma_start(out=wstage[:],
                             in_=wffn[s * 128:(s + 1) * 128,
                                      hf * TC:(hf + 1) * TC]
                             ).then_inc(sems["GQ"], 16)
                C.inc(None, "GQ", sems, 16, label=f"p:w1:{k}")
            for s in range(NS):
                C.wait(gp, f"p:wq:{2 * s + 1}", sems)
                gp.dma_start(out=wq_scr[s * 128:(s + 1) * 128, :],
                             in_=w1[s][:]).then_inc(sems["GQ"], 16)
                C.inc(None, "GQ", sems, 16, label=f"p:wqs:{s}")
            for s in range(NS):
                gp.dma_start(out=w2[s][:], in_=win[s * 128:(s + 1) * 128, :]
                             ).then_inc(sems["GQ"], 16)
                C.inc(None, "GQ", sems, 16)
            gp.wait_ge(sems["GQ"], C.cnt["GQ"])
            i = gp.nop()
            C.inc(i, "F", sems, 1, label="lwinf")
            C.wait(gp, "a1:slots", sems)
            i = gp.tensor_reduce(gsl1[0:1, 0:1], mx1[:],
                                 mybir.AxisListType.XYZWC, ALU.max)
            C.inc(i, "G", sems, 1, label="a1:gmax")
        @blk.vector
        def _(v):
            i = v.memset(carry[:], 0.0)
            C.inc(i, "V", sems, 1)
            i = v.memset(ones_bf[:], 1.0)
            C.inc(i, "V", sems, 1, label="p:ones")
            v.memset(ones1_t[:], 1.0)
            i = v.memset(cpk[:, 15:16], 1.0)
            C.inc(i, "V", sems, 1, label="p:ones1")
            v.wait_ge(sems["V"], C.cnt["V"])

            def _wpass1(k):
                C.wait(v, f"p:w1:{k}", sems)
                if k == 0:
                    i = v.tensor_reduce(wsum[:], wstage[:], AXX, ALU.add,
                                        apply_absolute_value=True)
                    nv = C.inc(i, "V", sems, 1, label=f"p:wr1:{k}")
                    v.wait_ge(sems["V"], nv)
                else:
                    i = v.tensor_reduce(wred[:], wstage[:], AXX, ALU.add,
                                        apply_absolute_value=True)
                    nv = C.inc(i, "V", sems, 1)
                    v.wait_ge(sems["V"], nv)
                    i = v.tensor_tensor(wsum[:], wsum[:], wred[:], ALU.add)
                    nv = C.inc(i, "V", sems, 1, label=f"p:wr1:{k}")
                    v.wait_ge(sems["V"], nv)

            C.wait(v, "p:exp", sems)
            v.tensor_scalar(a_cols[:], a_cols[:], -1.0, None, ALU.mult)
            C.wait(v, "vecf", sems)
            v.tensor_tensor(bb_cols[:], vc["bin"][:], vc["bvec"][:], ALU.mult)
            i = v.tensor_scalar(nd_colsb[:], vc["dd"][:], -1.0, None, ALU.mult)
            C.inc(i, "V", sems, 1, label="p:vecd")
            for c in range(NCH + 2):
                if c >= 2:
                    cc = c - 2
                    C.wait(v, f"a1:replmu:{cc}", sems)
                    C.wait(v, f"a1:repls:{cc}", sems)
                    pm, ps_ = PB[2 + (cc % 2) * 2], PB[3 + (cc % 2) * 2]
                    for s in range(NS):
                        v.tensor_tensor(tn1[s % 2][:], chunk(x_sb[s], cc),
                                        pm[:], ALU.subtract)
                        v.tensor_tensor(tn1[s % 2][:], tn1[s % 2][:], ps_[:],
                                        ALU.mult)
                        v.tensor_scalar(tn1[s % 2][:], tn1[s % 2][:],
                                        col(vc["gam"], s), col(vc["bet"], s),
                                        ALU.mult, ALU.add)
                        k = cc * NS + s
                        i = v.tensor_reduce(mslots[:, k:k + 1], tn1[s % 2][:],
                                            AXX, ALU.max,
                                            apply_absolute_value=True)
                        C.inc(i, "V", sems, 1, label=f"a1:sl:{cc}:{s}")
                if c <= 3:
                    for k in range(4 * c, 4 * c + 4):
                        _wpass1(k)
                if 1 <= c <= NCH:
                    cc = c - 1
                    C.wait(v, f"a1:sg:{cc}", sems)
                    if cc >= 1:
                        C.wait(v, f"a1:sr:{cc - 1}", sems)
                    i = v.reciprocal(dvt, dvin)
                    nv = C.inc(i, "V", sems, 1)
                    v.wait_ge(sems["V"], nv)
                    i = v.tensor_scalar(dvout, dvt, 1.0, None, ALU.mult)
                    C.inc(i, "V", sems, 1, label=f"a1:rc:{cc}")
                if c == 4:
                    def _ss(i):
                        nv = C.inc(i, "V", sems, 1)
                        v.wait_ge(sems["V"], nv)
                        return nv
                    C.wait(v, "p:wsmm", sems)
                    _ss(v.tensor_copy(wred[0:1, 0:1], PB[7][0:1, 2:3]))
                    _ss(v.tensor_scalar(beta_c[0:1, :], wred[0:1, 0:1],
                                        1.0 / (D * D), None, ALU.mult))
                    _ss(v.tensor_scalar(stg_b, beta_c[0:1, :], 0.5, None,
                                        ALU.mult))
                    _ss(v.tensor_scalar(inv_t[0:1, :], beta_c[0:1, :], 1e-5,
                                        None, ALU.add))
                    _ss(v.reciprocal(mx2[0:1, :], inv_t[0:1, :]))
                    i = v.tensor_scalar(stg_a, mx2[0:1, :], 1.0, None, ALU.mult)
                    nv = C.inc(i, "V", sems, 1, label="p:invrow")
                    v.wait_ge(sems["V"], nv)
                    C.wait(v, "p:invmm", sems)
                    i = v.tensor_copy(cpk[:, 7:9], PB[7][:, 0:2])
                    nv = C.inc(i, "V", sems, 1, label="p:inv")
                    v.wait_ge(sems["V"], nv)
                if 5 <= c <= 8:
                    for k in range(4 * (c - 5), 4 * (c - 4)):
                        C.wait(v, f"p:w2:{k}", sems)
                        v.tensor_scalar(wstage[:], wstage[:], col(inv_c, 0),
                                        None, ALU.mult)
                        half = w1[k // 2][:, (k % 2) * TC:(k % 2 + 1) * TC]
                        v.tensor_scalar(wsb[:], wstage[:], -0.5, None, ALU.is_le)
                        v.tensor_scalar(half, wstage[:], 0.5, None, ALU.is_ge)
                        i = v.tensor_tensor(half, half, wsb[:], ALU.subtract)
                        C.inc(i, "V", sems, 1, label=f"p:wq:{k}")
                if c <= NCH - 1:
                    C.wait(v, f"a1:stat:{c}", sems)
                    if c >= 1:
                        C.wait(v, f"a1:replmu:{c - 1}", sems)
                        C.wait(v, f"a1:mst:{c - 1}", sems)
                    v.tensor_scalar(mrow, PB[0][0:1, :], INV_D, None, ALU.mult)
                    i = v.tensor_scalar(vrow, PB[1][0:1, :], FFN_A, FFN_B,
                                        ALU.mult, ALU.add)
                    nv = C.inc(i, "V", sems, 1)
                    v.wait_ge(sems["V"], nv)
                    i = v.scalar_tensor_tensor(qrow, mrow, FFN_M, mrow,
                                               ALU.mult, ALU.mult)
                    nv = C.inc(i, "V", sems, 1)
                    v.wait_ge(sems["V"], nv)
                    i = v.tensor_tensor(vrow, vrow, qrow, ALU.subtract)
                    C.inc(i, "V", sems, 1, label=f"a1:math1:{c}")
            v.wait_ge(sems["V"], C.cnt["V"])
            i = v.tensor_reduce(mx1[:], mslots[:], AXX, ALU.max)
            nv = C.inc(i, "V", sems, 1, label="a1:slots")
            v.wait_ge(sems["V"], nv)
            C.wait(v, "a1:gmax", sems)
            i = v.tensor_scalar(stg_a, gsl1[0:1, 0:1], THR_C, None, ALU.mult)
            nv = C.inc(i, "V", sems, 1)
            v.wait_ge(sems["V"], nv)
            i = v.tensor_scalar(stg_b, stg_a, -1.0, None, ALU.mult)
            nv = C.inc(i, "V", sems, 1, label="thr1v")
            v.wait_ge(sems["V"], nv)

        @blk.scalar
        def _(sc):
            C.wait(sc, "vecf", sems)
            i = sc.activation(a_cols[:], vc["la"][:], ACT.Exp)
            C.inc(i, "S", sems, 1, label="p:exp")

            def _casts(c):
                C.wait(sc, f"lxf:{c}", sems)
                for s in range(NS):
                    if c >= 2:
                        C.wait(sc, f"a1:statmm:{c - 2}:{s}", sems)
                    sc.copy(bufX[c % 2][s][:], chunk(x_sb[s], c))
                    if c >= 1:
                        C.wait(sc, f"a1:statmm:{c - 1}:{s}", sems)
                    i = sc.activation(sqv[s][:], bufX[c % 2][s][:],
                                      ACT.Square)
                    C.inc(i, "S", sems, 1, label=f"a1:cast:{c}:{s}")

            def _dance_back(cc):
                C.wait(sc, f"a1:rc:{cc}", sems)
                if cc >= 1:
                    C.wait(sc, f"a1:repls:{cc - 1}", sems)
                    C.wait(sc, f"a1:sst:{cc - 1}", sems)
                sc.dma_start(out=srow_s, in_=dvout,
                             allow_slow_non_contiguous=True
                             ).then_inc(sems["RS"], 16)
                rs = C.inc(None, "RS", sems, 16)
                sc.wait_ge(sems["RS"], rs)
                i = sc.nop()
                C.inc(i, "S", sems, 1, label=f"a1:sr:{cc}")

            _casts(0)
            for c in range(NCH):
                if c + 1 <= NCH - 1:
                    _casts(c + 1)
                if c >= 1:
                    _dance_back(c - 1)
                C.wait(sc, f"a1:math1:{c}", sems)
                i = sc.activation(vrow, vrow, ACT.Sqrt)
                ns = C.inc(i, "S", sems, 1)
                sc.wait_ge(sems["S"], ns)
                sc.dma_start(out=dvin, in_=vrow,
                             allow_slow_non_contiguous=True
                             ).then_inc(sems["RS"], 16)
                rs = C.inc(None, "RS", sems, 16)
                sc.wait_ge(sems["RS"], rs)
                i = sc.nop()
                C.inc(i, "S", sems, 1, label=f"a1:sg:{c}")
            _dance_back(NCH - 1)

        @blk.tensor
        def _(t):
            C.wait(t, "p:ones", sems)
            C.wait(t, "p:ones1", sems)
            for c in range(NCH + 2):
                if c >= 2 and c - 2 <= NCH - 1:
                    cc = c - 2
                    C.wait(t, f"a1:sr:{cc}", sems)
                    if cc >= 2:
                        C.wait(t, f"a1:sl:{cc - 2}:{NS - 1}", sems)
                    i = t.matmul(PB[3 + (cc % 2) * 2][:], ones1, srow_s,
                                 start=True, stop=True)
                    C.inc(i, "T", sems, 1, label=f"a1:repls:{cc}")
                if c <= NCH - 1:
                    for s in range(NS):
                        C.wait(t, f"a1:cast:{c}:{s}", sems)
                        if c >= 1 and s == 0:
                            C.wait(t, f"a1:math1:{c - 1}", sems)
                        t.matmul(PB[0][:], ones_bf[:], bufX[c % 2][s][:],
                                 start=(s == 0), stop=(s == NS - 1))
                        i = t.matmul(PB[1][:], ones_bf[:], sqv[s][:],
                                     start=(s == 0), stop=(s == NS - 1))
                        C.inc(i, "T", sems, 1, label=f"a1:statmm:{c}:{s}")
                    C.mark("T", f"a1:stat:{c}")
                if c == 3:
                    C.wait(t, f"p:wr1:{2 * NS - 1}", sems)
                    i = t.matmul(PB[7][0:1, 2:3], wsum, cpk[:, 15:16],
                                 start=True, stop=True)
                    C.inc(i, "T", sems, 1, label="p:wsmm")
                if c == 4:
                    C.wait(t, "p:invrow", sems)
                    i = t.matmul(PB[7][:, 0:2], ones1, cpk[0:1, 13:15],
                                 start=True, stop=True)
                    C.inc(i, "T", sems, 1, label="p:invmm")
                if c >= 1 and c - 1 <= NCH - 1:
                    cc = c - 1
                    C.wait(t, f"a1:math1:{cc}", sems)
                    if cc >= 2:
                        C.wait(t, f"a1:sl:{cc - 2}:{NS - 1}", sems)
                    i = t.matmul(PB[2 + (cc % 2) * 2][:], ones1, mrow,
                                 start=True, stop=True)
                    C.inc(i, "T", sems, 1, label=f"a1:replmu:{cc}")

        for e in ("V", "S", "T", "G"):
            C.mark(e, f"end:A1:{e}")

        def barrier(eng, sweep):
            for e in ("V", "S", "T", "G"):
                C.wait(eng, f"end:{sweep}:{e}", sems)

        blk.__exit__(None, None, None)
        blk2 = nc.Block(); blk2.__enter__()
        mrowB = scr3[0:1, :]
        vrowB = vrow
        qrowB = qrow

        @blk2.sync
        def _(sy):
            for c in range(NCH):
                if c == 0:
                    C.wait(sy, f"a1:sst:{NCH - 1}", sems)
                    C.wait(sy, f"a1:mst:{NCH - 1}", sems)
                else:
                    C.wait(sy, f"b1:repl:{c - 1}", sems)
                sy.dma_start(out=srow[:], in_=cmp_s["x"][:, 4 * c:4 * c + 4]
                             ).then_inc(sems["RQ"], 16)
                C.inc(None, "RQ", sems, 16)
                sy.dma_start(out=m2row[:], in_=cmp_m2["x"][:, 4 * c:4 * c + 4]
                             ).then_inc(sems["RQ"], 16)
                C.inc(None, "RQ", sems, 16, label=f"b1:rows:{c}")
                if c >= 1:
                    C.wait(sy, f"b1:math1:{c - 1}", sems)
                    sy.dma_start(out=cmp_mu[:, 4 * (c - 1):4 * (c - 1) + 4],
                                 in_=mrowB).then_inc(sems["RM"], 16)
                    C.inc(None, "RM", sems, 16, label=f"b1:mst:{c - 1}")
                    if DEBUG:
                        C.wait(sy, f"b1:dbgq:{c - 1}", sems)
                        sy.dma_start(out=dbgr[:, 4 * (c - 1):4 * (c - 1) + 4],
                                     in_=qrowB).then_inc(sems["RQ"], 16)
                        C.inc(None, "RQ", sems, 16, label=f"b1:dbgst:{c - 1}")
            C.wait(sy, f"b1:math1:{NCH - 1}", sems)
            sy.dma_start(out=cmp_mu[:, 4 * (NCH - 1):4 * NCH],
                         in_=mrowB).then_inc(sems["RM"], 16)
            C.inc(None, "RM", sems, 16, label=f"b1:mst:{NCH - 1}")
            if DEBUG:
                C.wait(sy, f"b1:dbgq:{NCH - 1}", sems)
                sy.dma_start(out=dbgr[:, 4 * (NCH - 1):4 * NCH],
                             in_=qrowB).then_inc(sems["RQ"], 16)
                C.inc(None, "RQ", sems, 16, label=f"b1:dbgst:{NCH - 1}")

        @blk2.tensor
        def _(t):
            barrier(t, "A1")
            C.wait(t, "thr1v", sems)
            i = t.matmul(PB[6][:, 0:2], ones1, cpk[0:1, 13:15],
                         start=True, stop=True)
            C.inc(i, "T", sems, 1, label="thr1mm")
            for c in range(NCH + 1):
                if c <= NCH - 1:
                    C.wait(t, f"b1:rows:{c}", sems)
                    if c >= 2:
                        C.wait(t, f"b1:xq:{c - 2}", sems)
                    r1, r2 = PB[2 + (c % 2) * 2], PB[3 + (c % 2) * 2]
                    t.matmul(r1[:], ones1, srow[:], start=True, stop=True)
                    i = t.matmul(r2[:], ones1, m2row[:], start=True, stop=True)
                    C.inc(i, "T", sems, 1, label=f"b1:repl:{c}")
                if c >= 1:
                    cc = c - 1
                    C.wait(t, f"b1:xq:{cc}", sems)
                    i = None
                    for e in range(NS):
                        k = cc * NS + e
                        if k >= 2:
                            C.wait(t, f"b1:evac:{(k - 2) // NS}:{(k - 2) % NS}",
                                   sems)
                        for s in range(NS):
                            i = t.matmul(PB[6 + e % 2][:],
                                         w1[s][:, e * 128:(e + 1) * 128],
                                         bufX[cc % 2][s][:], start=(s == 0),
                                         stop=(s == NS - 1))
                        C.inc(i, "T", sems, 1, label=f"b1:mm:{cc}:{e}")
                    for s in range(NS):
                        C.wait(t, f"b1:cast1:{cc}:{s}", sems)
                        if cc >= 1 and s == 0:
                            C.wait(t, f"b1:math1:{cc - 1}", sems)
                        t.matmul(PB[0][:], ones_bf[:], bufX[cc % 2][s][:],
                                 start=(s == 0), stop=(s == NS - 1))
                        i = t.matmul(PB[1][:], ones_bf[:], sqt[s % 2][:],
                                     start=(s == 0), stop=(s == NS - 1))
                        C.inc(i, "T", sems, 1, label=f"b1:statmm:{cc}:{s}")
                    C.mark("T", f"b1:stat:{cc}")
            C.mark("T", "b1:mmall")

        @blk2.vector
        def _(v):
            barrier(v, "A1")
            C.wait(v, "thr1mm", sems)
            i = v.tensor_copy(cpk[:, 9:11], PB[6][:, 0:2])
            nv = C.inc(i, "V", sems, 1)
            v.wait_ge(sems["V"], nv)
            for c in range(NCH + 2):
                if c <= NCH - 1 or c == NCH:
                    cc = c - 1
                    if c <= NCH - 1:
                        C.wait(v, f"b1:repl:{c}", sems)
                        if c >= 2:
                            C.wait(v, f"b1:statmm:{c - 2}:{NS - 1}", sems)
                            if DEBUG and c - 2 == 4:
                                C.wait(v, f"b1:dbgbx:4", sems)
                    r1, r2 = PB[2 + (c % 2) * 2], PB[3 + (c % 2) * 2]
                    i = None
                    for s in range(NS):
                        if c <= NCH - 1:
                            v.tensor_tensor(b_sb[:], chunk(x_sb[s], c), r2[:],
                                            ALU.subtract)
                            v.tensor_tensor(s_rep[:], b_sb[:], r1[:], ALU.mult)
                            v.tensor_scalar(b_sb[:], s_rep[:],
                                            col(vc["gam"], s),
                                            col(vc["bet"], s),
                                            ALU.mult, ALU.add)
                            i = v.tensor_scalar(bufX[c % 2][s][:], b_sb[:],
                                                col(thr1, 0), col(nthr1, 0),
                                                ALU.min, ALU.max)
                            if s == NS - 1:
                                C.inc(i, "V", sems, 1, label=f"b1:xq:{c}")
                        if 1 <= c:
                            C.wait(v, f"b1:mm:{cc}:{s}", sems)
                            i = v.scalar_tensor_tensor(chunk(x_sb[s], cc),
                                                       PB[6 + s % 2][:],
                                                       col(hbeta_c, 0),
                                                       chunk(x_sb[s], cc),
                                                       ALU.mult, ALU.add)
                            C.inc(i, "V", sems, 1, label=f"b1:evac:{cc}:{s}")
                if 2 <= c <= NCH + 1:
                    cc = c - 2
                    C.wait(v, f"b1:sg:{cc}", sems)
                    if cc >= 1:
                        C.wait(v, f"b1:sr:{cc - 1}", sems)
                    i = v.reciprocal(dvt, dvin)
                    nv = C.inc(i, "V", sems, 1)
                    v.wait_ge(sems["V"], nv)
                    i = v.tensor_scalar(dvout, dvt, 1.0, None, ALU.mult)
                    C.inc(i, "V", sems, 1, label=f"b1:rc:{cc}")
                if 1 <= c <= NCH:
                    cc = c - 1
                    C.wait(v, f"b1:stat:{cc}", sems)
                    if cc >= 1:
                        C.wait(v, f"b1:mst:{cc - 1}", sems)
                    v.tensor_scalar(mrowB, PB[0][0:1, :], INV_D, None, ALU.mult)
                    i = v.tensor_scalar(vrowB, PB[1][0:1, :], SSM_A, SSM_B,
                                        ALU.mult, ALU.add)
                    nv = C.inc(i, "V", sems, 1)
                    v.wait_ge(sems["V"], nv)
                    i = v.scalar_tensor_tensor(qrowB, mrowB, SSM_M, mrowB,
                                               ALU.mult, ALU.mult)
                    nv = C.inc(i, "V", sems, 1)
                    v.wait_ge(sems["V"], nv)
                    i = v.tensor_tensor(vrowB, vrowB, qrowB, ALU.subtract)
                    C.inc(i, "V", sems, 1, label=f"b1:math1:{cc}")
                    if DEBUG:
                        if cc >= 1:
                            C.wait(v, f"b1:dbgst:{cc - 1}", sems)
                        i = v.tensor_copy(qrowB, PB[0][0:1, :])
                        C.inc(i, "V", sems, 1, label=f"b1:dbgq:{cc}")
            C.wait(v, f"b1:sr:{NCH - 1}", sems)
            C.wait(v, f"b1:mst:{NCH - 1}", sems)
            i = v.tensor_tensor(cmp_m2["x1"][:], cmp_mu[:], cmp_s["x1"][:],
                                ALU.mult)
            nv = C.inc(i, "V", sems, 1, label="b1:m2c")
            v.wait_ge(sems["V"], nv)

        @blk2.scalar
        def _(sc):
            barrier(sc, "A1")
            for c in range(1, NCH + 2):
                if c - 1 <= NCH - 1:
                    cc = c - 1
                    for s in range(NS):
                        k = cc * NS + s
                        C.wait(sc, f"b1:evac:{cc}:{s}", sems)
                        C.wait(sc, f"b1:mm:{cc}:{NS - 1}", sems)
                        sc.copy(bufX[cc % 2][s][:], chunk(x_sb[s], cc))
                        if k >= 1:
                            C.wait(sc,
                                   f"b1:statmm:{(k - 1) // NS}:{(k - 1) % NS}",
                                   sems)
                        i = sc.activation(sqt[s % 2][:], bufX[cc % 2][s][:],
                                          ACT.Square)
                        C.inc(i, "S", sems, 1, label=f"b1:cast1:{cc}:{s}")
                if c >= 2:
                    cc = c - 2
                    C.wait(sc, f"b1:rc:{cc}", sems)
                    sc.dma_start(out=cmp_s["x1"][:, 4 * cc:4 * cc + 4],
                                 in_=dvout, allow_slow_non_contiguous=True
                                 ).then_inc(sems["RS"], 16)
                    rs = C.inc(None, "RS", sems, 16)
                    sc.wait_ge(sems["RS"], rs)
                    i = sc.nop()
                    C.inc(i, "S", sems, 1, label=f"b1:sr:{cc}")
                if c - 1 <= NCH - 1:
                    cc = c - 1
                    C.wait(sc, f"b1:math1:{cc}", sems)
                    i = sc.activation(vrowB, vrowB, ACT.Sqrt)
                    ns = C.inc(i, "S", sems, 1)
                    sc.wait_ge(sems["S"], ns)
                    sc.dma_start(out=dvin, in_=vrowB,
                                 allow_slow_non_contiguous=True
                                 ).then_inc(sems["RS"], 16)
                    rs = C.inc(None, "RS", sems, 16)
                    sc.wait_ge(sems["RS"], rs)
                    i = sc.nop()
                    C.inc(i, "S", sems, 1, label=f"b1:sg:{cc}")

        @blk2.gpsimd
        def _(gp):
            barrier(gp, "A1")
            if DEBUG:
                C.wait(gp, "b1:stat:4", sems)
                for s in range(NS):
                    gp.dma_start(out=dbga[:, s * TC:(s + 1) * TC],
                                 in_=bufX[0][s][:]).then_inc(sems["GQ"], 16)
                    C.inc(None, "GQ", sems, 16,
                          label=f"b1:dbgbx:4" if s == NS - 1 else None)
            C.wait(gp, "b1:mmall", sems)
            for s in range(NS):
                gp.dma_start(out=w1[s][:], in_=cmat[s * 128:(s + 1) * 128, :]
                             ).then_inc(sems["GQ"], 16)
                C.inc(None, "GQ", sems, 16)
            gp.wait_ge(sems["GQ"], C.cnt["GQ"])
            i = gp.nop()
            C.inc(i, "F", sems, 1, label="lcmatf")

        for e in ("V", "S", "T", "G"):
            C.mark(e, f"end:B1:{e}")

        blk2.__exit__(None, None, None)
        blk3 = nc.Block(); blk3.__enter__()

        @blk3.sync
        def _(sy):
            if DEBUG:
                for e in ("V", "S", "T", "G"):
                    C.wait(sy, f"end:B1:{e}", sems)
                for s in range(NS):
                    sy.dma_start(out=dbg1[s * 128:(s + 1) * 128, :],
                                 in_=x_sb[s][:]).then_inc(sems["RQ"], 16)
                    C.inc(None, "RQ", sems, 16)
            def _ssm_store(c):
                C.wait(sy, f"ssm:m2:{c}", sems)
                sy.dma_start(out=cmp_s["x2"][:, 4 * c:4 * c + 4],
                             in_=scr3[0:1, :]).then_inc(sems["RQ"], 16)
                C.inc(None, "RQ", sems, 16)
                sy.dma_start(out=cmp_m2["x2"][:, 4 * c:4 * c + 4],
                             in_=scr2[0:1, :]).then_inc(sems["RQ"], 16)
                C.inc(None, "RQ", sems, 16, label=f"ssm:rowst:{c}")

            for c in range(NCH):
                if c == 0:
                    C.wait(sy, f"b1:sr:{NCH - 1}", sems)
                else:
                    C.wait(sy, f"ssm:repl:{c - 1}", sems)
                if c >= 1:
                    _ssm_store(c - 1)
                sy.dma_start(out=srow[:], in_=cmp_s["x1"][:, 4 * c:4 * c + 4]
                             ).then_inc(sems["RQ"], 16)
                C.inc(None, "RQ", sems, 16, label=f"ssm:rows:{c}")
            _ssm_store(NCH - 1)

        @blk3.gpsimd
        def _(gp):
            barrier(gp, "B1")
            for c in range(NCH):
                if c > 0:
                    C.wait(gp, f"ssm:umm:{c - 1}:{NS - 1}", sems)
                    C.wait(gp, f"ssm:ymm:{c - 1}:{NS - 1}", sems)
                gp.dma_start(out=murow_b[:], in_=cmp_mu[:, 4 * c:4 * c + 4]
                             ).then_inc(sems["GQ"], 16)
                C.inc(None, "GQ", sems, 16)
                gp.dma_start(out=m2row_b[:], in_=cmp_m2["x1"][:, 4 * c:4 * c + 4]
                             ).then_inc(sems["GQ"], 16)
                C.inc(None, "GQ", sems, 16, label=f"ssm:browz:{c}")
                for f in range(NS):
                    C.wait(gp, f"ssm:xy:{c}:{f}", sems)
                    i = gp.tensor_tensor(chunk(x_sb[f], c), chunk(x_sb[f], c),
                                         (scr1 if f % 2 == 0 else scr3)[:],
                                         ALU.add)
                    C.inc(i, "G", sems, 1, label=f"ssm:xt4:{c}:{f}")
                for s in range(NS):
                    k = c * NS + s
                    C.wait(gp, f"ssm:ab:{c}:{s}", sems)
                    if c == 0 and s == 0:
                        C.wait(gp, "thr1v", sems)
                    i = gp.tensor_reduce(gsl2[0:1, k:k + 1], tn1[s % 2][:],
                                         mybir.AxisListType.XYZWC, ALU.max)
                    C.inc(i, "G", sems, 1, label=f"ssm:red:{c}:{s}")
            C.wait(gp, "ssm:ulast", sems)
            gp.wait_ge(sems["GQ"], C.cnt["GQ"])
            for s in range(NS):
                gp.dma_start(out=w2[s][:], in_=wq_scr[s * 128:(s + 1) * 128, :]
                             ).then_inc(sems["GQ"], 16)
                C.inc(None, "GQ", sems, 16)
            gp.wait_ge(sems["GQ"], C.cnt["GQ"])
            i = gp.nop()
            C.inc(i, "F", sems, 1, label="lwq2f")

        @blk3.tensor
        def _(t):
            barrier(t, "B1")
            i = None
            for e in range(NS):
                if e == 0:
                    C.wait(t, "lwinf", sems)
                for s in range(NS):
                    i = t.matmul(PB[2][:, e:e + 1],
                                 w2[s][:, e * 128:(e + 1) * 128],
                                 onescol_b[:], start=(s == 0), stop=(s == NS - 1))
                C.inc(i, "T", sems, 1, label=f"ssm:ws:{e}")
            for c in range(NCH):
                C.wait(t, f"ssm:rows:{c}", sems)
                if c == 0:
                    C.wait(t, f"ssm:wsc:{NS - 1}", sems)
                if c >= 2:
                    C.wait(t, f"ssm:srepdone:{c - 2}", sems)
                i = t.matmul(PB[2 + c % 2][:], ones1, srow[:],
                             start=True, stop=True)
                C.inc(i, "T", sems, 1, label=f"ssm:repl:{c}")
                for s in range(NS):
                    C.wait(t, f"ssm:cast1:{c}:{s}", sems)
                C.wait(t, f"ssm:browz:{c}", sems)
                if c == 0:
                    C.wait(t, "p:vecws", sems)
                for e in range(NS):
                    k = c * NS + e
                    if k >= 2:
                        C.wait(t, f"ssm:b:{(k - 2) // NS}:{(k - 2) % NS}", sems)
                    for s in range(NS):
                        t.matmul(PB[4 + e % 2][:], w2[s][:, e * 128:(e + 1) * 128],
                                 bufX[c % 2][s][:], start=(s == 0), stop=False)
                    i = t.matmul(PB[4 + e % 2][:], nwrow[e][:], murow_b[:],
                                 start=False, stop=True)
                    C.inc(i, "T", sems, 1, label=f"ssm:umm:{c}:{e}")
                if c == NCH - 1:
                    C.mark("T", "ssm:ulast")
                for f in range(NS):
                    k = c * NS + f
                    C.wait(t, f"ssm:hhat:{c}:{NS - 1}", sems)
                    if k >= 2:
                        C.wait(t, f"ssm:xy:{(k - 2) // NS}:{(k - 2) % NS}", sems)
                    if c == 0 and f == 0:
                        C.wait(t, "lcmatf", sems)
                    for e in range(NS):
                        t.matmul(PB[6 + f % 2][:], w1[e][:, f * 128:(f + 1) * 128],
                                 bufH[e][:], start=(e == 0), stop=False)
                    i = t.matmul(PB[6 + f % 2][:], ndrow[f][:], m2row_b[:],
                                 start=False, stop=True)
                    C.inc(i, "T", sems, 1, label=f"ssm:ymm:{c}:{f}")
                for s in range(NS):
                    k = c * NS + s
                    C.wait(t, f"ssm:cast2:{c}:{s}", sems)
                    if c >= 1 and s == 0:
                        C.wait(t, f"ssm:xn2:{c - 1}:{NS - 1}", sems)
                    t.matmul(PB[0][:], ones_bf[:], bufX[c % 2][s][:],
                             start=(s == 0), stop=(s == NS - 1))
                    i = t.matmul(PB[1][:], ones_bf[:], sqt[s % 2][:],
                                 start=(s == 0), stop=(s == NS - 1))
                    C.inc(i, "T", sems, 1, label=f"ssm:statmm:{c}:{s}")
                C.mark("T", f"ssm:stat:{c}")
                C.wait(t, f"ssm:sr:{c}", sems)
                C.wait(t, f"ssm:math1:{c}", sems)
                i = t.matmul(PB[0][:], ones1, scr3[0:1, :], start=True, stop=True)
                C.inc(i, "T", sems, 1, label=f"ssm:srep2:{c}")

        @blk3.vector
        def _(v):
            barrier(v, "B1")
            for e in range(NS):
                C.wait(v, f"ssm:ws:{e}", sems)
                i = v.tensor_scalar(col(ws_cols, e), PB[2][:, e:e + 1], -1.0,
                                    None, ALU.mult)
                C.inc(i, "V", sems, 1, label=f"ssm:wsc:{e}")
            for c in range(NCH):
                C.wait(v, f"ssm:repl:{c}", sems)
                i = v.tensor_copy(s_rep[:], PB[2 + c % 2][:])
                nv = C.inc(i, "V", sems, 1, label=f"ssm:srepdone:{c}")
                v.wait_ge(sems["V"], nv)
                for e in range(NS):
                    C.wait(v, f"ssm:umm:{c}:{e}", sems)
                    if e >= 1:
                        C.wait(v, f"ssm:hcast:{c}:{e - 1}", sems)
                    elif c >= 1:
                        C.wait(v, f"ssm:hcast:{c - 1}:{NS - 1}", sems)
                    v.scalar_tensor_tensor(b_sb[:], PB[4 + e % 2][:],
                                           col(vc["bvec"], e), s_rep[:],
                                           ALU.mult, ALU.mult)
                    i = v.tensor_scalar(b_sb[:], b_sb[:], col(bb_cols, e), None,
                                        ALU.add)
                    C.inc(i, "V", sems, 1, label=f"ssm:b:{c}:{e}")
                    v.tensor_tensor_scan(h_sb[:], bcast(a_cols, e), b_sb[:],
                                         col(carry, e), ALU.mult, ALU.add)
                    i = v.tensor_copy(col(carry, e), h_sb[:, TC - 1:TC])
                    nv = C.inc(i, "V", sems, 1, label=f"ssm:scan:{c}:{e}")
                    v.wait_ge(sems["V"], nv)
                for f in range(NS):
                    C.wait(v, f"ssm:ymm:{c}:{f}", sems)
                    if f >= 2:
                        C.wait(v, f"ssm:xt4:{c}:{f - 2}", sems)
                    elif c >= 1:
                        C.wait(v, f"ssm:xt4:{c - 1}:{f + NS - 2}", sems)
                    t4buf = scr1 if f % 2 == 0 else scr3
                    v.scalar_tensor_tensor(t4buf[:], chunk(x_sb[f], c),
                                           col(vc["dd"], f), s_rep[:],
                                           ALU.mult, ALU.mult)
                    i = v.tensor_tensor(chunk(x_sb[f], c), chunk(x_sb[f], c),
                                        PB[6 + f % 2][:], ALU.add)
                    C.inc(i, "V", sems, 1, label=f"ssm:xy:{c}:{f}")
                for f in range(NS - 2, NS):
                    C.wait(v, f"ssm:xt4:{c}:{f}", sems)
                C.wait(v, f"ssm:stat:{c}", sems)
                if c >= 1:
                    C.wait(v, f"ssm:rowst:{c - 1}", sems)
                    C.wait(v, f"ssm:srep2:{c - 1}", sems)
                v.tensor_scalar(scr1[:], PB[0][:], INV_D, None, ALU.mult)
                v.tensor_scalar(scr2[:], PB[1][:], FFN_A, FFN_B, ALU.mult, ALU.add)
                v.scalar_tensor_tensor(scr3[:], scr1[:], FFN_M, scr1[:],
                                       ALU.mult, ALU.mult)
                i = v.tensor_tensor(scr2[:], scr2[:], scr3[:], ALU.subtract)
                C.inc(i, "V", sems, 1, label=f"ssm:math1:{c}")
                C.wait(v, f"ssm:sg:{c}", sems)
                i = v.reciprocal(vcc2[:], vcc1[:, 0:4])
                nv = C.inc(i, "V", sems, 1)
                v.wait_ge(sems["V"], nv)
                i = v.tensor_scalar(vcc1[:, 0:4], vcc2[:], 1.0, None, ALU.mult)
                C.inc(i, "V", sems, 1, label=f"ssm:rc:{c}")
                C.wait(v, f"ssm:srep2:{c}", sems)
                i = v.tensor_tensor(scr2[:], scr1[:], PB[0][:], ALU.mult)
                C.inc(i, "V", sems, 1, label=f"ssm:m2:{c}")
                C.wait(v, f"ssm:casts:{c}", sems)
                for s in range(NS):
                    k = c * NS + s
                    if k >= 1:
                        C.wait(v, f"ssm:red:{(k - 1) // NS}:{(k - 1) % NS}", sems)
                    v.tensor_tensor(tn1[s % 2][:], bufX[c % 2][s][:], PB[0][:],
                                    ALU.mult)
                    v.tensor_tensor(tn1[s % 2][:], tn1[s % 2][:], scr2[:],
                                    ALU.subtract)
                    i = v.tensor_scalar(tn1[s % 2][:], tn1[s % 2][:],
                                        col(vc["gam"], s), col(vc["bet"], s),
                                        ALU.mult, ALU.add)
                    C.inc(i, "V", sems, 1, label=f"ssm:xn2:{c}:{s}")
            C.wait(v, f"ssm:red:{NCH - 1}:{NS - 1}", sems)
            i = v.tensor_reduce(mx1[0:1, :], gsl1[0:1, 0:64], AXX, ALU.max)
            nv = C.inc(i, "V", sems, 1)
            v.wait_ge(sems["V"], nv)
            i = v.tensor_scalar(stg_a, mx1[0:1, :], THR_C, None, ALU.mult)
            nv = C.inc(i, "V", sems, 1)
            v.wait_ge(sems["V"], nv)
            i = v.tensor_scalar(stg_b, stg_a, -1.0, None, ALU.mult)
            nv = C.inc(i, "V", sems, 1, label="thr2v")
            v.wait_ge(sems["V"], nv)

        @blk3.scalar
        def _(sc):
            barrier(sc, "B1")
            C.wait(sc, "p:vecd", sems)
            for e in range(NS):
                C.wait(sc, f"ssm:wsc:{e}", sems)
                sc.dma_start(out=nwrow[e][:], in_=col(ws_cols, e),
                             allow_slow_non_contiguous=True
                             ).then_inc(sems["RS"], 16)
                C.inc(None, "RS", sems, 16)
                sc.dma_start(out=ndrow[e][:], in_=col(nd_colsb, e),
                             allow_slow_non_contiguous=True
                             ).then_inc(sems["RS"], 16)
                C.inc(None, "RS", sems, 16, label=f"ssm:rowdma:{e}")
            sc.wait_ge(sems["RS"], C.cnt["RS"])
            i = sc.nop()
            C.inc(i, "S", sems, 1, label="p:vecws")
            for c in range(NCH):
                for s in range(NS):
                    if c >= 2:
                        C.wait(sc, f"ssm:statmm:{c - 2}:{s}", sems)
                        C.wait(sc, f"ssm:xn2:{c - 2}:{s}", sems)
                    i = sc.copy(bufX[c % 2][s][:], chunk(x_sb[s], c))
                    C.inc(i, "S", sems, 1, label=f"ssm:cast1:{c}:{s}")
                for e in range(NS):
                    C.wait(sc, f"ssm:scan:{c}:{e}", sems)
                    if c >= 1 and e == 0:
                        C.wait(sc, f"ssm:ymm:{c - 1}:{NS - 1}", sems)
                    i = sc.copy(bufH[e][:], h_sb[:])
                    C.inc(i, "S", sems, 1, label=f"ssm:hhat:{c}:{e}")
                    C.mark("S", f"ssm:hcast:{c}:{e}")
                for s in range(NS):
                    k = c * NS + s
                    C.wait(sc, f"ssm:xt4:{c}:{s}", sems)
                    C.wait(sc, f"ssm:umm:{c}:{NS - 1}", sems)
                    sc.copy(bufX[c % 2][s][:], chunk(x_sb[s], c))
                    if k >= 1:
                        C.wait(sc, f"ssm:statmm:{(k - 1) // NS}:{(k - 1) % NS}", sems)
                    i = sc.activation(sqt[s % 2][:], bufX[c % 2][s][:], ACT.Square)
                    C.inc(i, "S", sems, 1, label=f"ssm:cast2:{c}:{s}")
                C.wait(sc, f"ssm:math1:{c}", sems)
                if c >= 1:
                    C.wait(sc, f"ssm:rowst:{c - 1}", sems)
                i = sc.activation(scr3[:], scr2[:], ACT.Sqrt)
                ns = C.inc(i, "S", sems, 1)
                sc.wait_ge(sems["S"], ns)
                sc.dma_start(out=vcc1[:, 0:4], in_=scr3[0:1, :],
                             allow_slow_non_contiguous=True
                             ).then_inc(sems["RS"], 16)
                rs = C.inc(None, "RS", sems, 16)
                sc.wait_ge(sems["RS"], rs)
                i = sc.nop()
                C.inc(i, "S", sems, 1, label=f"ssm:sg:{c}")
                C.wait(sc, f"ssm:rc:{c}", sems)
                sc.dma_start(out=scr3[0:1, :], in_=vcc1[:, 0:4],
                             allow_slow_non_contiguous=True
                             ).then_inc(sems["RS"], 16)
                rs = C.inc(None, "RS", sems, 16)
                sc.wait_ge(sems["RS"], rs)
                i = sc.nop()
                C.inc(i, "S", sems, 1, label=f"ssm:sr:{c}")
                C.mark("S", f"ssm:casts:{c}")
                for s in range(NS):
                    C.wait(sc, f"ssm:xn2:{c}:{s}", sems)
                    i = sc.activation(tn1[s % 2][:], tn1[s % 2][:], ACT.Abs)
                    C.inc(i, "S", sems, 1, label=f"ssm:ab:{c}:{s}")

        for e in ("V", "S", "T", "G"):
            C.mark(e, f"end:SSM:{e}")

        blk3.__exit__(None, None, None)
        blk4 = nc.Block(); blk4.__enter__()

        @blk4.sync
        def _(sy):
            if DEBUG:
                for e in ("V", "S", "T", "G"):
                    C.wait(sy, f"end:SSM:{e}", sems)
                for s in range(NS):
                    sy.dma_start(out=dbg2[s * 128:(s + 1) * 128, :],
                                 in_=x_sb[s][:]).then_inc(sems["RQ"], 16)
                    C.inc(None, "RQ", sems, 16)
            def _b2_store(c):
                for e in range(NS):
                    C.wait(sy, f"b2:evac:{c}:{e}", sems)
                    sy.dma_start(out=outT[e * 128:(e + 1) * 128,
                                          c * TC:(c + 1) * TC],
                                 in_=chunk(x_sb[e], c)).then_inc(sems["ST"], 16)
                    C.inc(None, "ST", sems, 16, label=f"st:{c}:{e}")

            for c in range(NCH):
                if c == 0:
                    C.wait(sy, f"ssm:rowst:{NCH - 1}", sems)
                else:
                    C.wait(sy, f"b2:repl:{c - 1}", sems)
                sy.dma_start(out=srow[:], in_=cmp_s["x2"][:, 4 * c:4 * c + 4]
                             ).then_inc(sems["RQ"], 16)
                C.inc(None, "RQ", sems, 16)
                sy.dma_start(out=m2row[:], in_=cmp_m2["x2"][:, 4 * c:4 * c + 4]
                             ).then_inc(sems["RQ"], 16)
                C.inc(None, "RQ", sems, 16, label=f"b2:rows:{c}")
                if c >= 1:
                    _b2_store(c - 1)
            _b2_store(NCH - 1)

        def _b2_big(t, c):
            C.wait(t, f"b2:xq:{c}", sems)
            i = None
            for e in range(NS):
                k = c * NS + e
                if k >= 2:
                    C.wait(t, f"b2:evac:{(k - 2) // NS}:{(k - 2) % NS}", sems)
                for s in range(NS):
                    i = t.matmul(PB[6 + e % 2][:], w2[s][:, e * 128:(e + 1) * 128],
                                 bufX[c % 2][s][:], start=(s == 0),
                                 stop=(s == NS - 1))
                C.inc(i, "T", sems, 1, label=f"b2:mm:{c}:{e}")

        @blk4.tensor
        def _(t):
            barrier(t, "SSM")
            C.wait(t, "thr2v", sems)
            i = t.matmul(PB[6][:, 0:2], ones1, cpk[0:1, 13:15],
                         start=True, stop=True)
            C.inc(i, "T", sems, 1, label="thr2mm")
            C.wait(t, "lwq2f", sems)
            for c in range(NCH):
                C.wait(t, f"b2:rows:{c}", sems)
                if c >= 2:
                    C.wait(t, f"b2:xq:{c - 2}", sems)
                r1, r2 = PB[2 + (c % 2) * 2], PB[3 + (c % 2) * 2]
                t.matmul(r1[:], ones1, srow[:], start=True, stop=True)
                i = t.matmul(r2[:], ones1, m2row[:], start=True, stop=True)
                C.inc(i, "T", sems, 1, label=f"b2:repl:{c}")
                C.mark("T", f"b2:replq:{c}")
                if c >= 1:
                    _b2_big(t, c - 1)
            _b2_big(t, NCH - 1)

        @blk4.vector
        def _(v):
            barrier(v, "SSM")
            C.wait(v, "thr2mm", sems)
            i = v.tensor_copy(cpk[:, 11:13], PB[6][:, 0:2])
            nv = C.inc(i, "V", sems, 1)
            v.wait_ge(sems["V"], nv)
            for c in range(NCH + 1):
                if c <= NCH - 1:
                    C.wait(v, f"b2:repl:{c}", sems)
                    if c >= 2:
                        C.wait(v, f"b2:mm:{c - 2}:{NS - 1}", sems)
                    r1, r2 = PB[2 + (c % 2) * 2], PB[3 + (c % 2) * 2]
                    i = None
                    for s in range(NS):
                        v.tensor_tensor(b_sb[:], chunk(x_sb[s], c), r1[:],
                                        ALU.mult)
                        v.tensor_tensor(s_rep[:], b_sb[:], r2[:], ALU.subtract)
                        v.tensor_scalar(b_sb[:], s_rep[:], col(vc["gam"], s),
                                        col(vc["bet"], s), ALU.mult, ALU.add)
                        i = v.tensor_scalar(bufX[c % 2][s][:], b_sb[:],
                                            col(thr2, 0), col(nthr2, 0),
                                            ALU.min, ALU.max)
                    C.inc(i, "V", sems, 1, label=f"b2:xq:{c}")
                if c >= 1:
                    cc = c - 1
                    for e in range(NS):
                        C.wait(v, f"b2:mm:{cc}:{e}", sems)
                        i = v.scalar_tensor_tensor(chunk(x_sb[e], cc),
                                                   PB[6 + e % 2][:],
                                                   col(hbeta_c, 0),
                                                   chunk(x_sb[e], cc),
                                                   ALU.mult, ALU.add)
                        C.inc(i, "V", sems, 1, label=f"b2:evac:{cc}:{e}")

        @blk4.gpsimd
        def _(gp):
            barrier(gp, "SSM")
            gp.wait_ge(sems["ST"], 16 * NCH * NS)
            if DEBUG:
                w = NCH * 4
                for j, tt in enumerate([cmp_s["x"], cmp_m2["x"], cmp_s["x1"],
                                        cmp_m2["x1"], cmp_mu, cmp_s["x2"],
                                        cmp_m2["x2"]]):
                    gp.dma_start(out=dbgs[:, j * w:(j + 1) * w], in_=tt[:]
                                 ).then_inc(sems["GQ"], 16)
                    C.inc(None, "GQ", sems, 16)
                gp.dma_start(out=dbgs[:, 7 * w:7 * w + 16], in_=cpk[:]
                             ).then_inc(sems["GQ"], 16)
                C.inc(None, "GQ", sems, 16)
                gp.dma_start(out=dbgs[:, 7 * w + 16:7 * w + 16 + NS],
                             in_=carry[:]).then_inc(sems["GQ"], 16)
                C.inc(None, "GQ", sems, 16)
                gp.wait_ge(sems["GQ"], C.cnt["GQ"])

        blk4.__exit__(None, None, None)

    return C.out


def _build():
    nc1 = bass.Bass("TRN2", target_bir_lowering=False, detect_race_conditions=False)
    labels = _emit(nc1, {}, strict=False)
    nc = bass.Bass("TRN2", target_bir_lowering=False, detect_race_conditions=False)
    labels2 = _emit(nc, labels, strict=True)
    return nc


_BUILT = {}


def kernel(L0_attn_x, L0_attn_W_in, L0_attn_b_in, L0_attn_log_A, L0_attn_B_vec,
           L0_attn_C, L0_attn_D, L0_ffn_weight, L0_ffn_ln_gamma, L0_ffn_ln_beta):
    if "nc" not in _BUILT:
        _BUILT["nc"] = _build()
    nc = _BUILT["nc"]
    vecs = np.stack([
        np.asarray(L0_attn_b_in), np.asarray(L0_attn_log_A),
        np.asarray(L0_attn_B_vec), np.asarray(L0_attn_D),
        np.asarray(L0_ffn_ln_gamma), np.asarray(L0_ffn_ln_beta)
    ]).astype(np.float32, copy=False)
    vecs = np.ascontiguousarray(
        vecs.reshape(6, 8, 128).transpose(2, 0, 1).reshape(128, 48))
    wffn = np.ascontiguousarray(np.asarray(L0_ffn_weight, dtype=np.float32))
    win = np.ascontiguousarray(np.asarray(L0_attn_W_in, dtype=np.float32))
    cmat = np.ascontiguousarray(np.asarray(L0_attn_C, dtype=np.float32))
    x = np.asarray(L0_attn_x)
    in_maps = [{
        "xT": np.ascontiguousarray(x[i].T),
        "wffn": wffn, "win": win, "cmat": cmat, "vecs": vecs,
    } for i in range(x.shape[0])]
    res = run_bass_kernel_spmd(nc, in_maps, core_ids=list(range(len(in_maps))))
    return np.stack([np.ascontiguousarray(r["outT"].T) for r in res.results])


# version 12
# speedup vs baseline: 1.1074x; 1.1074x over previous
from contextlib import ExitStack

import numpy as np
import concourse.bass as bass
import concourse.mybir as mybir
from concourse.bass_isa import ReduceOp
from concourse.bass_utils import run_bass_kernel_spmd

F32 = mybir.dt.float32
BF16 = mybir.dt.bfloat16
ALU = mybir.AluOpType
ACT = mybir.ActivationFunctionType
AXX = mybir.AxisListType.X

D = 1024
NS = 8
TC = 512
T = 4096
NCH = T // TC
QB = 128.0
THR_C = (QB - 1e-5) / QB
INV_D = 1.0 / D
FFN_A = 1.001 / D
FFN_B = 1e-8
FFN_M = 1.001
SSM_A = 1.0 / D
SSM_B = 1e-5
SSM_M = 1.0


class Ctr:
    def __init__(self, labels, strict):
        self.labels = labels
        self.strict = strict
        self.cnt = {}
        self.out = {}

    def inc(self, instr, semname, sems, n=1, label=None):
        c = self.cnt.get(semname, 0) + n
        self.cnt[semname] = c
        if label is not None:
            self.out[label] = (semname, c)
        if instr is not None and n > 0:
            instr.then_inc(sems[semname], n)
        return c

    def mark(self, semname, label):
        self.out[label] = (semname, self.cnt.get(semname, 0))

    def wait(self, eng, label, sems):
        if label not in self.labels:
            if self.strict:
                raise KeyError(f"unknown label {label!r}")
            return
        semname, val = self.labels[label]
        if val > 0:
            eng.wait_ge(sems[semname], val)


DEBUG = False


def _emit(nc, labels, strict):
    C = Ctr(labels, strict)

    xT_h = nc.dram_tensor("xT", [D, T], F32, kind="ExternalInput")
    wffn_h = nc.dram_tensor("wffn", [D, D], F32, kind="ExternalInput")
    win_h = nc.dram_tensor("win", [D, D], F32, kind="ExternalInput")
    cmat_h = nc.dram_tensor("cmat", [D, D], F32, kind="ExternalInput")
    vecs_h = nc.dram_tensor("vecs", [128, 6 * NS], F32, kind="ExternalInput")
    outT_h = nc.dram_tensor("outT", [D, T], F32, kind="ExternalOutput")
    wqs_h = nc.dram_tensor("wq_scr", [D, D], BF16, kind="Internal")
    xT, wffn, win, cmat = xT_h.ap(), wffn_h.ap(), win_h.ap(), cmat_h.ap()
    outT, wq_scr = outT_h.ap(), wqs_h.ap()
    if DEBUG:
        dbg1 = nc.dram_tensor("dbg1", [D, T], F32, kind="ExternalOutput").ap()
        dbg2 = nc.dram_tensor("dbg2", [D, T], F32, kind="ExternalOutput").ap()
        dbgs = nc.dram_tensor("dbgs", [128, 7 * NCH * 4 + 16 + NS], F32,
                              kind="ExternalOutput").ap()
        dbga = nc.dram_tensor("dbga", [128, 8 * TC], F32, kind="ExternalOutput").ap()
        dbgr = nc.dram_tensor("dbgr", [128, NCH * 4], F32, kind="ExternalOutput").ap()

    with ExitStack() as st:
        def sb(name, shape, dt=F32):
            return st.enter_context(nc.sbuf_tensor(name, shape, dt))

        x_sb = [sb(f"x{s}", [128, T]) for s in range(NS)]
        w1 = [sb(f"w1_{s}", [128, D], BF16) for s in range(NS)]
        w2 = [sb(f"w2_{s}", [128, D], BF16) for s in range(NS)]
        bufX = [[sb(f"bx{p}_{s}", [128, TC], BF16) for s in range(NS)]
                for p in range(2)]
        bufH = [sb(f"bh{s}", [128, TC], BF16) for s in range(NS)]
        sqt_t = sb("sqt", [128, TC], BF16)
        sqt = [sqt_t, sqt_t]
        tn1_t = sb("tn1", [128, TC], BF16)
        tn1_b = sb("tn1b", [128, TC], BF16)
        tn1 = [tn1_t, tn1_b]

        scr1 = sb("scr1", [128, TC]); scr2 = sb("scr2", [128, TC])
        scr3 = sb("scr3", [128, TC])
        b_sb = sb("b_sb", [128, TC]); h_sb = b_sb
        s_rep = sb("s_rep", [128, TC])
        srow = sb("srow", [1, TC]); m2row = sb("m2row", [1, TC])
        mrow = m2row[0:1, :]
        srow_s = srow[0:1, :]
        vrow = scr2[0:1, :]
        qrow = scr1[0:1, :]
        mslots = sb("mslots", [128, NCH * NS], BF16)
        sqv = [sqt_t] + [bufH[i] for i in range(1, NS)]
        p2rb = sb("p2rb", [33, TC], BF16)
        murow_b = p2rb[0:1, :]
        m2row_b = p2rb[32:33, :]
        cmp_s = {k: sb(f"cs_{k}", [128, NCH * 4]) for k in ("x", "x1")}
        cmp_m2 = {k: sb(f"cm_{k}", [128, NCH * 4]) for k in ("x", "x1")}
        cmp_s["x2"] = cmp_s["x"]
        cmp_m2["x2"] = cmp_m2["x"]
        cmp_mu = cmp_m2["x"]
        wstage = sb("wst", [128, TC])
        wsb = bufH[0]
        vct = sb("vct", [128, 6 * NS])
        vc = {k: vct[:, j * NS:(j + 1) * NS] for j, k in
              enumerate(("bin", "la", "bvec", "dd", "gam", "bet"))}
        a_cols = vc["la"]
        bb_cols = vc["bin"]
        nd_colsb = sb("nd_colsb", [128, NS], BF16)
        ws_cols = sb("ws_cols", [128, NS], BF16)
        vcc1 = sb("vcc1", [128, 8]); vcc2 = sb("vcc2", [128, 4])
        dvin = vcc1[:, 0:4]; dvout = vcc1[:, 4:8]; dvt = vcc2[:, 0:4]
        gsl1 = mslots[0:1, 0:NCH * NS]
        gsl2 = gsl1
        p2r = sb("p2r", [33, D], BF16)
        nwrow = [p2r[0:1, e * 128:(e + 1) * 128] for e in range(NS)]
        ndrow = [p2r[32:33, e * 128:(e + 1) * 128] for e in range(NS)]
        ones_bf = sb("ones_bf", [128, 128], BF16)
        ones1_t = sb("ones1", [1, 128])
        ones1 = ones1_t[:]
        onescol_b = ones_bf[:, 0:1]
        carry = sb("carry", [128, NS])
        cpk = sb("cpk", [128, 16])
        (mx1, mx2, gred, _unused_wsum, wred, beta_c, inv_t, inv_c, hbeta_c,
         thr1, nthr1, thr2, nthr2) = [cpk[:, j:j + 1] for j in range(13)]
        wsum = cpk[:, 3:4]
        stg_a = cpk[0:1, 13:14]
        stg_b = cpk[0:1, 14:15]

        PB = [st.enter_context(nc.psum_tensor(f"pb{i}", [128, TC], F32))
              for i in range(8)]

        semnames = ["LX", "LW", "GQ", "RQ", "RS", "ST", "V", "S", "T", "G", "F", "RM"]
        sems = {n: st.enter_context(nc.semaphore(n)) for n in semnames}

        def col(tile, j):
            try:
                return tile[:, j:j + 1]
            except Exception:
                return tile

        def bcast(tile, j, n=TC):
            free = int(tile.shape[1])
            return bass.AP(tile, j, [[free, 128], [0, n]])

        def chunk(tile, c):
            return tile[:, c * TC:(c + 1) * TC]

        blk = nc.Block(); blk.__enter__()

        @blk.sync
        def _(sy):
            for c in range(NCH):
                for s in range(NS):
                    sy.dma_start(out=chunk(x_sb[s], c),
                                 in_=xT[s * 128:(s + 1) * 128, c * TC:(c + 1) * TC]
                                 ).then_inc(sems["LX"], 16)
                    C.inc(None, "LX", sems, 16)
                if c >= 1:
                    sy.wait_ge(sems["LX"], 16 * NS * c)
                    i = sy.nop()
                    C.inc(i, "F", sems, 1, label=f"lxf:{c - 1}")
            sy.wait_ge(sems["LX"], 16 * NS * NCH)
            i = sy.nop()
            C.inc(i, "F", sems, 1, label=f"lxf:{NCH - 1}")
            def _wload(k):
                s, hf = k // 2, k % 2
                if k == 0:
                    C.wait(sy, f"p:wr1:{2 * NS - 1}", sems)
                else:
                    C.wait(sy, f"p:wq:{k - 1}", sems)
                sy.dma_start(out=wstage[:],
                             in_=wffn[s * 128:(s + 1) * 128,
                                      hf * TC:(hf + 1) * TC]
                             ).then_inc(sems["LW"], 16)
                C.inc(None, "LW", sems, 16, label=f"p:w2:{k}")

            wl_sched = {4: [0, 1, 2, 3, 4], 5: [5, 6, 7, 8],
                        6: [9, 10, 11, 12], 7: [13, 14, 15]}
            for c in range(NCH):
                C.wait(sy, f"a1:math1:{c}", sems)
                sy.dma_start(out=cmp_m2["x"][:, 4 * c:4 * c + 4],
                             in_=mrow).then_inc(sems["RM"], 16)
                C.inc(None, "RM", sems, 16, label=f"a1:mst:{c}")
                C.wait(sy, f"a1:sr:{c}", sems)
                sy.dma_start(out=cmp_s["x"][:, 4 * c:4 * c + 4],
                             in_=srow_s).then_inc(sems["RQ"], 16)
                C.inc(None, "RQ", sems, 16, label=f"a1:sst:{c}")
                for k in wl_sched.get(c, []):
                    _wload(k)

        @blk.gpsimd
        def _(gp):
            gp.dma_start(out=vct[:], in_=vecs_h.ap()[:]).then_inc(sems["GQ"], 16)
            C.inc(None, "GQ", sems, 16)
            gp.wait_ge(sems["GQ"], C.cnt["GQ"])
            i = gp.nop()
            C.inc(i, "F", sems, 1, label="vecf")
            for k in range(2 * NS):
                s, hf = k // 2, k % 2
                if k >= 1:
                    C.wait(gp, f"p:wr1:{k - 1}", sems)
                gp.dma_start(out=wstage[:],
                             in_=wffn[s * 128:(s + 1) * 128,
                                      hf * TC:(hf + 1) * TC]
                             ).then_inc(sems["GQ"], 16)
                C.inc(None, "GQ", sems, 16, label=f"p:w1:{k}")
            for s in range(NS):
                C.wait(gp, f"p:wq:{2 * s + 1}", sems)
                gp.dma_start(out=wq_scr[s * 128:(s + 1) * 128, :],
                             in_=w1[s][:]).then_inc(sems["GQ"], 16)
                C.inc(None, "GQ", sems, 16, label=f"p:wqs:{s}")
            for s in range(NS):
                gp.dma_start(out=w2[s][:], in_=win[s * 128:(s + 1) * 128, :]
                             ).then_inc(sems["GQ"], 16)
                C.inc(None, "GQ", sems, 16)
            gp.wait_ge(sems["GQ"], C.cnt["GQ"])
            i = gp.nop()
            C.inc(i, "F", sems, 1, label="lwinf")
            C.wait(gp, "a1:slots", sems)
            i = gp.tensor_reduce(gsl1[0:1, 0:1], mx1[:],
                                 mybir.AxisListType.XYZWC, ALU.max)
            C.inc(i, "G", sems, 1, label="a1:gmax")
        @blk.vector
        def _(v):
            i = v.memset(carry[:], 0.0)
            C.inc(i, "V", sems, 1)
            i = v.memset(ones_bf[:], 1.0)
            C.inc(i, "V", sems, 1, label="p:ones")
            v.memset(ones1_t[:], 1.0)
            i = v.memset(cpk[:, 15:16], 1.0)
            C.inc(i, "V", sems, 1, label="p:ones1")
            v.wait_ge(sems["V"], C.cnt["V"])

            def _wpass1(k):
                C.wait(v, f"p:w1:{k}", sems)
                if k == 0:
                    i = v.tensor_reduce(wsum[:], wstage[:], AXX, ALU.add,
                                        apply_absolute_value=True)
                    nv = C.inc(i, "V", sems, 1, label=f"p:wr1:{k}")
                    v.wait_ge(sems["V"], nv)
                else:
                    i = v.tensor_reduce(wred[:], wstage[:], AXX, ALU.add,
                                        apply_absolute_value=True)
                    nv = C.inc(i, "V", sems, 1)
                    v.wait_ge(sems["V"], nv)
                    i = v.tensor_tensor(wsum[:], wsum[:], wred[:], ALU.add)
                    nv = C.inc(i, "V", sems, 1, label=f"p:wr1:{k}")
                    v.wait_ge(sems["V"], nv)

            C.wait(v, "p:exp", sems)
            v.tensor_scalar(a_cols[:], a_cols[:], -1.0, None, ALU.mult)
            C.wait(v, "vecf", sems)
            v.tensor_tensor(bb_cols[:], vc["bin"][:], vc["bvec"][:], ALU.mult)
            i = v.tensor_scalar(nd_colsb[:], vc["dd"][:], -1.0, None, ALU.mult)
            C.inc(i, "V", sems, 1, label="p:vecd")
            for c in range(NCH + 2):
                if c >= 2:
                    cc = c - 2
                    C.wait(v, f"a1:replmu:{cc}", sems)
                    C.wait(v, f"a1:repls:{cc}", sems)
                    pm, ps_ = PB[2 + (cc % 2) * 2], PB[3 + (cc % 2) * 2]
                    for s in range(NS):
                        v.tensor_tensor(tn1[s % 2][:], chunk(x_sb[s], cc),
                                        pm[:], ALU.subtract)
                        v.tensor_tensor(tn1[s % 2][:], tn1[s % 2][:], ps_[:],
                                        ALU.mult)
                        v.tensor_scalar(tn1[s % 2][:], tn1[s % 2][:],
                                        col(vc["gam"], s), col(vc["bet"], s),
                                        ALU.mult, ALU.add)
                        k = cc * NS + s
                        i = v.tensor_reduce(mslots[:, k:k + 1], tn1[s % 2][:],
                                            AXX, ALU.max,
                                            apply_absolute_value=True)
                        C.inc(i, "V", sems, 1, label=f"a1:sl:{cc}:{s}")
                if c <= 3:
                    for k in range(4 * c, 4 * c + 4):
                        _wpass1(k)
                if 1 <= c <= NCH:
                    cc = c - 1
                    C.wait(v, f"a1:sg:{cc}", sems)
                    if cc >= 1:
                        C.wait(v, f"a1:sr:{cc - 1}", sems)
                    i = v.reciprocal(dvt, dvin)
                    nv = C.inc(i, "V", sems, 1)
                    v.wait_ge(sems["V"], nv)
                    i = v.tensor_scalar(dvout, dvt, 1.0, None, ALU.mult)
                    C.inc(i, "V", sems, 1, label=f"a1:rc:{cc}")
                if c == 4:
                    def _ss(i):
                        nv = C.inc(i, "V", sems, 1)
                        v.wait_ge(sems["V"], nv)
                        return nv
                    C.wait(v, "p:wsmm", sems)
                    _ss(v.tensor_copy(wred[0:1, 0:1], PB[7][0:1, 2:3]))
                    _ss(v.tensor_scalar(beta_c[0:1, :], wred[0:1, 0:1],
                                        1.0 / (D * D), None, ALU.mult))
                    _ss(v.tensor_scalar(stg_b, beta_c[0:1, :], 0.5, None,
                                        ALU.mult))
                    _ss(v.tensor_scalar(inv_t[0:1, :], beta_c[0:1, :], 1e-5,
                                        None, ALU.add))
                    _ss(v.reciprocal(mx2[0:1, :], inv_t[0:1, :]))
                    i = v.tensor_scalar(stg_a, mx2[0:1, :], 1.0, None, ALU.mult)
                    nv = C.inc(i, "V", sems, 1, label="p:invrow")
                    v.wait_ge(sems["V"], nv)
                    C.wait(v, "p:invmm", sems)
                    i = v.tensor_copy(cpk[:, 7:9], PB[7][:, 0:2])
                    nv = C.inc(i, "V", sems, 1, label="p:inv")
                    v.wait_ge(sems["V"], nv)
                if 5 <= c <= 8:
                    for k in range(4 * (c - 5), 4 * (c - 4)):
                        C.wait(v, f"p:w2:{k}", sems)
                        v.tensor_scalar(wstage[:], wstage[:], col(inv_c, 0),
                                        None, ALU.mult)
                        half = w1[k // 2][:, (k % 2) * TC:(k % 2 + 1) * TC]
                        v.tensor_scalar(wsb[:], wstage[:], -0.5, None, ALU.is_le)
                        v.tensor_scalar(half, wstage[:], 0.5, None, ALU.is_ge)
                        i = v.tensor_tensor(half, half, wsb[:], ALU.subtract)
                        C.inc(i, "V", sems, 1, label=f"p:wq:{k}")
                if c <= NCH - 1:
                    C.wait(v, f"a1:stat:{c}", sems)
                    if c >= 1:
                        C.wait(v, f"a1:replmu:{c - 1}", sems)
                        C.wait(v, f"a1:mst:{c - 1}", sems)
                    v.tensor_scalar(mrow, PB[0][0:1, :], INV_D, None, ALU.mult)
                    i = v.tensor_scalar(vrow, PB[1][0:1, :], FFN_A, FFN_B,
                                        ALU.mult, ALU.add)
                    nv = C.inc(i, "V", sems, 1)
                    v.wait_ge(sems["V"], nv)
                    i = v.scalar_tensor_tensor(qrow, mrow, FFN_M, mrow,
                                               ALU.mult, ALU.mult)
                    nv = C.inc(i, "V", sems, 1)
                    v.wait_ge(sems["V"], nv)
                    i = v.tensor_tensor(vrow, vrow, qrow, ALU.subtract)
                    C.inc(i, "V", sems, 1, label=f"a1:math1:{c}")
            v.wait_ge(sems["V"], C.cnt["V"])
            i = v.tensor_reduce(mx1[:], mslots[:], AXX, ALU.max)
            nv = C.inc(i, "V", sems, 1, label="a1:slots")
            v.wait_ge(sems["V"], nv)
            C.wait(v, "a1:gmax", sems)
            i = v.tensor_scalar(stg_a, gsl1[0:1, 0:1], THR_C, None, ALU.mult)
            nv = C.inc(i, "V", sems, 1)
            v.wait_ge(sems["V"], nv)
            i = v.tensor_scalar(stg_b, stg_a, -1.0, None, ALU.mult)
            nv = C.inc(i, "V", sems, 1, label="thr1v")
            v.wait_ge(sems["V"], nv)

        @blk.scalar
        def _(sc):
            C.wait(sc, "vecf", sems)
            i = sc.activation(a_cols[:], vc["la"][:], ACT.Exp)
            C.inc(i, "S", sems, 1, label="p:exp")

            def _casts(c):
                C.wait(sc, f"lxf:{c}", sems)
                for s in range(NS):
                    if c >= 2:
                        C.wait(sc, f"a1:statmm:{c - 2}:{s}", sems)
                    sc.copy(bufX[c % 2][s][:], chunk(x_sb[s], c))
                    if c >= 1:
                        C.wait(sc, f"a1:statmm:{c - 1}:{s}", sems)
                    i = sc.activation(sqv[s][:], bufX[c % 2][s][:],
                                      ACT.Square)
                    C.inc(i, "S", sems, 1, label=f"a1:cast:{c}:{s}")

            def _dance_back(cc):
                C.wait(sc, f"a1:rc:{cc}", sems)
                if cc >= 1:
                    C.wait(sc, f"a1:repls:{cc - 1}", sems)
                    C.wait(sc, f"a1:sst:{cc - 1}", sems)
                sc.dma_start(out=srow_s, in_=dvout,
                             allow_slow_non_contiguous=True
                             ).then_inc(sems["RS"], 16)
                rs = C.inc(None, "RS", sems, 16)
                sc.wait_ge(sems["RS"], rs)
                i = sc.nop()
                C.inc(i, "S", sems, 1, label=f"a1:sr:{cc}")

            _casts(0)
            for c in range(NCH):
                if c + 1 <= NCH - 1:
                    _casts(c + 1)
                if c >= 1:
                    _dance_back(c - 1)
                C.wait(sc, f"a1:math1:{c}", sems)
                i = sc.activation(vrow, vrow, ACT.Sqrt)
                ns = C.inc(i, "S", sems, 1)
                sc.wait_ge(sems["S"], ns)
                sc.dma_start(out=dvin, in_=vrow,
                             allow_slow_non_contiguous=True
                             ).then_inc(sems["RS"], 16)
                rs = C.inc(None, "RS", sems, 16)
                sc.wait_ge(sems["RS"], rs)
                i = sc.nop()
                C.inc(i, "S", sems, 1, label=f"a1:sg:{c}")
            _dance_back(NCH - 1)

        @blk.tensor
        def _(t):
            C.wait(t, "p:ones", sems)
            C.wait(t, "p:ones1", sems)
            for c in range(NCH + 2):
                if c >= 2 and c - 2 <= NCH - 1:
                    cc = c - 2
                    C.wait(t, f"a1:sr:{cc}", sems)
                    if cc >= 2:
                        C.wait(t, f"a1:sl:{cc - 2}:{NS - 1}", sems)
                    i = t.matmul(PB[3 + (cc % 2) * 2][:], ones1, srow_s,
                                 start=True, stop=True)
                    C.inc(i, "T", sems, 1, label=f"a1:repls:{cc}")
                if c <= NCH - 1:
                    for s in range(NS):
                        C.wait(t, f"a1:cast:{c}:{s}", sems)
                        if c >= 1 and s == 0:
                            C.wait(t, f"a1:math1:{c - 1}", sems)
                        t.matmul(PB[0][:], ones_bf[:], bufX[c % 2][s][:],
                                 start=(s == 0), stop=(s == NS - 1))
                        i = t.matmul(PB[1][:], ones_bf[:], sqv[s][:],
                                     start=(s == 0), stop=(s == NS - 1))
                        C.inc(i, "T", sems, 1, label=f"a1:statmm:{c}:{s}")
                    C.mark("T", f"a1:stat:{c}")
                if c == 3:
                    C.wait(t, f"p:wr1:{2 * NS - 1}", sems)
                    i = t.matmul(PB[7][0:1, 2:3], wsum, cpk[:, 15:16],
                                 start=True, stop=True)
                    C.inc(i, "T", sems, 1, label="p:wsmm")
                if c == 4:
                    C.wait(t, "p:invrow", sems)
                    i = t.matmul(PB[7][:, 0:2], ones1, cpk[0:1, 13:15],
                                 start=True, stop=True)
                    C.inc(i, "T", sems, 1, label="p:invmm")
                if c >= 1 and c - 1 <= NCH - 1:
                    cc = c - 1
                    C.wait(t, f"a1:math1:{cc}", sems)
                    if cc >= 2:
                        C.wait(t, f"a1:sl:{cc - 2}:{NS - 1}", sems)
                    i = t.matmul(PB[2 + (cc % 2) * 2][:], ones1, mrow,
                                 start=True, stop=True)
                    C.inc(i, "T", sems, 1, label=f"a1:replmu:{cc}")

        for e in ("V", "S", "T", "G"):
            C.mark(e, f"end:A1:{e}")

        def barrier(eng, sweep):
            for e in ("V", "S", "T", "G"):
                C.wait(eng, f"end:{sweep}:{e}", sems)

        blk.__exit__(None, None, None)
        blk2 = nc.Block(); blk2.__enter__()
        mrowB = scr3[0:1, :]
        vrowB = vrow
        qrowB = qrow

        @blk2.sync
        def _(sy):
            for c in range(NCH):
                if c == 0:
                    C.wait(sy, f"a1:sst:{NCH - 1}", sems)
                    C.wait(sy, f"a1:mst:{NCH - 1}", sems)
                else:
                    C.wait(sy, f"b1:repl:{c - 1}", sems)
                sy.dma_start(out=srow[:], in_=cmp_s["x"][:, 4 * c:4 * c + 4]
                             ).then_inc(sems["RQ"], 16)
                C.inc(None, "RQ", sems, 16)
                sy.dma_start(out=m2row[:], in_=cmp_m2["x"][:, 4 * c:4 * c + 4]
                             ).then_inc(sems["RQ"], 16)
                C.inc(None, "RQ", sems, 16, label=f"b1:rows:{c}")
                if c >= 1:
                    C.wait(sy, f"b1:math1:{c - 1}", sems)
                    sy.dma_start(out=cmp_mu[:, 4 * (c - 1):4 * (c - 1) + 4],
                                 in_=mrowB).then_inc(sems["RM"], 16)
                    C.inc(None, "RM", sems, 16, label=f"b1:mst:{c - 1}")
                    if DEBUG:
                        C.wait(sy, f"b1:dbgq:{c - 1}", sems)
                        sy.dma_start(out=dbgr[:, 4 * (c - 1):4 * (c - 1) + 4],
                                     in_=qrowB).then_inc(sems["RQ"], 16)
                        C.inc(None, "RQ", sems, 16, label=f"b1:dbgst:{c - 1}")
            C.wait(sy, f"b1:math1:{NCH - 1}", sems)
            sy.dma_start(out=cmp_mu[:, 4 * (NCH - 1):4 * NCH],
                         in_=mrowB).then_inc(sems["RM"], 16)
            C.inc(None, "RM", sems, 16, label=f"b1:mst:{NCH - 1}")
            if DEBUG:
                C.wait(sy, f"b1:dbgq:{NCH - 1}", sems)
                sy.dma_start(out=dbgr[:, 4 * (NCH - 1):4 * NCH],
                             in_=qrowB).then_inc(sems["RQ"], 16)
                C.inc(None, "RQ", sems, 16, label=f"b1:dbgst:{NCH - 1}")

        @blk2.tensor
        def _(t):
            barrier(t, "A1")
            C.wait(t, "thr1v", sems)
            i = t.matmul(PB[6][:, 0:2], ones1, cpk[0:1, 13:15],
                         start=True, stop=True)
            C.inc(i, "T", sems, 1, label="thr1mm")
            for c in range(NCH + 1):
                if c <= NCH - 1:
                    C.wait(t, f"b1:rows:{c}", sems)
                    if c >= 2:
                        C.wait(t, f"b1:xq:{c - 2}", sems)
                    r1, r2 = PB[2 + (c % 2) * 2], PB[3 + (c % 2) * 2]
                    t.matmul(r1[:], ones1, srow[:], start=True, stop=True)
                    i = t.matmul(r2[:], ones1, m2row[:], start=True, stop=True)
                    C.inc(i, "T", sems, 1, label=f"b1:repl:{c}")
                if c >= 1:
                    cc = c - 1
                    C.wait(t, f"b1:xq:{cc}", sems)
                    i = None
                    for e in range(NS):
                        k = cc * NS + e
                        if k >= 2:
                            C.wait(t, f"b1:evac:{(k - 2) // NS}:{(k - 2) % NS}",
                                   sems)
                        for s in range(NS):
                            i = t.matmul(PB[6 + e % 2][:],
                                         w1[s][:, e * 128:(e + 1) * 128],
                                         bufX[cc % 2][s][:], start=(s == 0),
                                         stop=(s == NS - 1))
                        C.inc(i, "T", sems, 1, label=f"b1:mm:{cc}:{e}")
                    for s in range(NS):
                        C.wait(t, f"b1:cast1:{cc}:{s}", sems)
                        if cc >= 1 and s == 0:
                            C.wait(t, f"b1:math1:{cc - 1}", sems)
                        t.matmul(PB[0][:], ones_bf[:], bufX[cc % 2][s][:],
                                 start=(s == 0), stop=(s == NS - 1))
                        i = t.matmul(PB[1][:], ones_bf[:], sqt[s % 2][:],
                                     start=(s == 0), stop=(s == NS - 1))
                        C.inc(i, "T", sems, 1, label=f"b1:statmm:{cc}:{s}")
                    C.mark("T", f"b1:stat:{cc}")
            C.mark("T", "b1:mmall")

        @blk2.vector
        def _(v):
            barrier(v, "A1")
            C.wait(v, "thr1mm", sems)
            i = v.tensor_copy(cpk[:, 9:11], PB[6][:, 0:2])
            nv = C.inc(i, "V", sems, 1)
            v.wait_ge(sems["V"], nv)
            for c in range(NCH + 2):
                if c <= NCH - 1 or c == NCH:
                    cc = c - 1
                    if c <= NCH - 1:
                        C.wait(v, f"b1:repl:{c}", sems)
                        if c >= 2:
                            C.wait(v, f"b1:statmm:{c - 2}:{NS - 1}", sems)
                            if DEBUG and c - 2 == 4:
                                C.wait(v, f"b1:dbgbx:4", sems)
                    r1, r2 = PB[2 + (c % 2) * 2], PB[3 + (c % 2) * 2]
                    i = None
                    for s in range(NS):
                        if c <= NCH - 1:
                            v.tensor_tensor(b_sb[:], chunk(x_sb[s], c), r2[:],
                                            ALU.subtract)
                            v.tensor_tensor(s_rep[:], b_sb[:], r1[:], ALU.mult)
                            v.tensor_scalar(b_sb[:], s_rep[:],
                                            col(vc["gam"], s),
                                            col(vc["bet"], s),
                                            ALU.mult, ALU.add)
                            i = v.tensor_scalar(bufX[c % 2][s][:], b_sb[:],
                                                col(thr1, 0), col(nthr1, 0),
                                                ALU.min, ALU.max)
                            if s == NS - 1:
                                C.inc(i, "V", sems, 1, label=f"b1:xq:{c}")
                        if 1 <= c:
                            C.wait(v, f"b1:mm:{cc}:{s}", sems)
                            i = v.scalar_tensor_tensor(chunk(x_sb[s], cc),
                                                       PB[6 + s % 2][:],
                                                       col(hbeta_c, 0),
                                                       chunk(x_sb[s], cc),
                                                       ALU.mult, ALU.add)
                            C.inc(i, "V", sems, 1, label=f"b1:evac:{cc}:{s}")
                if 2 <= c <= NCH + 1:
                    cc = c - 2
                    C.wait(v, f"b1:sg:{cc}", sems)
                    if cc >= 1:
                        C.wait(v, f"b1:sr:{cc - 1}", sems)
                    i = v.reciprocal(dvt, dvin)
                    nv = C.inc(i, "V", sems, 1)
                    v.wait_ge(sems["V"], nv)
                    i = v.tensor_scalar(dvout, dvt, 1.0, None, ALU.mult)
                    C.inc(i, "V", sems, 1, label=f"b1:rc:{cc}")
                if 1 <= c <= NCH:
                    cc = c - 1
                    C.wait(v, f"b1:stat:{cc}", sems)
                    if cc >= 1:
                        C.wait(v, f"b1:mst:{cc - 1}", sems)
                    v.tensor_scalar(mrowB, PB[0][0:1, :], INV_D, None, ALU.mult)
                    i = v.tensor_scalar(vrowB, PB[1][0:1, :], SSM_A, SSM_B,
                                        ALU.mult, ALU.add)
                    nv = C.inc(i, "V", sems, 1)
                    v.wait_ge(sems["V"], nv)
                    i = v.scalar_tensor_tensor(qrowB, mrowB, SSM_M, mrowB,
                                               ALU.mult, ALU.mult)
                    nv = C.inc(i, "V", sems, 1)
                    v.wait_ge(sems["V"], nv)
                    i = v.tensor_tensor(vrowB, vrowB, qrowB, ALU.subtract)
                    C.inc(i, "V", sems, 1, label=f"b1:math1:{cc}")
                    if DEBUG:
                        if cc >= 1:
                            C.wait(v, f"b1:dbgst:{cc - 1}", sems)
                        i = v.tensor_copy(qrowB, PB[0][0:1, :])
                        C.inc(i, "V", sems, 1, label=f"b1:dbgq:{cc}")
            C.wait(v, f"b1:sr:{NCH - 1}", sems)
            C.wait(v, f"b1:mst:{NCH - 1}", sems)
            i = v.tensor_tensor(cmp_m2["x1"][:], cmp_mu[:], cmp_s["x1"][:],
                                ALU.mult)
            nv = C.inc(i, "V", sems, 1, label="b1:m2c")
            v.wait_ge(sems["V"], nv)

        @blk2.scalar
        def _(sc):
            barrier(sc, "A1")
            for c in range(1, NCH + 2):
                if c - 1 <= NCH - 1:
                    cc = c - 1
                    for s in range(NS):
                        k = cc * NS + s
                        C.wait(sc, f"b1:evac:{cc}:{s}", sems)
                        C.wait(sc, f"b1:mm:{cc}:{NS - 1}", sems)
                        sc.copy(bufX[cc % 2][s][:], chunk(x_sb[s], cc))
                        if k >= 1:
                            C.wait(sc,
                                   f"b1:statmm:{(k - 1) // NS}:{(k - 1) % NS}",
                                   sems)
                        i = sc.activation(sqt[s % 2][:], bufX[cc % 2][s][:],
                                          ACT.Square)
                        C.inc(i, "S", sems, 1, label=f"b1:cast1:{cc}:{s}")
                if c >= 2:
                    cc = c - 2
                    C.wait(sc, f"b1:rc:{cc}", sems)
                    sc.dma_start(out=cmp_s["x1"][:, 4 * cc:4 * cc + 4],
                                 in_=dvout, allow_slow_non_contiguous=True
                                 ).then_inc(sems["RS"], 16)
                    rs = C.inc(None, "RS", sems, 16)
                    sc.wait_ge(sems["RS"], rs)
                    i = sc.nop()
                    C.inc(i, "S", sems, 1, label=f"b1:sr:{cc}")
                if c - 1 <= NCH - 1:
                    cc = c - 1
                    C.wait(sc, f"b1:math1:{cc}", sems)
                    i = sc.activation(vrowB, vrowB, ACT.Sqrt)
                    ns = C.inc(i, "S", sems, 1)
                    sc.wait_ge(sems["S"], ns)
                    sc.dma_start(out=dvin, in_=vrowB,
                                 allow_slow_non_contiguous=True
                                 ).then_inc(sems["RS"], 16)
                    rs = C.inc(None, "RS", sems, 16)
                    sc.wait_ge(sems["RS"], rs)
                    i = sc.nop()
                    C.inc(i, "S", sems, 1, label=f"b1:sg:{cc}")

        @blk2.gpsimd
        def _(gp):
            barrier(gp, "A1")
            if DEBUG:
                C.wait(gp, "b1:stat:4", sems)
                for s in range(NS):
                    gp.dma_start(out=dbga[:, s * TC:(s + 1) * TC],
                                 in_=bufX[0][s][:]).then_inc(sems["GQ"], 16)
                    C.inc(None, "GQ", sems, 16,
                          label=f"b1:dbgbx:4" if s == NS - 1 else None)
            C.wait(gp, "b1:mmall", sems)
            for s in range(NS):
                gp.dma_start(out=w1[s][:], in_=cmat[s * 128:(s + 1) * 128, :]
                             ).then_inc(sems["GQ"], 16)
                C.inc(None, "GQ", sems, 16)
            gp.wait_ge(sems["GQ"], C.cnt["GQ"])
            i = gp.nop()
            C.inc(i, "F", sems, 1, label="lcmatf")

        for e in ("V", "S", "T", "G"):
            C.mark(e, f"end:B1:{e}")

        blk2.__exit__(None, None, None)
        blk3 = nc.Block(); blk3.__enter__()
        mrowS = m2row[0:1, :]
        vrowS = scr2[0:1, :]
        qrowS = wstage[0:1, :]

        @blk3.sync
        def _(sy):
            if DEBUG:
                for e in ("V", "S", "T", "G"):
                    C.wait(sy, f"end:B1:{e}", sems)
                for s in range(NS):
                    sy.dma_start(out=dbg1[s * 128:(s + 1) * 128, :],
                                 in_=x_sb[s][:]).then_inc(sems["RQ"], 16)
                    C.inc(None, "RQ", sems, 16)
            for c in range(NCH):
                if c == 0:
                    C.wait(sy, f"b1:sr:{NCH - 1}", sems)
                    C.wait(sy, "b1:m2c", sems)
                else:
                    C.wait(sy, f"ssm:repl:{c - 1}", sems)
                if c >= 3:
                    C.wait(sy, f"ssm:rep3:{c - 3}", sems)
                sy.dma_start(out=srow[:], in_=cmp_s["x1"][:, 4 * c:4 * c + 4]
                             ).then_inc(sems["RQ"], 16)
                C.inc(None, "RQ", sems, 16, label=f"ssm:rows:{c}")
                if c >= 1:
                    C.wait(sy, f"ssm:math1:{c - 1}", sems)
                    sy.dma_start(out=cmp_m2["x2"][:, 4 * (c - 1):4 * (c - 1) + 4],
                                 in_=mrowS).then_inc(sems["RM"], 16)
                    C.inc(None, "RM", sems, 16, label=f"ssm:mst:{c - 1}")
            C.wait(sy, f"ssm:math1:{NCH - 1}", sems)
            sy.dma_start(out=cmp_m2["x2"][:, 4 * (NCH - 1):4 * NCH],
                         in_=mrowS).then_inc(sems["RM"], 16)
            C.inc(None, "RM", sems, 16, label=f"ssm:mst:{NCH - 1}")

        @blk3.gpsimd
        def _(gp):
            barrier(gp, "B1")
            for c in range(NCH + 2):
                if c <= NCH - 1:
                    if c > 0:
                        C.wait(gp, f"ssm:umm:{c - 1}:{NS - 1}", sems)
                        C.wait(gp, f"ssm:ymm:{c - 1}:{NS - 1}", sems)
                    gp.dma_start(out=murow_b[:], in_=cmp_mu[:, 4 * c:4 * c + 4]
                                 ).then_inc(sems["GQ"], 16)
                    C.inc(None, "GQ", sems, 16)
                    gp.dma_start(out=m2row_b[:],
                                 in_=cmp_m2["x1"][:, 4 * c:4 * c + 4]
                                 ).then_inc(sems["GQ"], 16)
                    C.inc(None, "GQ", sems, 16, label=f"ssm:browz:{c}")
                if c >= 2:
                    cc = c - 2
                    for s in range(NS):
                        k = cc * NS + s
                        C.wait(gp, f"ssm:ab:{cc}:{s}", sems)
                        if cc == 0 and s == 0:
                            C.wait(gp, "thr1v", sems)
                        i = gp.tensor_reduce(gsl2[0:1, k:k + 1], tn1[s % 2][:],
                                             mybir.AxisListType.XYZWC, ALU.max)
                        C.inc(i, "G", sems, 1, label=f"ssm:red:{cc}:{s}")
                if c <= NCH - 1:
                    for f in range(NS):
                        C.wait(gp, f"ssm:xy:{c}:{f}", sems)
                        i = gp.tensor_tensor(chunk(x_sb[f], c),
                                             chunk(x_sb[f], c),
                                             (scr1 if f % 2 == 0 else scr3)[:],
                                             ALU.add)
                        C.inc(i, "G", sems, 1, label=f"ssm:xt4:{c}:{f}")
            C.wait(gp, "ssm:ulast", sems)
            gp.wait_ge(sems["GQ"], C.cnt["GQ"])
            for s in range(NS):
                gp.dma_start(out=w2[s][:], in_=wq_scr[s * 128:(s + 1) * 128, :]
                             ).then_inc(sems["GQ"], 16)
                C.inc(None, "GQ", sems, 16)
            gp.wait_ge(sems["GQ"], C.cnt["GQ"])
            i = gp.nop()
            C.inc(i, "F", sems, 1, label="lwq2f")

        @blk3.tensor
        def _(t):
            barrier(t, "B1")
            i = None
            for e in range(NS):
                if e == 0:
                    C.wait(t, "lwinf", sems)
                for s in range(NS):
                    i = t.matmul(PB[2][:, e:e + 1],
                                 w2[s][:, e * 128:(e + 1) * 128],
                                 onescol_b[:], start=(s == 0),
                                 stop=(s == NS - 1))
                C.inc(i, "T", sems, 1, label=f"ssm:ws:{e}")
            for c in range(NCH):
                C.wait(t, f"ssm:rows:{c}", sems)
                if c == 0:
                    C.wait(t, f"ssm:wsc:{NS - 1}", sems)
                if c >= 2:
                    C.wait(t, f"ssm:srepdone:{c - 2}", sems)
                i = t.matmul(PB[2 + c % 2][:], ones1, srow[:],
                             start=True, stop=True)
                C.inc(i, "T", sems, 1, label=f"ssm:repl:{c}")
                for s in range(NS):
                    C.wait(t, f"ssm:cast1:{c}:{s}", sems)
                C.wait(t, f"ssm:browz:{c}", sems)
                if c == 0:
                    C.wait(t, "p:vecws", sems)
                for e in range(NS):
                    k = c * NS + e
                    if e == 4 and c >= 2:
                        cc = c - 2
                        C.wait(t, f"ssm:sr:{cc}", sems)
                        C.wait(t, f"ssm:math1:{c - 1}", sems)
                        t.matmul(PB[0][:], ones1, srow[:], start=True,
                                 stop=True)
                        i = t.matmul(PB[1][:], ones1, mrowS, start=True,
                                     stop=True)
                        C.inc(i, "T", sems, 1, label=f"ssm:rep3:{cc}")
                    if k >= 2:
                        C.wait(t, f"ssm:b:{(k - 2) // NS}:{(k - 2) % NS}", sems)
                    for s in range(NS):
                        t.matmul(PB[4 + e % 2][:],
                                 w2[s][:, e * 128:(e + 1) * 128],
                                 bufX[c % 2][s][:], start=(s == 0), stop=False)
                    i = t.matmul(PB[4 + e % 2][:], nwrow[e][:], murow_b[:],
                                 start=False, stop=True)
                    C.inc(i, "T", sems, 1, label=f"ssm:umm:{c}:{e}")
                if c == NCH - 1:
                    C.mark("T", "ssm:ulast")
                for f in range(NS):
                    k = c * NS + f
                    C.wait(t, f"ssm:hhat:{c}:{NS - 1}", sems)
                    if k >= 2:
                        C.wait(t, f"ssm:xy:{(k - 2) // NS}:{(k - 2) % NS}",
                               sems)
                    if c == 0 and f == 0:
                        C.wait(t, "lcmatf", sems)
                    for e in range(NS):
                        t.matmul(PB[6 + f % 2][:],
                                 w1[e][:, f * 128:(f + 1) * 128],
                                 bufH[e][:], start=(e == 0), stop=False)
                    i = t.matmul(PB[6 + f % 2][:], ndrow[f][:], m2row_b[:],
                                 start=False, stop=True)
                    C.inc(i, "T", sems, 1, label=f"ssm:ymm:{c}:{f}")
                for s in range(NS):
                    C.wait(t, f"ssm:cast2:{c}:{s}", sems)
                    if c >= 2 and s == 0:
                        C.wait(t, f"ssm:xn2:{c - 2}:{NS - 1}", sems)
                    t.matmul(PB[0][:], ones_bf[:], bufX[c % 2][s][:],
                             start=(s == 0), stop=(s == NS - 1))
                    i = t.matmul(PB[1][:], ones_bf[:], sqt[s % 2][:],
                                 start=(s == 0), stop=(s == NS - 1))
                    C.inc(i, "T", sems, 1, label=f"ssm:statmm:{c}:{s}")
                C.mark("T", f"ssm:stat:{c}")
            for cc in (NCH - 2, NCH - 1):
                C.wait(t, f"ssm:sr:{cc}", sems)
                C.wait(t, f"ssm:math1:{NCH - 1}", sems)
                if cc == NCH - 2:
                    C.wait(t, f"ssm:xn2:{NCH - 3}:{NS - 1}", sems)
                else:
                    C.wait(t, f"ssm:xn2:{NCH - 2}:{NS - 1}", sems)
                t.matmul(PB[0][:], ones1, srow[:], start=True, stop=True)
                i = t.matmul(PB[1][:], ones1, mrowS, start=True, stop=True)
                C.inc(i, "T", sems, 1, label=f"ssm:rep3:{cc}")

        @blk3.vector
        def _(v):
            barrier(v, "B1")
            for e in range(NS):
                C.wait(v, f"ssm:ws:{e}", sems)
                i = v.tensor_scalar(col(ws_cols, e), PB[2][:, e:e + 1], -1.0,
                                    None, ALU.mult)
                C.inc(i, "V", sems, 1, label=f"ssm:wsc:{e}")
            for c in range(NCH + 2):
                if c <= NCH - 1:
                    C.wait(v, f"ssm:repl:{c}", sems)
                    i = v.tensor_copy(s_rep[:], PB[2 + c % 2][:])
                    nv = C.inc(i, "V", sems, 1, label=f"ssm:srepdone:{c}")
                    v.wait_ge(sems["V"], nv)
                if 2 <= c <= NCH + 1:
                    cc = c - 2
                    C.wait(v, f"ssm:sg:{cc}", sems)
                    if cc >= 1:
                        C.wait(v, f"ssm:sr:{cc - 1}", sems)
                    i = v.reciprocal(dvt, dvin)
                    nv = C.inc(i, "V", sems, 1)
                    v.wait_ge(sems["V"], nv)
                    i = v.tensor_scalar(dvout, dvt, 1.0, None, ALU.mult)
                    C.inc(i, "V", sems, 1, label=f"ssm:rc:{cc}")
                if 1 <= c <= NCH:
                    cc = c - 1
                    C.wait(v, f"ssm:stat:{cc}", sems)
                    if cc >= 1:
                        C.wait(v, f"ssm:mst:{cc - 1}", sems)
                    v.tensor_scalar(mrowS, PB[0][0:1, :], INV_D, None, ALU.mult)
                    i = v.tensor_scalar(vrowS, PB[1][0:1, :], FFN_A, FFN_B,
                                        ALU.mult, ALU.add)
                    nv = C.inc(i, "V", sems, 1)
                    v.wait_ge(sems["V"], nv)
                    i = v.scalar_tensor_tensor(qrowS, mrowS, FFN_M, mrowS,
                                               ALU.mult, ALU.mult)
                    nv = C.inc(i, "V", sems, 1)
                    v.wait_ge(sems["V"], nv)
                    i = v.tensor_tensor(vrowS, vrowS, qrowS, ALU.subtract)
                    C.inc(i, "V", sems, 1, label=f"ssm:math1:{cc}")
                if c <= NCH - 1:
                    for e in range(NS):
                        C.wait(v, f"ssm:umm:{c}:{e}", sems)
                        if e >= 1:
                            C.wait(v, f"ssm:hcast:{c}:{e - 1}", sems)
                        elif c >= 1:
                            C.wait(v, f"ssm:hcast:{c - 1}:{NS - 1}", sems)
                        v.scalar_tensor_tensor(b_sb[:], PB[4 + e % 2][:],
                                               col(vc["bvec"], e), s_rep[:],
                                               ALU.mult, ALU.mult)
                        i = v.tensor_scalar(b_sb[:], b_sb[:], col(bb_cols, e),
                                            None, ALU.add)
                        C.inc(i, "V", sems, 1, label=f"ssm:b:{c}:{e}")
                        a_bc = bass.AP(vct, NS + e, [[6 * NS, 128], [0, TC]])
                        v.tensor_tensor_scan(h_sb[:], a_bc, b_sb[:],
                                             col(carry, e), ALU.mult, ALU.add)
                        i = v.tensor_copy(col(carry, e), h_sb[:, TC - 1:TC])
                        nv = C.inc(i, "V", sems, 1, label=f"ssm:scan:{c}:{e}")
                        v.wait_ge(sems["V"], nv)
                if 2 <= c <= NCH + 1:
                    cc = c - 2
                    C.wait(v, f"ssm:rep3:{cc}", sems)
                    for s in range(NS):
                        k = cc * NS + s
                        if k >= 2:
                            C.wait(v, f"ssm:red:{(k - 2) // NS}:{(k - 2) % NS}",
                                   sems)
                        C.wait(v, f"ssm:xt4:{cc}:{s}", sems)
                        v.tensor_tensor(tn1[s % 2][:], chunk(x_sb[s], cc),
                                        PB[1][:], ALU.subtract)
                        v.tensor_tensor(tn1[s % 2][:], tn1[s % 2][:], PB[0][:],
                                        ALU.mult)
                        i = v.tensor_scalar(tn1[s % 2][:], tn1[s % 2][:],
                                            col(vc["gam"], s), col(vc["bet"], s),
                                            ALU.mult, ALU.add)
                        C.inc(i, "V", sems, 1, label=f"ssm:xn2:{cc}:{s}")
                if c <= NCH - 1:
                    for f in range(NS):
                        C.wait(v, f"ssm:ymm:{c}:{f}", sems)
                        if f >= 2:
                            C.wait(v, f"ssm:xt4:{c}:{f - 2}", sems)
                        elif c >= 1:
                            C.wait(v, f"ssm:xt4:{c - 1}:{f + NS - 2}", sems)
                        t4buf = scr1 if f % 2 == 0 else scr3
                        v.scalar_tensor_tensor(t4buf[:], chunk(x_sb[f], c),
                                               col(vc["dd"], f), s_rep[:],
                                               ALU.mult, ALU.mult)
                        i = v.tensor_tensor(chunk(x_sb[f], c),
                                            chunk(x_sb[f], c),
                                            PB[6 + f % 2][:], ALU.add)
                        C.inc(i, "V", sems, 1, label=f"ssm:xy:{c}:{f}")
            C.wait(v, f"ssm:red:{NCH - 1}:{NS - 1}", sems)
            i = v.tensor_reduce(mx1[0:1, :], gsl1[0:1, 0:64], AXX, ALU.max)
            nv = C.inc(i, "V", sems, 1)
            v.wait_ge(sems["V"], nv)
            i = v.tensor_scalar(stg_a, mx1[0:1, :], THR_C, None, ALU.mult)
            nv = C.inc(i, "V", sems, 1)
            v.wait_ge(sems["V"], nv)
            i = v.tensor_scalar(stg_b, stg_a, -1.0, None, ALU.mult)
            nv = C.inc(i, "V", sems, 1, label="thr2v")
            v.wait_ge(sems["V"], nv)

        @blk3.scalar
        def _(sc):
            barrier(sc, "B1")
            C.wait(sc, "p:vecd", sems)
            for e in range(NS):
                C.wait(sc, f"ssm:wsc:{e}", sems)
                sc.dma_start(out=nwrow[e][:], in_=col(ws_cols, e),
                             allow_slow_non_contiguous=True
                             ).then_inc(sems["RS"], 16)
                C.inc(None, "RS", sems, 16)
                sc.dma_start(out=ndrow[e][:], in_=col(nd_colsb, e),
                             allow_slow_non_contiguous=True
                             ).then_inc(sems["RS"], 16)
                C.inc(None, "RS", sems, 16, label=f"ssm:rowdma:{e}")
            sc.wait_ge(sems["RS"], C.cnt["RS"])
            i = sc.nop()
            C.inc(i, "S", sems, 1, label="p:vecws")
            for s in range(NS):
                i = sc.copy(bufX[0][s][:], chunk(x_sb[s], 0))
                C.inc(i, "S", sems, 1, label=f"ssm:cast1:0:{s}")
            for c in range(NCH + 2):
                if c + 1 <= NCH - 1:
                    for s in range(NS):
                        if c + 1 >= 2:
                            C.wait(sc, f"ssm:statmm:{c - 1}:{s}", sems)
                        i = sc.copy(bufX[(c + 1) % 2][s][:],
                                    chunk(x_sb[s], c + 1))
                        C.inc(i, "S", sems, 1, label=f"ssm:cast1:{c + 1}:{s}")
                if 2 <= c <= NCH + 1:
                    cc = c - 2
                    C.wait(sc, f"ssm:rc:{cc}", sems)
                    if cc + 2 <= NCH - 1:
                        C.wait(sc, f"ssm:repl:{cc + 2}", sems)
                    elif cc >= 1:
                        C.wait(sc, f"ssm:rep3:{cc - 1}", sems)
                    sc.dma_start(out=srow[:], in_=dvout,
                                 allow_slow_non_contiguous=True
                                 ).then_inc(sems["RS"], 16)
                    rs = C.inc(None, "RS", sems, 16)
                    sc.wait_ge(sems["RS"], rs)
                    sc.dma_start(out=cmp_s["x2"][:, 4 * cc:4 * cc + 4],
                                 in_=dvout, allow_slow_non_contiguous=True
                                 ).then_inc(sems["RS"], 16)
                    rs = C.inc(None, "RS", sems, 16)
                    sc.wait_ge(sems["RS"], rs)
                    i = sc.nop()
                    C.inc(i, "S", sems, 1, label=f"ssm:sr:{cc}")
                if 1 <= c <= NCH:
                    cc = c - 1
                    C.wait(sc, f"ssm:math1:{cc}", sems)
                    i = sc.activation(vrowS, vrowS, ACT.Sqrt)
                    ns = C.inc(i, "S", sems, 1)
                    sc.wait_ge(sems["S"], ns)
                    sc.dma_start(out=dvin, in_=vrowS,
                                 allow_slow_non_contiguous=True
                                 ).then_inc(sems["RS"], 16)
                    rs = C.inc(None, "RS", sems, 16)
                    sc.wait_ge(sems["RS"], rs)
                    i = sc.nop()
                    C.inc(i, "S", sems, 1, label=f"ssm:sg:{cc}")
                if c <= NCH - 1:
                    for e in range(NS):
                        C.wait(sc, f"ssm:scan:{c}:{e}", sems)
                        if c >= 1 and e == 0:
                            C.wait(sc, f"ssm:ymm:{c - 1}:{NS - 1}", sems)
                        i = sc.copy(bufH[e][:], h_sb[:])
                        C.inc(i, "S", sems, 1, label=f"ssm:hhat:{c}:{e}")
                        C.mark("S", f"ssm:hcast:{c}:{e}")
                if 2 <= c <= NCH + 1:
                    cc = c - 2
                    for s in range(NS):
                        C.wait(sc, f"ssm:xn2:{cc}:{s}", sems)
                        i = sc.activation(tn1[s % 2][:], tn1[s % 2][:], ACT.Abs)
                        C.inc(i, "S", sems, 1, label=f"ssm:ab:{cc}:{s}")
                if c <= NCH - 1:
                    for s in range(NS):
                        k = c * NS + s
                        C.wait(sc, f"ssm:xt4:{c}:{s}", sems)
                        C.wait(sc, f"ssm:umm:{c}:{NS - 1}", sems)
                        sc.copy(bufX[c % 2][s][:], chunk(x_sb[s], c))
                        if k >= 1:
                            C.wait(sc,
                                   f"ssm:statmm:{(k - 1) // NS}:{(k - 1) % NS}",
                                   sems)
                        i = sc.activation(sqt[s % 2][:], bufX[c % 2][s][:],
                                          ACT.Square)
                        C.inc(i, "S", sems, 1, label=f"ssm:cast2:{c}:{s}")

        for e in ("V", "S", "T", "G"):
            C.mark(e, f"end:SSM:{e}")

        blk3.__exit__(None, None, None)
        blk4 = nc.Block(); blk4.__enter__()

        @blk4.sync
        def _(sy):
            if DEBUG:
                for e in ("V", "S", "T", "G"):
                    C.wait(sy, f"end:SSM:{e}", sems)
                for s in range(NS):
                    sy.dma_start(out=dbg2[s * 128:(s + 1) * 128, :],
                                 in_=x_sb[s][:]).then_inc(sems["RQ"], 16)
                    C.inc(None, "RQ", sems, 16)
            def _b2_store(c):
                for e in range(NS):
                    C.wait(sy, f"b2:evac:{c}:{e}", sems)
                    sy.dma_start(out=outT[e * 128:(e + 1) * 128,
                                          c * TC:(c + 1) * TC],
                                 in_=chunk(x_sb[e], c)).then_inc(sems["ST"], 16)
                    C.inc(None, "ST", sems, 16, label=f"st:{c}:{e}")

            for c in range(NCH):
                if c == 0:
                    C.wait(sy, f"ssm:sr:{NCH - 1}", sems)
                    C.wait(sy, f"ssm:mst:{NCH - 1}", sems)
                    C.wait(sy, f"ssm:rep3:{NCH - 1}", sems)
                else:
                    C.wait(sy, f"b2:repl:{c - 1}", sems)
                sy.dma_start(out=srow[:], in_=cmp_s["x2"][:, 4 * c:4 * c + 4]
                             ).then_inc(sems["RQ"], 16)
                C.inc(None, "RQ", sems, 16)
                sy.dma_start(out=m2row[:], in_=cmp_m2["x2"][:, 4 * c:4 * c + 4]
                             ).then_inc(sems["RQ"], 16)
                C.inc(None, "RQ", sems, 16, label=f"b2:rows:{c}")
                if c >= 1:
                    _b2_store(c - 1)
            _b2_store(NCH - 1)

        def _b2_big(t, c):
            C.wait(t, f"b2:xq:{c}", sems)
            i = None
            for e in range(NS):
                k = c * NS + e
                if k >= 2:
                    C.wait(t, f"b2:evac:{(k - 2) // NS}:{(k - 2) % NS}", sems)
                for s in range(NS):
                    i = t.matmul(PB[6 + e % 2][:], w2[s][:, e * 128:(e + 1) * 128],
                                 bufX[c % 2][s][:], start=(s == 0),
                                 stop=(s == NS - 1))
                C.inc(i, "T", sems, 1, label=f"b2:mm:{c}:{e}")

        @blk4.tensor
        def _(t):
            barrier(t, "SSM")
            C.wait(t, "thr2v", sems)
            i = t.matmul(PB[6][:, 0:2], ones1, cpk[0:1, 13:15],
                         start=True, stop=True)
            C.inc(i, "T", sems, 1, label="thr2mm")
            C.wait(t, "lwq2f", sems)
            for c in range(NCH):
                C.wait(t, f"b2:rows:{c}", sems)
                if c >= 2:
                    C.wait(t, f"b2:xq:{c - 2}", sems)
                r1, r2 = PB[2 + (c % 2) * 2], PB[3 + (c % 2) * 2]
                t.matmul(r1[:], ones1, srow[:], start=True, stop=True)
                i = t.matmul(r2[:], ones1, m2row[:], start=True, stop=True)
                C.inc(i, "T", sems, 1, label=f"b2:repl:{c}")
                C.mark("T", f"b2:replq:{c}")
                if c >= 1:
                    _b2_big(t, c - 1)
            _b2_big(t, NCH - 1)

        @blk4.vector
        def _(v):
            barrier(v, "SSM")
            C.wait(v, "thr2mm", sems)
            i = v.tensor_copy(cpk[:, 11:13], PB[6][:, 0:2])
            nv = C.inc(i, "V", sems, 1)
            v.wait_ge(sems["V"], nv)
            for c in range(NCH + 1):
                if c <= NCH - 1:
                    C.wait(v, f"b2:repl:{c}", sems)
                    if c >= 2:
                        C.wait(v, f"b2:mm:{c - 2}:{NS - 1}", sems)
                    r1, r2 = PB[2 + (c % 2) * 2], PB[3 + (c % 2) * 2]
                    i = None
                    for s in range(NS):
                        v.tensor_tensor(b_sb[:], chunk(x_sb[s], c), r2[:],
                                        ALU.subtract)
                        v.tensor_tensor(s_rep[:], b_sb[:], r1[:], ALU.mult)
                        v.tensor_scalar(b_sb[:], s_rep[:], col(vc["gam"], s),
                                        col(vc["bet"], s), ALU.mult, ALU.add)
                        i = v.tensor_scalar(bufX[c % 2][s][:], b_sb[:],
                                            col(thr2, 0), col(nthr2, 0),
                                            ALU.min, ALU.max)
                    C.inc(i, "V", sems, 1, label=f"b2:xq:{c}")
                if c >= 1:
                    cc = c - 1
                    for e in range(NS):
                        C.wait(v, f"b2:mm:{cc}:{e}", sems)
                        i = v.scalar_tensor_tensor(chunk(x_sb[e], cc),
                                                   PB[6 + e % 2][:],
                                                   col(hbeta_c, 0),
                                                   chunk(x_sb[e], cc),
                                                   ALU.mult, ALU.add)
                        C.inc(i, "V", sems, 1, label=f"b2:evac:{cc}:{e}")

        @blk4.gpsimd
        def _(gp):
            barrier(gp, "SSM")
            gp.wait_ge(sems["ST"], 16 * NCH * NS)
            if DEBUG:
                w = NCH * 4
                for j, tt in enumerate([cmp_s["x"], cmp_m2["x"], cmp_s["x1"],
                                        cmp_m2["x1"], cmp_mu, cmp_s["x2"],
                                        cmp_m2["x2"]]):
                    gp.dma_start(out=dbgs[:, j * w:(j + 1) * w], in_=tt[:]
                                 ).then_inc(sems["GQ"], 16)
                    C.inc(None, "GQ", sems, 16)
                gp.dma_start(out=dbgs[:, 7 * w:7 * w + 16], in_=cpk[:]
                             ).then_inc(sems["GQ"], 16)
                C.inc(None, "GQ", sems, 16)
                gp.dma_start(out=dbgs[:, 7 * w + 16:7 * w + 16 + NS],
                             in_=carry[:]).then_inc(sems["GQ"], 16)
                C.inc(None, "GQ", sems, 16)
                gp.wait_ge(sems["GQ"], C.cnt["GQ"])

        blk4.__exit__(None, None, None)

    return C.out


def _build():
    nc1 = bass.Bass("TRN2", target_bir_lowering=False, detect_race_conditions=False)
    labels = _emit(nc1, {}, strict=False)
    nc = bass.Bass("TRN2", target_bir_lowering=False, detect_race_conditions=False)
    labels2 = _emit(nc, labels, strict=True)
    return nc


_BUILT = {}


def kernel(L0_attn_x, L0_attn_W_in, L0_attn_b_in, L0_attn_log_A, L0_attn_B_vec,
           L0_attn_C, L0_attn_D, L0_ffn_weight, L0_ffn_ln_gamma, L0_ffn_ln_beta):
    if "nc" not in _BUILT:
        _BUILT["nc"] = _build()
    nc = _BUILT["nc"]
    vecs = np.stack([
        np.asarray(L0_attn_b_in), np.asarray(L0_attn_log_A),
        np.asarray(L0_attn_B_vec), np.asarray(L0_attn_D),
        np.asarray(L0_ffn_ln_gamma), np.asarray(L0_ffn_ln_beta)
    ]).astype(np.float32, copy=False)
    vecs = np.ascontiguousarray(
        vecs.reshape(6, 8, 128).transpose(2, 0, 1).reshape(128, 48))
    wffn = np.ascontiguousarray(np.asarray(L0_ffn_weight, dtype=np.float32))
    win = np.ascontiguousarray(np.asarray(L0_attn_W_in, dtype=np.float32))
    cmat = np.ascontiguousarray(np.asarray(L0_attn_C, dtype=np.float32))
    x = np.asarray(L0_attn_x)
    in_maps = [{
        "xT": np.ascontiguousarray(x[i].T),
        "wffn": wffn, "win": win, "cmat": cmat, "vecs": vecs,
    } for i in range(x.shape[0])]
    res = run_bass_kernel_spmd(nc, in_maps, core_ids=list(range(len(in_maps))))
    return np.stack([np.ascontiguousarray(r["outT"].T) for r in res.results])


# version 13
# speedup vs baseline: 1.1658x; 1.0528x over previous
from contextlib import ExitStack

import numpy as np
import concourse.bass as bass
import concourse.mybir as mybir
from concourse.bass_isa import ReduceOp
from concourse.bass_utils import run_bass_kernel_spmd

F32 = mybir.dt.float32
BF16 = mybir.dt.bfloat16
ALU = mybir.AluOpType
ACT = mybir.ActivationFunctionType
AXX = mybir.AxisListType.X

D = 1024
NS = 8
TC = 512
T = 4096
NCH = T // TC
QB = 128.0
THR_C = (QB - 1e-5) / QB
INV_D = 1.0 / D
FFN_A = 1.001 / D
FFN_B = 1e-8
FFN_M = 1.001
SSM_A = 1.0 / D
SSM_B = 1e-5
SSM_M = 1.0


class Ctr:
    def __init__(self, labels, strict):
        self.labels = labels
        self.strict = strict
        self.cnt = {}
        self.out = {}

    def inc(self, instr, semname, sems, n=1, label=None):
        c = self.cnt.get(semname, 0) + n
        self.cnt[semname] = c
        if label is not None:
            self.out[label] = (semname, c)
        if instr is not None and n > 0:
            instr.then_inc(sems[semname], n)
        return c

    def mark(self, semname, label):
        self.out[label] = (semname, self.cnt.get(semname, 0))

    def wait(self, eng, label, sems):
        if label not in self.labels:
            if self.strict:
                raise KeyError(f"unknown label {label!r}")
            return
        semname, val = self.labels[label]
        if val > 0:
            eng.wait_ge(sems[semname], val)


DEBUG = False


def _emit(nc, labels, strict):
    C = Ctr(labels, strict)

    xT_h = nc.dram_tensor("xT", [D, T], F32, kind="ExternalInput")
    wffn_h = nc.dram_tensor("wffn", [D, D], F32, kind="ExternalInput")
    win_h = nc.dram_tensor("win", [D, D], F32, kind="ExternalInput")
    cmat_h = nc.dram_tensor("cmat", [D, D], F32, kind="ExternalInput")
    vecs_h = nc.dram_tensor("vecs", [128, 6 * NS], F32, kind="ExternalInput")
    outT_h = nc.dram_tensor("outT", [D, T], F32, kind="ExternalOutput")
    wqs_h = nc.dram_tensor("wq_scr", [D, D], BF16, kind="Internal")
    xT, wffn, win, cmat = xT_h.ap(), wffn_h.ap(), win_h.ap(), cmat_h.ap()
    outT, wq_scr = outT_h.ap(), wqs_h.ap()
    if DEBUG:
        dbg1 = nc.dram_tensor("dbg1", [D, T], F32, kind="ExternalOutput").ap()
        dbg2 = nc.dram_tensor("dbg2", [D, T], F32, kind="ExternalOutput").ap()
        dbgs = nc.dram_tensor("dbgs", [128, 7 * NCH * 4 + 16 + NS], F32,
                              kind="ExternalOutput").ap()
        dbga = nc.dram_tensor("dbga", [128, 8 * TC], F32, kind="ExternalOutput").ap()
        dbgr = nc.dram_tensor("dbgr", [128, NCH * 4], F32, kind="ExternalOutput").ap()

    with ExitStack() as st:
        def sb(name, shape, dt=F32):
            return st.enter_context(nc.sbuf_tensor(name, shape, dt))

        x_sb = [sb(f"x{s}", [128, T]) for s in range(NS)]
        w1 = [sb(f"w1_{s}", [128, D], BF16) for s in range(NS)]
        w2 = [sb(f"w2_{s}", [128, D], BF16) for s in range(NS)]
        bufX = [[sb(f"bx{p}_{s}", [128, TC], BF16) for s in range(NS)]
                for p in range(2)]
        bufH = [sb(f"bh{s}", [128, TC], BF16) for s in range(NS)]
        sqt_t = sb("sqt", [128, TC], BF16)
        sqt = [sqt_t, sqt_t]
        tn1_t = sb("tn1", [128, TC], BF16)
        tn1_b = sb("tn1b", [128, TC], BF16)
        tn1 = [tn1_t, tn1_b]

        scr1 = sb("scr1", [128, TC]); scr2 = sb("scr2", [128, TC])
        scr3 = sb("scr3", [128, TC])
        b_sb = sb("b_sb", [128, TC]); h_sb = b_sb
        s_rep = sb("s_rep", [128, TC])
        srow = sb("srow", [1, TC]); m2row = sb("m2row", [1, TC])
        mrow = m2row[0:1, :]
        srow_s = srow[0:1, :]
        vrow = scr2[0:1, :]
        qrow = scr1[0:1, :]
        mslots = sb("mslots", [128, NCH * NS], BF16)
        sqv = [sqt_t] + [bufH[i] for i in range(1, NS)]
        p2rb = sb("p2rb", [33, TC], BF16)
        murow_b = p2rb[0:1, :]
        m2row_b = p2rb[32:33, :]
        cmp_s = {k: sb(f"cs_{k}", [128, NCH * 4]) for k in ("x", "x1")}
        cmp_m2 = {k: sb(f"cm_{k}", [128, NCH * 4]) for k in ("x", "x1")}
        cmp_s["x2"] = cmp_s["x"]
        cmp_m2["x2"] = cmp_m2["x"]
        cmp_mu = cmp_m2["x"]
        wstage = sb("wst", [128, TC])
        wsb = bufH[0]
        vct = sb("vct", [128, 6 * NS])
        vc = {k: vct[:, j * NS:(j + 1) * NS] for j, k in
              enumerate(("bin", "la", "bvec", "dd", "gam", "bet"))}
        a_cols = vc["la"]
        bb_cols = vc["bin"]
        nd_colsb = sb("nd_colsb", [128, NS], BF16)
        ws_cols = sb("ws_cols", [128, NS], BF16)
        vcc1 = sb("vcc1", [128, 8]); vcc2 = sb("vcc2", [128, 4])
        dvin = vcc1[:, 0:4]; dvout = vcc1[:, 4:8]; dvt = vcc2[:, 0:4]
        gsl1 = mslots[0:1, 0:NCH * NS]
        gsl2 = gsl1
        p2r = sb("p2r", [33, D], BF16)
        nwrow = [p2r[0:1, e * 128:(e + 1) * 128] for e in range(NS)]
        ndrow = [p2r[32:33, e * 128:(e + 1) * 128] for e in range(NS)]
        ones_bf = sb("ones_bf", [128, 128], BF16)
        ones1_t = sb("ones1", [1, 128])
        ones1 = ones1_t[:]
        onescol_b = ones_bf[:, 0:1]
        carry = sb("carry", [128, NS])
        cpk = sb("cpk", [128, 16])
        (mx1, mx2, gred, _unused_wsum, wred, beta_c, inv_t, inv_c, hbeta_c,
         thr1, nthr1, thr2, nthr2) = [cpk[:, j:j + 1] for j in range(13)]
        wsum = cpk[:, 3:4]
        stg_a = cpk[0:1, 13:14]
        stg_b = cpk[0:1, 14:15]

        PB = [st.enter_context(nc.psum_tensor(f"pb{i}", [128, TC], F32))
              for i in range(8)]

        semnames = ["LX", "LW", "GQ", "RQ", "RS", "ST", "V", "S", "T", "G", "F", "RM"]
        sems = {n: st.enter_context(nc.semaphore(n)) for n in semnames}

        def col(tile, j):
            try:
                return tile[:, j:j + 1]
            except Exception:
                return tile

        def bcast(tile, j, n=TC):
            free = int(tile.shape[1])
            return bass.AP(tile, j, [[free, 128], [0, n]])

        def chunk(tile, c):
            return tile[:, c * TC:(c + 1) * TC]

        blk = nc.Block(); blk.__enter__()

        @blk.sync
        def _(sy):
            sy.dma_start(out=vct[:], in_=vecs_h.ap()[:]).then_inc(sems["RQ"], 16)
            rq = C.inc(None, "RQ", sems, 16)
            sy.wait_ge(sems["RQ"], rq)
            i = sy.nop()
            C.inc(i, "F", sems, 1, label="vecf")
            for c in range(NCH):
                for s in range(NS):
                    sy.dma_start(out=chunk(x_sb[s], c),
                                 in_=xT[s * 128:(s + 1) * 128, c * TC:(c + 1) * TC]
                                 ).then_inc(sems["LX"], 16)
                    C.inc(None, "LX", sems, 16)
                if c >= 1:
                    sy.wait_ge(sems["LX"], 16 * NS * c)
                    i = sy.nop()
                    C.inc(i, "F", sems, 1, label=f"lxf:{c - 1}")
            sy.wait_ge(sems["LX"], 16 * NS * NCH)
            i = sy.nop()
            C.inc(i, "F", sems, 1, label=f"lxf:{NCH - 1}")
            def _wload(k):
                s, hf = k // 2, k % 2
                if k == 0:
                    C.wait(sy, f"p:wr1:{2 * NS - 1}", sems)
                else:
                    C.wait(sy, f"p:wq:{k - 1}", sems)
                sy.dma_start(out=wstage[:],
                             in_=wffn[s * 128:(s + 1) * 128,
                                      hf * TC:(hf + 1) * TC]
                             ).then_inc(sems["LW"], 16)
                C.inc(None, "LW", sems, 16, label=f"p:w2:{k}")

            wl_sched = {4: [0, 1, 2, 3, 4], 5: [5, 6, 7, 8],
                        6: [9, 10, 11, 12], 7: [13, 14, 15]}
            for c in range(NCH):
                C.wait(sy, f"a1:math1:{c}", sems)
                sy.dma_start(out=cmp_m2["x"][:, 4 * c:4 * c + 4],
                             in_=mrow).then_inc(sems["RM"], 16)
                C.inc(None, "RM", sems, 16, label=f"a1:mst:{c}")
                C.wait(sy, f"a1:sr:{c}", sems)
                sy.dma_start(out=cmp_s["x"][:, 4 * c:4 * c + 4],
                             in_=srow_s).then_inc(sems["RQ"], 16)
                C.inc(None, "RQ", sems, 16, label=f"a1:sst:{c}")
                for k in wl_sched.get(c, []):
                    _wload(k)

        @blk.gpsimd
        def _(gp):
            for k in range(2 * NS):
                s, hf = k // 2, k % 2
                if k >= 1:
                    C.wait(gp, f"p:wr1:{k - 1}", sems)
                gp.dma_start(out=wstage[:],
                             in_=wffn[s * 128:(s + 1) * 128,
                                      hf * TC:(hf + 1) * TC]
                             ).then_inc(sems["GQ"], 16)
                C.inc(None, "GQ", sems, 16, label=f"p:w1:{k}")
            for s in range(NS):
                C.wait(gp, f"p:wq:{2 * s + 1}", sems)
                gp.dma_start(out=wq_scr[s * 128:(s + 1) * 128, :],
                             in_=w1[s][:]).then_inc(sems["GQ"], 16)
                C.inc(None, "GQ", sems, 16, label=f"p:wqs:{s}")
            for s in range(NS):
                gp.dma_start(out=w2[s][:], in_=win[s * 128:(s + 1) * 128, :]
                             ).then_inc(sems["GQ"], 16)
                C.inc(None, "GQ", sems, 16)
            gp.wait_ge(sems["GQ"], C.cnt["GQ"])
            i = gp.nop()
            C.inc(i, "F", sems, 1, label="lwinf")
            C.wait(gp, "a1:slots", sems)
            i = gp.tensor_reduce(gsl1[0:1, 0:1], mx1[:],
                                 mybir.AxisListType.XYZWC, ALU.max)
            C.inc(i, "G", sems, 1, label="a1:gmax")
        @blk.vector
        def _(v):
            i = v.memset(carry[:], 0.0)
            C.inc(i, "V", sems, 1)
            i = v.memset(ones_bf[:], 1.0)
            C.inc(i, "V", sems, 1, label="p:ones")
            v.memset(ones1_t[:], 1.0)
            i = v.memset(cpk[:, 15:16], 1.0)
            C.inc(i, "V", sems, 1, label="p:ones1")
            v.wait_ge(sems["V"], C.cnt["V"])

            def _wpass1(k):
                C.wait(v, f"p:w1:{k}", sems)
                if k == 0:
                    i = v.tensor_reduce(wsum[:], wstage[:], AXX, ALU.add,
                                        apply_absolute_value=True)
                    nv = C.inc(i, "V", sems, 1, label=f"p:wr1:{k}")
                    v.wait_ge(sems["V"], nv)
                else:
                    i = v.tensor_reduce(wred[:], wstage[:], AXX, ALU.add,
                                        apply_absolute_value=True)
                    nv = C.inc(i, "V", sems, 1)
                    v.wait_ge(sems["V"], nv)
                    i = v.tensor_tensor(wsum[:], wsum[:], wred[:], ALU.add)
                    nv = C.inc(i, "V", sems, 1, label=f"p:wr1:{k}")
                    v.wait_ge(sems["V"], nv)

            C.wait(v, "p:exp", sems)
            v.tensor_scalar(a_cols[:], a_cols[:], -1.0, None, ALU.mult)
            C.wait(v, "vecf", sems)
            v.tensor_tensor(bb_cols[:], vc["bin"][:], vc["bvec"][:], ALU.mult)
            i = v.tensor_scalar(nd_colsb[:], vc["dd"][:], -1.0, None, ALU.mult)
            C.inc(i, "V", sems, 1, label="p:vecd")
            for c in range(NCH + 2):
                if c >= 2:
                    cc = c - 2
                    C.wait(v, f"a1:replmu:{cc}", sems)
                    C.wait(v, f"a1:repls:{cc}", sems)
                    pm, ps_ = PB[2 + (cc % 2) * 2], PB[3 + (cc % 2) * 2]
                    for s in range(NS):
                        v.tensor_tensor(tn1[s % 2][:], chunk(x_sb[s], cc),
                                        pm[:], ALU.subtract)
                        v.tensor_tensor(tn1[s % 2][:], tn1[s % 2][:], ps_[:],
                                        ALU.mult)
                        v.tensor_scalar(tn1[s % 2][:], tn1[s % 2][:],
                                        col(vc["gam"], s), col(vc["bet"], s),
                                        ALU.mult, ALU.add)
                        k = cc * NS + s
                        i = v.tensor_reduce(mslots[:, k:k + 1], tn1[s % 2][:],
                                            AXX, ALU.max,
                                            apply_absolute_value=True)
                        C.inc(i, "V", sems, 1, label=f"a1:sl:{cc}:{s}")
                if c <= 3:
                    for k in range(4 * c, 4 * c + 4):
                        _wpass1(k)
                if 1 <= c <= NCH:
                    cc = c - 1
                    C.wait(v, f"a1:sg:{cc}", sems)
                    if cc >= 1:
                        C.wait(v, f"a1:sr:{cc - 1}", sems)
                    i = v.reciprocal(dvt, dvin)
                    nv = C.inc(i, "V", sems, 1)
                    v.wait_ge(sems["V"], nv)
                    i = v.tensor_scalar(dvout, dvt, 1.0, None, ALU.mult)
                    C.inc(i, "V", sems, 1, label=f"a1:rc:{cc}")
                if c == 4:
                    def _ss(i):
                        nv = C.inc(i, "V", sems, 1)
                        v.wait_ge(sems["V"], nv)
                        return nv
                    C.wait(v, "p:wsmm", sems)
                    _ss(v.tensor_copy(wred[0:1, 0:1], PB[7][0:1, 2:3]))
                    _ss(v.tensor_scalar(beta_c[0:1, :], wred[0:1, 0:1],
                                        1.0 / (D * D), None, ALU.mult))
                    _ss(v.tensor_scalar(stg_b, beta_c[0:1, :], 0.5, None,
                                        ALU.mult))
                    _ss(v.tensor_scalar(inv_t[0:1, :], beta_c[0:1, :], 1e-5,
                                        None, ALU.add))
                    _ss(v.reciprocal(mx2[0:1, :], inv_t[0:1, :]))
                    i = v.tensor_scalar(stg_a, mx2[0:1, :], 1.0, None, ALU.mult)
                    nv = C.inc(i, "V", sems, 1, label="p:invrow")
                    v.wait_ge(sems["V"], nv)
                    C.wait(v, "p:invmm", sems)
                    i = v.tensor_copy(cpk[:, 7:9], PB[7][:, 0:2])
                    nv = C.inc(i, "V", sems, 1, label="p:inv")
                    v.wait_ge(sems["V"], nv)
                if 5 <= c <= 8:
                    for k in range(4 * (c - 5), 4 * (c - 4)):
                        C.wait(v, f"p:w2:{k}", sems)
                        v.tensor_scalar(wstage[:], wstage[:], col(inv_c, 0),
                                        None, ALU.mult)
                        half = w1[k // 2][:, (k % 2) * TC:(k % 2 + 1) * TC]
                        v.tensor_scalar(wsb[:], wstage[:], -0.5, None, ALU.is_le)
                        v.tensor_scalar(half, wstage[:], 0.5, None, ALU.is_ge)
                        i = v.tensor_tensor(half, half, wsb[:], ALU.subtract)
                        C.inc(i, "V", sems, 1, label=f"p:wq:{k}")
                if c <= NCH - 1:
                    C.wait(v, f"a1:stat:{c}", sems)
                    if c >= 1:
                        C.wait(v, f"a1:replmu:{c - 1}", sems)
                        C.wait(v, f"a1:mst:{c - 1}", sems)
                    v.tensor_scalar(mrow, PB[0][0:1, :], INV_D, None, ALU.mult)
                    i = v.tensor_scalar(vrow, PB[1][0:1, :], FFN_A, FFN_B,
                                        ALU.mult, ALU.add)
                    nv = C.inc(i, "V", sems, 1)
                    v.wait_ge(sems["V"], nv)
                    i = v.scalar_tensor_tensor(qrow, mrow, FFN_M, mrow,
                                               ALU.mult, ALU.mult)
                    nv = C.inc(i, "V", sems, 1)
                    v.wait_ge(sems["V"], nv)
                    i = v.tensor_tensor(vrow, vrow, qrow, ALU.subtract)
                    C.inc(i, "V", sems, 1, label=f"a1:math1:{c}")
            v.wait_ge(sems["V"], C.cnt["V"])
            i = v.tensor_reduce(mx1[:], mslots[:], AXX, ALU.max)
            nv = C.inc(i, "V", sems, 1, label="a1:slots")
            v.wait_ge(sems["V"], nv)
            C.wait(v, "a1:gmax", sems)
            i = v.tensor_scalar(stg_a, gsl1[0:1, 0:1], THR_C, None, ALU.mult)
            nv = C.inc(i, "V", sems, 1)
            v.wait_ge(sems["V"], nv)
            i = v.tensor_scalar(stg_b, stg_a, -1.0, None, ALU.mult)
            nv = C.inc(i, "V", sems, 1, label="thr1v")
            v.wait_ge(sems["V"], nv)

        @blk.scalar
        def _(sc):
            def _casts(c):
                C.wait(sc, f"lxf:{c}", sems)
                for s in range(NS):
                    if c >= 2:
                        C.wait(sc, f"a1:statmm:{c - 2}:{s}", sems)
                    sc.copy(bufX[c % 2][s][:], chunk(x_sb[s], c))
                    if c >= 1:
                        C.wait(sc, f"a1:statmm:{c - 1}:{s}", sems)
                    i = sc.activation(sqv[s][:], bufX[c % 2][s][:],
                                      ACT.Square)
                    C.inc(i, "S", sems, 1, label=f"a1:cast:{c}:{s}")

            def _dance_back(cc):
                C.wait(sc, f"a1:rc:{cc}", sems)
                if cc >= 1:
                    C.wait(sc, f"a1:repls:{cc - 1}", sems)
                    C.wait(sc, f"a1:sst:{cc - 1}", sems)
                sc.dma_start(out=srow_s, in_=dvout,
                             allow_slow_non_contiguous=True
                             ).then_inc(sems["RS"], 16)
                rs = C.inc(None, "RS", sems, 16)
                sc.wait_ge(sems["RS"], rs)
                i = sc.nop()
                C.inc(i, "S", sems, 1, label=f"a1:sr:{cc}")

            _casts(0)
            C.wait(sc, "vecf", sems)
            i = sc.activation(a_cols[:], vc["la"][:], ACT.Exp)
            C.inc(i, "S", sems, 1, label="p:exp")
            for c in range(NCH):
                if c + 1 <= NCH - 1:
                    _casts(c + 1)
                if c >= 1:
                    _dance_back(c - 1)
                C.wait(sc, f"a1:math1:{c}", sems)
                i = sc.activation(vrow, vrow, ACT.Sqrt)
                ns = C.inc(i, "S", sems, 1)
                sc.wait_ge(sems["S"], ns)
                sc.dma_start(out=dvin, in_=vrow,
                             allow_slow_non_contiguous=True
                             ).then_inc(sems["RS"], 16)
                rs = C.inc(None, "RS", sems, 16)
                sc.wait_ge(sems["RS"], rs)
                i = sc.nop()
                C.inc(i, "S", sems, 1, label=f"a1:sg:{c}")
            _dance_back(NCH - 1)

        @blk.tensor
        def _(t):
            C.wait(t, "p:ones", sems)
            C.wait(t, "p:ones1", sems)
            for c in range(NCH + 2):
                if c >= 2 and c - 2 <= NCH - 1:
                    cc = c - 2
                    C.wait(t, f"a1:sr:{cc}", sems)
                    if cc >= 2:
                        C.wait(t, f"a1:sl:{cc - 2}:{NS - 1}", sems)
                    i = t.matmul(PB[3 + (cc % 2) * 2][:], ones1, srow_s,
                                 start=True, stop=True)
                    C.inc(i, "T", sems, 1, label=f"a1:repls:{cc}")
                if c <= NCH - 1:
                    for s in range(NS):
                        C.wait(t, f"a1:cast:{c}:{s}", sems)
                        if c >= 1 and s == 0:
                            C.wait(t, f"a1:math1:{c - 1}", sems)
                        t.matmul(PB[0][:], ones_bf[:], bufX[c % 2][s][:],
                                 start=(s == 0), stop=(s == NS - 1))
                        i = t.matmul(PB[1][:], ones_bf[:], sqv[s][:],
                                     start=(s == 0), stop=(s == NS - 1))
                        C.inc(i, "T", sems, 1, label=f"a1:statmm:{c}:{s}")
                    C.mark("T", f"a1:stat:{c}")
                if c == 3:
                    C.wait(t, f"p:wr1:{2 * NS - 1}", sems)
                    i = t.matmul(PB[7][0:1, 2:3], wsum, cpk[:, 15:16],
                                 start=True, stop=True)
                    C.inc(i, "T", sems, 1, label="p:wsmm")
                if c == 4:
                    C.wait(t, "p:invrow", sems)
                    i = t.matmul(PB[7][:, 0:2], ones1, cpk[0:1, 13:15],
                                 start=True, stop=True)
                    C.inc(i, "T", sems, 1, label="p:invmm")
                if c >= 1 and c - 1 <= NCH - 1:
                    cc = c - 1
                    C.wait(t, f"a1:math1:{cc}", sems)
                    if cc >= 2:
                        C.wait(t, f"a1:sl:{cc - 2}:{NS - 1}", sems)
                    i = t.matmul(PB[2 + (cc % 2) * 2][:], ones1, mrow,
                                 start=True, stop=True)
                    C.inc(i, "T", sems, 1, label=f"a1:replmu:{cc}")

        for e in ("V", "S", "T", "G"):
            C.mark(e, f"end:A1:{e}")

        def barrier(eng, sweep):
            for e in ("V", "S", "T", "G"):
                C.wait(eng, f"end:{sweep}:{e}", sems)

        blk.__exit__(None, None, None)
        blk2 = nc.Block(); blk2.__enter__()
        mrowB = scr3[0:1, :]
        vrowB = vrow
        qrowB = qrow

        @blk2.sync
        def _(sy):
            for c in range(NCH):
                if c == 0:
                    C.wait(sy, f"a1:sst:{NCH - 1}", sems)
                    C.wait(sy, f"a1:mst:{NCH - 1}", sems)
                else:
                    C.wait(sy, f"b1:repl:{c - 1}", sems)
                sy.dma_start(out=srow[:], in_=cmp_s["x"][:, 4 * c:4 * c + 4]
                             ).then_inc(sems["RQ"], 16)
                C.inc(None, "RQ", sems, 16)
                sy.dma_start(out=m2row[:], in_=cmp_m2["x"][:, 4 * c:4 * c + 4]
                             ).then_inc(sems["RQ"], 16)
                C.inc(None, "RQ", sems, 16, label=f"b1:rows:{c}")
                if c >= 1:
                    C.wait(sy, f"b1:math1:{c - 1}", sems)
                    sy.dma_start(out=cmp_mu[:, 4 * (c - 1):4 * (c - 1) + 4],
                                 in_=mrowB).then_inc(sems["RM"], 16)
                    C.inc(None, "RM", sems, 16, label=f"b1:mst:{c - 1}")
                    if DEBUG:
                        C.wait(sy, f"b1:dbgq:{c - 1}", sems)
                        sy.dma_start(out=dbgr[:, 4 * (c - 1):4 * (c - 1) + 4],
                                     in_=qrowB).then_inc(sems["RQ"], 16)
                        C.inc(None, "RQ", sems, 16, label=f"b1:dbgst:{c - 1}")
            C.wait(sy, f"b1:math1:{NCH - 1}", sems)
            sy.dma_start(out=cmp_mu[:, 4 * (NCH - 1):4 * NCH],
                         in_=mrowB).then_inc(sems["RM"], 16)
            C.inc(None, "RM", sems, 16, label=f"b1:mst:{NCH - 1}")
            if DEBUG:
                C.wait(sy, f"b1:dbgq:{NCH - 1}", sems)
                sy.dma_start(out=dbgr[:, 4 * (NCH - 1):4 * NCH],
                             in_=qrowB).then_inc(sems["RQ"], 16)
                C.inc(None, "RQ", sems, 16, label=f"b1:dbgst:{NCH - 1}")

        @blk2.tensor
        def _(t):
            barrier(t, "A1")
            C.wait(t, "thr1v", sems)
            i = t.matmul(PB[6][:, 0:2], ones1, cpk[0:1, 13:15],
                         start=True, stop=True)
            C.inc(i, "T", sems, 1, label="thr1mm")
            for c in range(NCH + 1):
                if c <= NCH - 1:
                    C.wait(t, f"b1:rows:{c}", sems)
                    if c >= 2:
                        C.wait(t, f"b1:xq:{c - 2}", sems)
                    r1, r2 = PB[2 + (c % 2) * 2], PB[3 + (c % 2) * 2]
                    t.matmul(r1[:], ones1, srow[:], start=True, stop=True)
                    i = t.matmul(r2[:], ones1, m2row[:], start=True, stop=True)
                    C.inc(i, "T", sems, 1, label=f"b1:repl:{c}")
                if c >= 1:
                    cc = c - 1
                    C.wait(t, f"b1:xq:{cc}", sems)
                    i = None
                    for e in range(NS):
                        k = cc * NS + e
                        if k >= 2:
                            C.wait(t, f"b1:evac:{(k - 2) // NS}:{(k - 2) % NS}",
                                   sems)
                        for s in range(NS):
                            i = t.matmul(PB[6 + e % 2][:],
                                         w1[s][:, e * 128:(e + 1) * 128],
                                         bufX[cc % 2][s][:], start=(s == 0),
                                         stop=(s == NS - 1))
                        C.inc(i, "T", sems, 1, label=f"b1:mm:{cc}:{e}")
                    for s in range(NS):
                        C.wait(t, f"b1:cast1:{cc}:{s}", sems)
                        if cc >= 1 and s == 0:
                            C.wait(t, f"b1:math1:{cc - 1}", sems)
                        t.matmul(PB[0][:], ones_bf[:], bufX[cc % 2][s][:],
                                 start=(s == 0), stop=(s == NS - 1))
                        i = t.matmul(PB[1][:], ones_bf[:], sqt[s % 2][:],
                                     start=(s == 0), stop=(s == NS - 1))
                        C.inc(i, "T", sems, 1, label=f"b1:statmm:{cc}:{s}")
                    C.mark("T", f"b1:stat:{cc}")
            C.mark("T", "b1:mmall")

        @blk2.vector
        def _(v):
            barrier(v, "A1")
            C.wait(v, "thr1mm", sems)
            i = v.tensor_copy(cpk[:, 9:11], PB[6][:, 0:2])
            nv = C.inc(i, "V", sems, 1)
            v.wait_ge(sems["V"], nv)
            for c in range(NCH + 2):
                if c <= NCH - 1 or c == NCH:
                    cc = c - 1
                    if c <= NCH - 1:
                        C.wait(v, f"b1:repl:{c}", sems)
                        if c >= 2:
                            C.wait(v, f"b1:statmm:{c - 2}:{NS - 1}", sems)
                            if DEBUG and c - 2 == 4:
                                C.wait(v, f"b1:dbgbx:4", sems)
                    r1, r2 = PB[2 + (c % 2) * 2], PB[3 + (c % 2) * 2]
                    i = None
                    for s in range(NS):
                        if c <= NCH - 1:
                            v.tensor_tensor(b_sb[:], chunk(x_sb[s], c), r2[:],
                                            ALU.subtract)
                            v.tensor_tensor(s_rep[:], b_sb[:], r1[:], ALU.mult)
                            v.tensor_scalar(b_sb[:], s_rep[:],
                                            col(vc["gam"], s),
                                            col(vc["bet"], s),
                                            ALU.mult, ALU.add)
                            i = v.tensor_scalar(bufX[c % 2][s][:], b_sb[:],
                                                col(thr1, 0), col(nthr1, 0),
                                                ALU.min, ALU.max)
                            if s == NS - 1:
                                C.inc(i, "V", sems, 1, label=f"b1:xq:{c}")
                        if 1 <= c:
                            C.wait(v, f"b1:mm:{cc}:{s}", sems)
                            i = v.scalar_tensor_tensor(chunk(x_sb[s], cc),
                                                       PB[6 + s % 2][:],
                                                       col(hbeta_c, 0),
                                                       chunk(x_sb[s], cc),
                                                       ALU.mult, ALU.add)
                            C.inc(i, "V", sems, 1, label=f"b1:evac:{cc}:{s}")
                if 2 <= c <= NCH + 1:
                    cc = c - 2
                    C.wait(v, f"b1:sg:{cc}", sems)
                    if cc >= 1:
                        C.wait(v, f"b1:sr:{cc - 1}", sems)
                    i = v.reciprocal(dvt, dvin)
                    nv = C.inc(i, "V", sems, 1)
                    v.wait_ge(sems["V"], nv)
                    i = v.tensor_scalar(dvout, dvt, 1.0, None, ALU.mult)
                    C.inc(i, "V", sems, 1, label=f"b1:rc:{cc}")
                if 1 <= c <= NCH:
                    cc = c - 1
                    C.wait(v, f"b1:stat:{cc}", sems)
                    if cc >= 1:
                        C.wait(v, f"b1:mst:{cc - 1}", sems)
                    v.tensor_scalar(mrowB, PB[0][0:1, :], INV_D, None, ALU.mult)
                    i = v.tensor_scalar(vrowB, PB[1][0:1, :], SSM_A, SSM_B,
                                        ALU.mult, ALU.add)
                    nv = C.inc(i, "V", sems, 1)
                    v.wait_ge(sems["V"], nv)
                    i = v.scalar_tensor_tensor(qrowB, mrowB, SSM_M, mrowB,
                                               ALU.mult, ALU.mult)
                    nv = C.inc(i, "V", sems, 1)
                    v.wait_ge(sems["V"], nv)
                    i = v.tensor_tensor(vrowB, vrowB, qrowB, ALU.subtract)
                    C.inc(i, "V", sems, 1, label=f"b1:math1:{cc}")
                    if DEBUG:
                        if cc >= 1:
                            C.wait(v, f"b1:dbgst:{cc - 1}", sems)
                        i = v.tensor_copy(qrowB, PB[0][0:1, :])
                        C.inc(i, "V", sems, 1, label=f"b1:dbgq:{cc}")
            C.wait(v, f"b1:sr:{NCH - 1}", sems)
            C.wait(v, f"b1:mst:{NCH - 1}", sems)
            i = v.tensor_tensor(cmp_m2["x1"][:], cmp_mu[:], cmp_s["x1"][:],
                                ALU.mult)
            nv = C.inc(i, "V", sems, 1, label="b1:m2c")
            v.wait_ge(sems["V"], nv)

        @blk2.scalar
        def _(sc):
            barrier(sc, "A1")
            for c in range(1, NCH + 2):
                if c - 1 <= NCH - 1:
                    cc = c - 1
                    for s in range(NS):
                        k = cc * NS + s
                        C.wait(sc, f"b1:evac:{cc}:{s}", sems)
                        C.wait(sc, f"b1:mm:{cc}:{NS - 1}", sems)
                        sc.copy(bufX[cc % 2][s][:], chunk(x_sb[s], cc))
                        if k >= 1:
                            C.wait(sc,
                                   f"b1:statmm:{(k - 1) // NS}:{(k - 1) % NS}",
                                   sems)
                        i = sc.activation(sqt[s % 2][:], bufX[cc % 2][s][:],
                                          ACT.Square)
                        C.inc(i, "S", sems, 1, label=f"b1:cast1:{cc}:{s}")
                if c >= 2:
                    cc = c - 2
                    C.wait(sc, f"b1:rc:{cc}", sems)
                    sc.dma_start(out=cmp_s["x1"][:, 4 * cc:4 * cc + 4],
                                 in_=dvout, allow_slow_non_contiguous=True
                                 ).then_inc(sems["RS"], 16)
                    rs = C.inc(None, "RS", sems, 16)
                    sc.wait_ge(sems["RS"], rs)
                    i = sc.nop()
                    C.inc(i, "S", sems, 1, label=f"b1:sr:{cc}")
                if c - 1 <= NCH - 1:
                    cc = c - 1
                    C.wait(sc, f"b1:math1:{cc}", sems)
                    i = sc.activation(vrowB, vrowB, ACT.Sqrt)
                    ns = C.inc(i, "S", sems, 1)
                    sc.wait_ge(sems["S"], ns)
                    sc.dma_start(out=dvin, in_=vrowB,
                                 allow_slow_non_contiguous=True
                                 ).then_inc(sems["RS"], 16)
                    rs = C.inc(None, "RS", sems, 16)
                    sc.wait_ge(sems["RS"], rs)
                    i = sc.nop()
                    C.inc(i, "S", sems, 1, label=f"b1:sg:{cc}")

        @blk2.gpsimd
        def _(gp):
            barrier(gp, "A1")
            if DEBUG:
                C.wait(gp, "b1:stat:4", sems)
                for s in range(NS):
                    gp.dma_start(out=dbga[:, s * TC:(s + 1) * TC],
                                 in_=bufX[0][s][:]).then_inc(sems["GQ"], 16)
                    C.inc(None, "GQ", sems, 16,
                          label=f"b1:dbgbx:4" if s == NS - 1 else None)
            C.wait(gp, "b1:mmall", sems)
            for s in range(NS):
                gp.dma_start(out=w1[s][:], in_=cmat[s * 128:(s + 1) * 128, :]
                             ).then_inc(sems["GQ"], 16)
                C.inc(None, "GQ", sems, 16)
            gp.wait_ge(sems["GQ"], C.cnt["GQ"])
            i = gp.nop()
            C.inc(i, "F", sems, 1, label="lcmatf")

        for e in ("V", "S", "T", "G"):
            C.mark(e, f"end:B1:{e}")

        blk2.__exit__(None, None, None)
        blk3 = nc.Block(); blk3.__enter__()
        mrowS = m2row[0:1, :]
        vrowS = scr2[0:1, :]
        qrowS = wstage[0:1, :]

        @blk3.sync
        def _(sy):
            if DEBUG:
                for e in ("V", "S", "T", "G"):
                    C.wait(sy, f"end:B1:{e}", sems)
                for s in range(NS):
                    sy.dma_start(out=dbg1[s * 128:(s + 1) * 128, :],
                                 in_=x_sb[s][:]).then_inc(sems["RQ"], 16)
                    C.inc(None, "RQ", sems, 16)
            for c in range(NCH):
                if c == 0:
                    C.wait(sy, f"b1:sr:{NCH - 1}", sems)
                    C.wait(sy, "b1:m2c", sems)
                else:
                    C.wait(sy, f"ssm:repl:{c - 1}", sems)
                if c >= 3:
                    C.wait(sy, f"ssm:rep3:{c - 3}", sems)
                sy.dma_start(out=srow[:], in_=cmp_s["x1"][:, 4 * c:4 * c + 4]
                             ).then_inc(sems["RQ"], 16)
                C.inc(None, "RQ", sems, 16, label=f"ssm:rows:{c}")
                if c >= 1:
                    C.wait(sy, f"ssm:math1:{c - 1}", sems)
                    sy.dma_start(out=cmp_m2["x2"][:, 4 * (c - 1):4 * (c - 1) + 4],
                                 in_=mrowS).then_inc(sems["RM"], 16)
                    C.inc(None, "RM", sems, 16, label=f"ssm:mst:{c - 1}")
            C.wait(sy, f"ssm:math1:{NCH - 1}", sems)
            sy.dma_start(out=cmp_m2["x2"][:, 4 * (NCH - 1):4 * NCH],
                         in_=mrowS).then_inc(sems["RM"], 16)
            C.inc(None, "RM", sems, 16, label=f"ssm:mst:{NCH - 1}")

        @blk3.gpsimd
        def _(gp):
            barrier(gp, "B1")
            for c in range(NCH + 2):
                if c <= NCH - 1:
                    if c > 0:
                        C.wait(gp, f"ssm:umm:{c - 1}:{NS - 1}", sems)
                        C.wait(gp, f"ssm:ymm:{c - 1}:{NS - 1}", sems)
                    gp.dma_start(out=murow_b[:], in_=cmp_mu[:, 4 * c:4 * c + 4]
                                 ).then_inc(sems["GQ"], 16)
                    C.inc(None, "GQ", sems, 16)
                    gp.dma_start(out=m2row_b[:],
                                 in_=cmp_m2["x1"][:, 4 * c:4 * c + 4]
                                 ).then_inc(sems["GQ"], 16)
                    C.inc(None, "GQ", sems, 16, label=f"ssm:browz:{c}")
                if c >= 2:
                    cc = c - 2
                    for s in range(NS):
                        k = cc * NS + s
                        C.wait(gp, f"ssm:ab:{cc}:{s}", sems)
                        if cc == 0 and s == 0:
                            C.wait(gp, "thr1v", sems)
                        i = gp.tensor_reduce(gsl2[0:1, k:k + 1], tn1[s % 2][:],
                                             mybir.AxisListType.XYZWC, ALU.max)
                        C.inc(i, "G", sems, 1, label=f"ssm:red:{cc}:{s}")
                if c <= NCH - 1:
                    for f in range(NS):
                        C.wait(gp, f"ssm:xy:{c}:{f}", sems)
                        i = gp.tensor_tensor(chunk(x_sb[f], c),
                                             chunk(x_sb[f], c),
                                             (scr1 if f % 2 == 0 else scr3)[:],
                                             ALU.add)
                        C.inc(i, "G", sems, 1, label=f"ssm:xt4:{c}:{f}")
            C.wait(gp, "ssm:ulast", sems)
            gp.wait_ge(sems["GQ"], C.cnt["GQ"])
            for s in range(NS):
                gp.dma_start(out=w2[s][:], in_=wq_scr[s * 128:(s + 1) * 128, :]
                             ).then_inc(sems["GQ"], 16)
                C.inc(None, "GQ", sems, 16)
            gp.wait_ge(sems["GQ"], C.cnt["GQ"])
            i = gp.nop()
            C.inc(i, "F", sems, 1, label="lwq2f")

        @blk3.tensor
        def _(t):
            barrier(t, "B1")
            i = None
            for e in range(NS):
                if e == 0:
                    C.wait(t, "lwinf", sems)
                for s in range(NS):
                    i = t.matmul(PB[2][:, e:e + 1],
                                 w2[s][:, e * 128:(e + 1) * 128],
                                 onescol_b[:], start=(s == 0),
                                 stop=(s == NS - 1))
                C.inc(i, "T", sems, 1, label=f"ssm:ws:{e}")
            for c in range(NCH):
                C.wait(t, f"ssm:rows:{c}", sems)
                if c == 0:
                    C.wait(t, f"ssm:wsc:{NS - 1}", sems)
                if c >= 2:
                    C.wait(t, f"ssm:srepdone:{c - 2}", sems)
                i = t.matmul(PB[2 + c % 2][:], ones1, srow[:],
                             start=True, stop=True)
                C.inc(i, "T", sems, 1, label=f"ssm:repl:{c}")
                for s in range(NS):
                    C.wait(t, f"ssm:cast1:{c}:{s}", sems)
                C.wait(t, f"ssm:browz:{c}", sems)
                if c == 0:
                    C.wait(t, "p:vecws", sems)
                for e in range(NS):
                    k = c * NS + e
                    if e == 4 and c >= 2:
                        cc = c - 2
                        C.wait(t, f"ssm:sr:{cc}", sems)
                        C.wait(t, f"ssm:math1:{c - 1}", sems)
                        t.matmul(PB[0][:], ones1, srow[:], start=True,
                                 stop=True)
                        i = t.matmul(PB[1][:], ones1, mrowS, start=True,
                                     stop=True)
                        C.inc(i, "T", sems, 1, label=f"ssm:rep3:{cc}")
                    if k >= 2:
                        C.wait(t, f"ssm:b:{(k - 2) // NS}:{(k - 2) % NS}", sems)
                    for s in range(NS):
                        t.matmul(PB[4 + e % 2][:],
                                 w2[s][:, e * 128:(e + 1) * 128],
                                 bufX[c % 2][s][:], start=(s == 0), stop=False)
                    i = t.matmul(PB[4 + e % 2][:], nwrow[e][:], murow_b[:],
                                 start=False, stop=True)
                    C.inc(i, "T", sems, 1, label=f"ssm:umm:{c}:{e}")
                if c == NCH - 1:
                    C.mark("T", "ssm:ulast")
                for f in range(NS):
                    k = c * NS + f
                    C.wait(t, f"ssm:hhat:{c}:{NS - 1}", sems)
                    if k >= 2:
                        C.wait(t, f"ssm:xy:{(k - 2) // NS}:{(k - 2) % NS}",
                               sems)
                    if c == 0 and f == 0:
                        C.wait(t, "lcmatf", sems)
                    for e in range(NS):
                        t.matmul(PB[6 + f % 2][:],
                                 w1[e][:, f * 128:(f + 1) * 128],
                                 bufH[e][:], start=(e == 0), stop=False)
                    i = t.matmul(PB[6 + f % 2][:], ndrow[f][:], m2row_b[:],
                                 start=False, stop=True)
                    C.inc(i, "T", sems, 1, label=f"ssm:ymm:{c}:{f}")
                for s in range(NS):
                    C.wait(t, f"ssm:cast2:{c}:{s}", sems)
                    if c >= 2 and s == 0:
                        C.wait(t, f"ssm:xn2:{c - 2}:{NS - 1}", sems)
                    t.matmul(PB[0][:], ones_bf[:], bufX[c % 2][s][:],
                             start=(s == 0), stop=(s == NS - 1))
                    i = t.matmul(PB[1][:], ones_bf[:], sqt[s % 2][:],
                                 start=(s == 0), stop=(s == NS - 1))
                    C.inc(i, "T", sems, 1, label=f"ssm:statmm:{c}:{s}")
                C.mark("T", f"ssm:stat:{c}")
            for cc in (NCH - 2, NCH - 1):
                C.wait(t, f"ssm:sr:{cc}", sems)
                C.wait(t, f"ssm:math1:{NCH - 1}", sems)
                if cc == NCH - 2:
                    C.wait(t, f"ssm:xn2:{NCH - 3}:{NS - 1}", sems)
                else:
                    C.wait(t, f"ssm:xn2:{NCH - 2}:{NS - 1}", sems)
                t.matmul(PB[0][:], ones1, srow[:], start=True, stop=True)
                i = t.matmul(PB[1][:], ones1, mrowS, start=True, stop=True)
                C.inc(i, "T", sems, 1, label=f"ssm:rep3:{cc}")

        @blk3.vector
        def _(v):
            barrier(v, "B1")
            for e in range(NS):
                C.wait(v, f"ssm:ws:{e}", sems)
                i = v.tensor_scalar(col(ws_cols, e), PB[2][:, e:e + 1], -1.0,
                                    None, ALU.mult)
                C.inc(i, "V", sems, 1, label=f"ssm:wsc:{e}")
            for c in range(NCH + 2):
                if c <= NCH - 1:
                    C.wait(v, f"ssm:repl:{c}", sems)
                    i = v.tensor_copy(s_rep[:], PB[2 + c % 2][:])
                    nv = C.inc(i, "V", sems, 1, label=f"ssm:srepdone:{c}")
                    v.wait_ge(sems["V"], nv)
                if 2 <= c <= NCH + 1:
                    cc = c - 2
                    C.wait(v, f"ssm:sg:{cc}", sems)
                    if cc >= 1:
                        C.wait(v, f"ssm:sr:{cc - 1}", sems)
                    i = v.reciprocal(dvt, dvin)
                    nv = C.inc(i, "V", sems, 1)
                    v.wait_ge(sems["V"], nv)
                    i = v.tensor_scalar(dvout, dvt, 1.0, None, ALU.mult)
                    C.inc(i, "V", sems, 1, label=f"ssm:rc:{cc}")
                if 1 <= c <= NCH:
                    cc = c - 1
                    C.wait(v, f"ssm:stat:{cc}", sems)
                    if cc >= 1:
                        C.wait(v, f"ssm:mst:{cc - 1}", sems)
                    v.tensor_scalar(mrowS, PB[0][0:1, :], INV_D, None, ALU.mult)
                    i = v.tensor_scalar(vrowS, PB[1][0:1, :], FFN_A, FFN_B,
                                        ALU.mult, ALU.add)
                    nv = C.inc(i, "V", sems, 1)
                    v.wait_ge(sems["V"], nv)
                    i = v.scalar_tensor_tensor(qrowS, mrowS, FFN_M, mrowS,
                                               ALU.mult, ALU.mult)
                    nv = C.inc(i, "V", sems, 1)
                    v.wait_ge(sems["V"], nv)
                    i = v.tensor_tensor(vrowS, vrowS, qrowS, ALU.subtract)
                    C.inc(i, "V", sems, 1, label=f"ssm:math1:{cc}")
                if c <= NCH - 1:
                    for e in range(NS):
                        C.wait(v, f"ssm:umm:{c}:{e}", sems)
                        if e >= 1:
                            C.wait(v, f"ssm:hcast:{c}:{e - 1}", sems)
                        elif c >= 1:
                            C.wait(v, f"ssm:hcast:{c - 1}:{NS - 1}", sems)
                        v.scalar_tensor_tensor(b_sb[:], PB[4 + e % 2][:],
                                               col(vc["bvec"], e), s_rep[:],
                                               ALU.mult, ALU.mult)
                        i = v.tensor_scalar(b_sb[:], b_sb[:], col(bb_cols, e),
                                            None, ALU.add)
                        C.inc(i, "V", sems, 1, label=f"ssm:b:{c}:{e}")
                        a_bc = bass.AP(vct, NS + e, [[6 * NS, 128], [0, TC]])
                        v.tensor_tensor_scan(h_sb[:], a_bc, b_sb[:],
                                             col(carry, e), ALU.mult, ALU.add)
                        i = v.tensor_copy(col(carry, e), h_sb[:, TC - 1:TC])
                        nv = C.inc(i, "V", sems, 1, label=f"ssm:scan:{c}:{e}")
                        v.wait_ge(sems["V"], nv)
                if 2 <= c <= NCH + 1:
                    cc = c - 2
                    C.wait(v, f"ssm:rep3:{cc}", sems)
                    for s in range(NS):
                        k = cc * NS + s
                        if k >= 2:
                            C.wait(v, f"ssm:red:{(k - 2) // NS}:{(k - 2) % NS}",
                                   sems)
                        C.wait(v, f"ssm:xt4:{cc}:{s}", sems)
                        v.tensor_tensor(tn1[s % 2][:], chunk(x_sb[s], cc),
                                        PB[1][:], ALU.subtract)
                        v.tensor_tensor(tn1[s % 2][:], tn1[s % 2][:], PB[0][:],
                                        ALU.mult)
                        i = v.tensor_scalar(tn1[s % 2][:], tn1[s % 2][:],
                                            col(vc["gam"], s), col(vc["bet"], s),
                                            ALU.mult, ALU.add)
                        C.inc(i, "V", sems, 1, label=f"ssm:xn2:{cc}:{s}")
                if c <= NCH - 1:
                    for f in range(NS):
                        C.wait(v, f"ssm:ymm:{c}:{f}", sems)
                        if f >= 2:
                            C.wait(v, f"ssm:xt4:{c}:{f - 2}", sems)
                        elif c >= 1:
                            C.wait(v, f"ssm:xt4:{c - 1}:{f + NS - 2}", sems)
                        t4buf = scr1 if f % 2 == 0 else scr3
                        v.scalar_tensor_tensor(t4buf[:], chunk(x_sb[f], c),
                                               col(vc["dd"], f), s_rep[:],
                                               ALU.mult, ALU.mult)
                        i = v.tensor_tensor(chunk(x_sb[f], c),
                                            chunk(x_sb[f], c),
                                            PB[6 + f % 2][:], ALU.add)
                        C.inc(i, "V", sems, 1, label=f"ssm:xy:{c}:{f}")
            C.wait(v, f"ssm:red:{NCH - 1}:{NS - 1}", sems)
            i = v.tensor_reduce(mx1[0:1, :], gsl1[0:1, 0:64], AXX, ALU.max)
            nv = C.inc(i, "V", sems, 1)
            v.wait_ge(sems["V"], nv)
            i = v.tensor_scalar(stg_a, mx1[0:1, :], THR_C, None, ALU.mult)
            nv = C.inc(i, "V", sems, 1)
            v.wait_ge(sems["V"], nv)
            i = v.tensor_scalar(stg_b, stg_a, -1.0, None, ALU.mult)
            nv = C.inc(i, "V", sems, 1, label="thr2v")
            v.wait_ge(sems["V"], nv)

        @blk3.scalar
        def _(sc):
            barrier(sc, "B1")
            C.wait(sc, "p:vecd", sems)
            for e in range(NS):
                C.wait(sc, f"ssm:wsc:{e}", sems)
                sc.dma_start(out=nwrow[e][:], in_=col(ws_cols, e),
                             allow_slow_non_contiguous=True
                             ).then_inc(sems["RS"], 16)
                C.inc(None, "RS", sems, 16)
                sc.dma_start(out=ndrow[e][:], in_=col(nd_colsb, e),
                             allow_slow_non_contiguous=True
                             ).then_inc(sems["RS"], 16)
                C.inc(None, "RS", sems, 16, label=f"ssm:rowdma:{e}")
            sc.wait_ge(sems["RS"], C.cnt["RS"])
            i = sc.nop()
            C.inc(i, "S", sems, 1, label="p:vecws")
            for s in range(NS):
                i = sc.copy(bufX[0][s][:], chunk(x_sb[s], 0))
                C.inc(i, "S", sems, 1, label=f"ssm:cast1:0:{s}")
            for c in range(NCH + 2):
                if c + 1 <= NCH - 1:
                    for s in range(NS):
                        if c + 1 >= 2:
                            C.wait(sc, f"ssm:statmm:{c - 1}:{s}", sems)
                        i = sc.copy(bufX[(c + 1) % 2][s][:],
                                    chunk(x_sb[s], c + 1))
                        C.inc(i, "S", sems, 1, label=f"ssm:cast1:{c + 1}:{s}")
                if 2 <= c <= NCH + 1:
                    cc = c - 2
                    C.wait(sc, f"ssm:rc:{cc}", sems)
                    if cc + 2 <= NCH - 1:
                        C.wait(sc, f"ssm:repl:{cc + 2}", sems)
                    elif cc >= 1:
                        C.wait(sc, f"ssm:rep3:{cc - 1}", sems)
                    sc.dma_start(out=srow[:], in_=dvout,
                                 allow_slow_non_contiguous=True
                                 ).then_inc(sems["RS"], 16)
                    rs = C.inc(None, "RS", sems, 16)
                    sc.wait_ge(sems["RS"], rs)
                    sc.dma_start(out=cmp_s["x2"][:, 4 * cc:4 * cc + 4],
                                 in_=dvout, allow_slow_non_contiguous=True
                                 ).then_inc(sems["RS"], 16)
                    rs = C.inc(None, "RS", sems, 16)
                    sc.wait_ge(sems["RS"], rs)
                    i = sc.nop()
                    C.inc(i, "S", sems, 1, label=f"ssm:sr:{cc}")
                if 1 <= c <= NCH:
                    cc = c - 1
                    C.wait(sc, f"ssm:math1:{cc}", sems)
                    i = sc.activation(vrowS, vrowS, ACT.Sqrt)
                    ns = C.inc(i, "S", sems, 1)
                    sc.wait_ge(sems["S"], ns)
                    sc.dma_start(out=dvin, in_=vrowS,
                                 allow_slow_non_contiguous=True
                                 ).then_inc(sems["RS"], 16)
                    rs = C.inc(None, "RS", sems, 16)
                    sc.wait_ge(sems["RS"], rs)
                    i = sc.nop()
                    C.inc(i, "S", sems, 1, label=f"ssm:sg:{cc}")
                if c <= NCH - 1:
                    for e in range(NS):
                        C.wait(sc, f"ssm:scan:{c}:{e}", sems)
                        if c >= 1 and e == 0:
                            C.wait(sc, f"ssm:ymm:{c - 1}:{NS - 1}", sems)
                        i = sc.copy(bufH[e][:], h_sb[:])
                        C.inc(i, "S", sems, 1, label=f"ssm:hhat:{c}:{e}")
                        C.mark("S", f"ssm:hcast:{c}:{e}")
                if 2 <= c <= NCH + 1:
                    cc = c - 2
                    for s in range(NS):
                        C.wait(sc, f"ssm:xn2:{cc}:{s}", sems)
                        i = sc.activation(tn1[s % 2][:], tn1[s % 2][:], ACT.Abs)
                        C.inc(i, "S", sems, 1, label=f"ssm:ab:{cc}:{s}")
                if c <= NCH - 1:
                    for s in range(NS):
                        k = c * NS + s
                        C.wait(sc, f"ssm:xt4:{c}:{s}", sems)
                        C.wait(sc, f"ssm:umm:{c}:{NS - 1}", sems)
                        sc.copy(bufX[c % 2][s][:], chunk(x_sb[s], c))
                        if k >= 1:
                            C.wait(sc,
                                   f"ssm:statmm:{(k - 1) // NS}:{(k - 1) % NS}",
                                   sems)
                        i = sc.activation(sqt[s % 2][:], bufX[c % 2][s][:],
                                          ACT.Square)
                        C.inc(i, "S", sems, 1, label=f"ssm:cast2:{c}:{s}")

        for e in ("V", "S", "T", "G"):
            C.mark(e, f"end:SSM:{e}")

        blk3.__exit__(None, None, None)
        blk4 = nc.Block(); blk4.__enter__()

        @blk4.sync
        def _(sy):
            if DEBUG:
                for e in ("V", "S", "T", "G"):
                    C.wait(sy, f"end:SSM:{e}", sems)
                for s in range(NS):
                    sy.dma_start(out=dbg2[s * 128:(s + 1) * 128, :],
                                 in_=x_sb[s][:]).then_inc(sems["RQ"], 16)
                    C.inc(None, "RQ", sems, 16)
            def _b2_store(c):
                for e in range(NS):
                    C.wait(sy, f"b2:evac:{c}:{e}", sems)
                    sy.dma_start(out=outT[e * 128:(e + 1) * 128,
                                          c * TC:(c + 1) * TC],
                                 in_=chunk(x_sb[e], c)).then_inc(sems["ST"], 16)
                    C.inc(None, "ST", sems, 16, label=f"st:{c}:{e}")

            for c in range(NCH):
                if c == 0:
                    C.wait(sy, f"ssm:sr:{NCH - 1}", sems)
                    C.wait(sy, f"ssm:mst:{NCH - 1}", sems)
                    C.wait(sy, f"ssm:rep3:{NCH - 1}", sems)
                else:
                    C.wait(sy, f"b2:repl:{c - 1}", sems)
                sy.dma_start(out=srow[:], in_=cmp_s["x2"][:, 4 * c:4 * c + 4]
                             ).then_inc(sems["RQ"], 16)
                C.inc(None, "RQ", sems, 16)
                sy.dma_start(out=m2row[:], in_=cmp_m2["x2"][:, 4 * c:4 * c + 4]
                             ).then_inc(sems["RQ"], 16)
                C.inc(None, "RQ", sems, 16, label=f"b2:rows:{c}")
                if c >= 1:
                    _b2_store(c - 1)
            _b2_store(NCH - 1)

        def _b2_big(t, c):
            C.wait(t, f"b2:xq:{c}", sems)
            i = None
            for e in range(NS):
                k = c * NS + e
                if k >= 2:
                    C.wait(t, f"b2:evac:{(k - 2) // NS}:{(k - 2) % NS}", sems)
                for s in range(NS):
                    i = t.matmul(PB[6 + e % 2][:], w2[s][:, e * 128:(e + 1) * 128],
                                 bufX[c % 2][s][:], start=(s == 0),
                                 stop=(s == NS - 1))
                C.inc(i, "T", sems, 1, label=f"b2:mm:{c}:{e}")

        @blk4.tensor
        def _(t):
            barrier(t, "SSM")
            C.wait(t, "thr2v", sems)
            i = t.matmul(PB[6][:, 0:2], ones1, cpk[0:1, 13:15],
                         start=True, stop=True)
            C.inc(i, "T", sems, 1, label="thr2mm")
            C.wait(t, "lwq2f", sems)
            for c in range(NCH):
                C.wait(t, f"b2:rows:{c}", sems)
                if c >= 2:
                    C.wait(t, f"b2:xq:{c - 2}", sems)
                r1, r2 = PB[2 + (c % 2) * 2], PB[3 + (c % 2) * 2]
                t.matmul(r1[:], ones1, srow[:], start=True, stop=True)
                i = t.matmul(r2[:], ones1, m2row[:], start=True, stop=True)
                C.inc(i, "T", sems, 1, label=f"b2:repl:{c}")
                C.mark("T", f"b2:replq:{c}")
                if c >= 1:
                    _b2_big(t, c - 1)
            _b2_big(t, NCH - 1)

        @blk4.vector
        def _(v):
            barrier(v, "SSM")
            C.wait(v, "thr2mm", sems)
            i = v.tensor_copy(cpk[:, 11:13], PB[6][:, 0:2])
            nv = C.inc(i, "V", sems, 1)
            v.wait_ge(sems["V"], nv)
            for c in range(NCH + 1):
                cc = c - 1
                if c <= NCH - 1:
                    C.wait(v, f"b2:repl:{c}", sems)
                    if c >= 2:
                        C.wait(v, f"b2:mm:{c - 2}:{NS - 1}", sems)
                r1, r2 = PB[2 + (c % 2) * 2], PB[3 + (c % 2) * 2]
                i = None
                for s in range(NS):
                    if c <= NCH - 1:
                        v.tensor_tensor(b_sb[:], chunk(x_sb[s], c), r2[:],
                                        ALU.subtract)
                        v.tensor_tensor(s_rep[:], b_sb[:], r1[:], ALU.mult)
                        v.tensor_scalar(b_sb[:], s_rep[:], col(vc["gam"], s),
                                        col(vc["bet"], s), ALU.mult, ALU.add)
                        i = v.tensor_scalar(bufX[c % 2][s][:], b_sb[:],
                                            col(thr2, 0), col(nthr2, 0),
                                            ALU.min, ALU.max)
                        if s == NS - 1:
                            C.inc(i, "V", sems, 1, label=f"b2:xq:{c}")
                    if c >= 1:
                        C.wait(v, f"b2:mm:{cc}:{s}", sems)
                        i = v.scalar_tensor_tensor(chunk(x_sb[s], cc),
                                                   PB[6 + s % 2][:],
                                                   col(hbeta_c, 0),
                                                   chunk(x_sb[s], cc),
                                                   ALU.mult, ALU.add)
                        C.inc(i, "V", sems, 1, label=f"b2:evac:{cc}:{s}")

        @blk4.gpsimd
        def _(gp):
            barrier(gp, "SSM")
            gp.wait_ge(sems["ST"], 16 * NCH * NS)
            if DEBUG:
                w = NCH * 4
                for j, tt in enumerate([cmp_s["x"], cmp_m2["x"], cmp_s["x1"],
                                        cmp_m2["x1"], cmp_mu, cmp_s["x2"],
                                        cmp_m2["x2"]]):
                    gp.dma_start(out=dbgs[:, j * w:(j + 1) * w], in_=tt[:]
                                 ).then_inc(sems["GQ"], 16)
                    C.inc(None, "GQ", sems, 16)
                gp.dma_start(out=dbgs[:, 7 * w:7 * w + 16], in_=cpk[:]
                             ).then_inc(sems["GQ"], 16)
                C.inc(None, "GQ", sems, 16)
                gp.dma_start(out=dbgs[:, 7 * w + 16:7 * w + 16 + NS],
                             in_=carry[:]).then_inc(sems["GQ"], 16)
                C.inc(None, "GQ", sems, 16)
                gp.wait_ge(sems["GQ"], C.cnt["GQ"])

        blk4.__exit__(None, None, None)

    return C.out


def _build():
    nc1 = bass.Bass("TRN2", target_bir_lowering=False, detect_race_conditions=False)
    labels = _emit(nc1, {}, strict=False)
    nc = bass.Bass("TRN2", target_bir_lowering=False, detect_race_conditions=False)
    labels2 = _emit(nc, labels, strict=True)
    return nc


_BUILT = {}


def kernel(L0_attn_x, L0_attn_W_in, L0_attn_b_in, L0_attn_log_A, L0_attn_B_vec,
           L0_attn_C, L0_attn_D, L0_ffn_weight, L0_ffn_ln_gamma, L0_ffn_ln_beta):
    if "nc" not in _BUILT:
        _BUILT["nc"] = _build()
    nc = _BUILT["nc"]
    vecs = np.stack([
        np.asarray(L0_attn_b_in), np.asarray(L0_attn_log_A),
        np.asarray(L0_attn_B_vec), np.asarray(L0_attn_D),
        np.asarray(L0_ffn_ln_gamma), np.asarray(L0_ffn_ln_beta)
    ]).astype(np.float32, copy=False)
    vecs = np.ascontiguousarray(
        vecs.reshape(6, 8, 128).transpose(2, 0, 1).reshape(128, 48))
    wffn = np.ascontiguousarray(np.asarray(L0_ffn_weight, dtype=np.float32))
    win = np.ascontiguousarray(np.asarray(L0_attn_W_in, dtype=np.float32))
    cmat = np.ascontiguousarray(np.asarray(L0_attn_C, dtype=np.float32))
    x = np.asarray(L0_attn_x)
    in_maps = [{
        "xT": np.ascontiguousarray(x[i].T),
        "wffn": wffn, "win": win, "cmat": cmat, "vecs": vecs,
    } for i in range(x.shape[0])]
    res = run_bass_kernel_spmd(nc, in_maps, core_ids=list(range(len(in_maps))))
    return np.stack([np.ascontiguousarray(r["outT"].T) for r in res.results])


# version 14
# speedup vs baseline: 1.1691x; 1.0028x over previous
from contextlib import ExitStack

import numpy as np
import concourse.bass as bass
import concourse.mybir as mybir
from concourse.bass_isa import ReduceOp
from concourse.bass_utils import run_bass_kernel_spmd

F32 = mybir.dt.float32
BF16 = mybir.dt.bfloat16
ALU = mybir.AluOpType
ACT = mybir.ActivationFunctionType
AXX = mybir.AxisListType.X

D = 1024
NS = 8
TC = 512
T = 4096
NCH = T // TC
QB = 128.0
THR_C = (QB - 1e-5) / QB
INV_D = 1.0 / D
FFN_A = 1.001 / D
FFN_B = 1e-8
FFN_M = 1.001
SSM_A = 1.0 / D
SSM_B = 1e-5
SSM_M = 1.0


class Ctr:
    def __init__(self, labels, strict):
        self.labels = labels
        self.strict = strict
        self.cnt = {}
        self.out = {}

    def inc(self, instr, semname, sems, n=1, label=None):
        c = self.cnt.get(semname, 0) + n
        self.cnt[semname] = c
        if label is not None:
            self.out[label] = (semname, c)
        if instr is not None and n > 0:
            instr.then_inc(sems[semname], n)
        return c

    def mark(self, semname, label):
        self.out[label] = (semname, self.cnt.get(semname, 0))

    def wait(self, eng, label, sems):
        if label not in self.labels:
            if self.strict:
                raise KeyError(f"unknown label {label!r}")
            return
        semname, val = self.labels[label]
        if val > 0:
            eng.wait_ge(sems[semname], val)


DEBUG = False


def _emit(nc, labels, strict):
    C = Ctr(labels, strict)

    xT_h = nc.dram_tensor("xT", [D, T], F32, kind="ExternalInput")
    wffn_h = nc.dram_tensor("wffn", [D, D], F32, kind="ExternalInput")
    win_h = nc.dram_tensor("win", [D, D], F32, kind="ExternalInput")
    cmat_h = nc.dram_tensor("cmat", [D, D], F32, kind="ExternalInput")
    vecs_h = nc.dram_tensor("vecs", [128, 6 * NS], F32, kind="ExternalInput")
    outT_h = nc.dram_tensor("outT", [D, T], F32, kind="ExternalOutput")
    wqs_h = nc.dram_tensor("wq_scr", [D, D], BF16, kind="Internal")
    xT, wffn, win, cmat = xT_h.ap(), wffn_h.ap(), win_h.ap(), cmat_h.ap()
    outT, wq_scr = outT_h.ap(), wqs_h.ap()
    if DEBUG:
        dbg1 = nc.dram_tensor("dbg1", [D, T], F32, kind="ExternalOutput").ap()
        dbg2 = nc.dram_tensor("dbg2", [D, T], F32, kind="ExternalOutput").ap()
        dbgs = nc.dram_tensor("dbgs", [128, 7 * NCH * 4 + 16 + NS], F32,
                              kind="ExternalOutput").ap()
        dbga = nc.dram_tensor("dbga", [128, 8 * TC], F32, kind="ExternalOutput").ap()
        dbgr = nc.dram_tensor("dbgr", [128, NCH * 4], F32, kind="ExternalOutput").ap()

    with ExitStack() as st:
        def sb(name, shape, dt=F32):
            return st.enter_context(nc.sbuf_tensor(name, shape, dt))

        x_sb = [sb(f"x{s}", [128, T]) for s in range(NS)]
        w1 = [sb(f"w1_{s}", [128, D], BF16) for s in range(NS)]
        w2 = [sb(f"w2_{s}", [128, D], BF16) for s in range(NS)]
        bufX = [[sb(f"bx{p}_{s}", [128, TC], BF16) for s in range(NS)]
                for p in range(2)]
        bufH = [sb(f"bh{s}", [128, TC], BF16) for s in range(NS)]
        sqt_t = sb("sqt", [128, TC], BF16)
        sqt = [sqt_t, sqt_t]
        tn1_t = sb("tn1", [128, TC], BF16)
        tn1_b = sb("tn1b", [128, TC], BF16)
        tn1 = [tn1_t, tn1_b]

        scr1 = sb("scr1", [128, TC]); scr2 = sb("scr2", [128, TC])
        scr3 = sb("scr3", [128, TC])
        b_sb = sb("b_sb", [128, TC]); h_sb = b_sb
        s_rep = sb("s_rep", [128, TC])
        srow = sb("srow", [1, TC]); m2row = sb("m2row", [1, TC])
        mrow = m2row[0:1, :]
        srow_s = srow[0:1, :]
        vrow = scr2[0:1, :]
        qrow = scr1[0:1, :]
        mslots = sb("mslots", [128, NCH * NS], BF16)
        sqv = [sqt_t] + [bufH[i] for i in range(1, NS)]
        p2rb = sb("p2rb", [33, TC], BF16)
        murow_b = p2rb[0:1, :]
        m2row_b = p2rb[32:33, :]
        cmp_s = {k: sb(f"cs_{k}", [128, NCH * 4]) for k in ("x", "x1")}
        cmp_m2 = {k: sb(f"cm_{k}", [128, NCH * 4]) for k in ("x", "x1")}
        cmp_s["x2"] = cmp_s["x"]
        cmp_m2["x2"] = cmp_m2["x"]
        cmp_mu = cmp_m2["x"]
        wstage = sb("wst", [128, TC])
        wsb = bufH[0]
        vct = sb("vct", [128, 6 * NS])
        vc = {k: vct[:, j * NS:(j + 1) * NS] for j, k in
              enumerate(("bin", "la", "bvec", "dd", "gam", "bet"))}
        a_cols = vc["la"]
        bb_cols = vc["bin"]
        nd_colsb = sb("nd_colsb", [128, NS], BF16)
        ws_cols = sb("ws_cols", [128, NS], BF16)
        vcc1 = sb("vcc1", [128, 8]); vcc2 = sb("vcc2", [128, 4])
        dvin = vcc1[:, 0:4]; dvout = vcc1[:, 4:8]; dvt = vcc2[:, 0:4]
        gsl1 = mslots[0:1, 0:NCH * NS]
        gsl2 = gsl1
        p2r = sb("p2r", [33, D], BF16)
        nwrow = [p2r[0:1, e * 128:(e + 1) * 128] for e in range(NS)]
        ndrow = [p2r[32:33, e * 128:(e + 1) * 128] for e in range(NS)]
        ones_bf = sb("ones_bf", [128, 128], BF16)
        ones1_t = sb("ones1", [1, 128])
        ones1 = ones1_t[:]
        onescol_b = ones_bf[:, 0:1]
        carry = sb("carry", [128, NS])
        cpk = sb("cpk", [128, 16])
        (mx1, mx2, gred, _unused_wsum, wred, beta_c, inv_t, inv_c, hbeta_c,
         thr1, nthr1, thr2, nthr2) = [cpk[:, j:j + 1] for j in range(13)]
        wsum = cpk[:, 3:4]
        stg_a = cpk[0:1, 13:14]
        stg_b = cpk[0:1, 14:15]

        PB = [st.enter_context(nc.psum_tensor(f"pb{i}", [128, TC], F32))
              for i in range(8)]

        semnames = ["LX", "LW", "GQ", "RQ", "RS", "ST", "V", "S", "T", "G", "F", "RM"]
        sems = {n: st.enter_context(nc.semaphore(n)) for n in semnames}

        def col(tile, j):
            try:
                return tile[:, j:j + 1]
            except Exception:
                return tile

        def bcast(tile, j, n=TC):
            free = int(tile.shape[1])
            return bass.AP(tile, j, [[free, 128], [0, n]])

        def chunk(tile, c):
            return tile[:, c * TC:(c + 1) * TC]

        blk = nc.Block(); blk.__enter__()

        @blk.sync
        def _(sy):
            sy.dma_start(out=vct[:], in_=vecs_h.ap()[:]).then_inc(sems["RQ"], 16)
            rq = C.inc(None, "RQ", sems, 16)
            sy.wait_ge(sems["RQ"], rq)
            i = sy.nop()
            C.inc(i, "F", sems, 1, label="vecf")
            for c in range(NCH):
                for s in range(NS):
                    sy.dma_start(out=chunk(x_sb[s], c),
                                 in_=xT[s * 128:(s + 1) * 128, c * TC:(c + 1) * TC]
                                 ).then_inc(sems["LX"], 16)
                    C.inc(None, "LX", sems, 16)
                if c >= 1:
                    sy.wait_ge(sems["LX"], 16 * NS * c)
                    i = sy.nop()
                    C.inc(i, "F", sems, 1, label=f"lxf:{c - 1}")
            sy.wait_ge(sems["LX"], 16 * NS * NCH)
            i = sy.nop()
            C.inc(i, "F", sems, 1, label=f"lxf:{NCH - 1}")
            def _wload(k):
                s, hf = k // 2, k % 2
                if k == 0:
                    C.wait(sy, f"p:wr1:{2 * NS - 1}", sems)
                else:
                    C.wait(sy, f"p:wq:{k - 1}", sems)
                sy.dma_start(out=wstage[:],
                             in_=wffn[s * 128:(s + 1) * 128,
                                      hf * TC:(hf + 1) * TC]
                             ).then_inc(sems["LW"], 16)
                C.inc(None, "LW", sems, 16, label=f"p:w2:{k}")

            wl_sched = {4: [0, 1, 2, 3, 4], 5: [5, 6, 7, 8],
                        6: [9, 10, 11, 12], 7: [13, 14, 15]}
            for c in range(NCH):
                C.wait(sy, f"a1:math1:{c}", sems)
                sy.dma_start(out=cmp_m2["x"][:, 4 * c:4 * c + 4],
                             in_=mrow).then_inc(sems["RM"], 16)
                C.inc(None, "RM", sems, 16, label=f"a1:mst:{c}")
                for k in wl_sched.get(c, []):
                    _wload(k)
                C.wait(sy, f"a1:sr:{c}", sems)
                sy.dma_start(out=cmp_s["x"][:, 4 * c:4 * c + 4],
                             in_=srow_s).then_inc(sems["RQ"], 16)
                C.inc(None, "RQ", sems, 16, label=f"a1:sst:{c}")

        @blk.gpsimd
        def _(gp):
            for k in range(2 * NS):
                s, hf = k // 2, k % 2
                if k >= 1:
                    C.wait(gp, f"p:wr1:{k - 1}", sems)
                gp.dma_start(out=wstage[:],
                             in_=wffn[s * 128:(s + 1) * 128,
                                      hf * TC:(hf + 1) * TC]
                             ).then_inc(sems["GQ"], 16)
                C.inc(None, "GQ", sems, 16, label=f"p:w1:{k}")
            for s in range(NS):
                C.wait(gp, f"p:wq:{2 * s + 1}", sems)
                gp.dma_start(out=wq_scr[s * 128:(s + 1) * 128, :],
                             in_=w1[s][:]).then_inc(sems["GQ"], 16)
                C.inc(None, "GQ", sems, 16, label=f"p:wqs:{s}")
            for s in range(NS):
                gp.dma_start(out=w2[s][:], in_=win[s * 128:(s + 1) * 128, :]
                             ).then_inc(sems["GQ"], 16)
                C.inc(None, "GQ", sems, 16)
            gp.wait_ge(sems["GQ"], C.cnt["GQ"])
            i = gp.nop()
            C.inc(i, "F", sems, 1, label="lwinf")
            C.wait(gp, "a1:slots", sems)
            i = gp.tensor_reduce(gsl1[0:1, 0:1], mx1[:],
                                 mybir.AxisListType.XYZWC, ALU.max)
            C.inc(i, "G", sems, 1, label="a1:gmax")
        @blk.vector
        def _(v):
            i = v.memset(carry[:], 0.0)
            C.inc(i, "V", sems, 1)
            i = v.memset(ones_bf[:], 1.0)
            C.inc(i, "V", sems, 1, label="p:ones")
            v.memset(ones1_t[:], 1.0)
            i = v.memset(cpk[:, 15:16], 1.0)
            C.inc(i, "V", sems, 1, label="p:ones1")
            v.wait_ge(sems["V"], C.cnt["V"])

            def _wpass1(k):
                C.wait(v, f"p:w1:{k}", sems)
                if k == 0:
                    i = v.tensor_reduce(wsum[:], wstage[:], AXX, ALU.add,
                                        apply_absolute_value=True)
                    nv = C.inc(i, "V", sems, 1, label=f"p:wr1:{k}")
                    v.wait_ge(sems["V"], nv)
                else:
                    i = v.tensor_reduce(wred[:], wstage[:], AXX, ALU.add,
                                        apply_absolute_value=True)
                    nv = C.inc(i, "V", sems, 1)
                    v.wait_ge(sems["V"], nv)
                    i = v.tensor_tensor(wsum[:], wsum[:], wred[:], ALU.add)
                    nv = C.inc(i, "V", sems, 1, label=f"p:wr1:{k}")
                    v.wait_ge(sems["V"], nv)

            C.wait(v, "p:exp", sems)
            v.tensor_scalar(a_cols[:], a_cols[:], -1.0, None, ALU.mult)
            C.wait(v, "vecf", sems)
            v.tensor_tensor(bb_cols[:], vc["bin"][:], vc["bvec"][:], ALU.mult)
            i = v.tensor_scalar(nd_colsb[:], vc["dd"][:], -1.0, None, ALU.mult)
            C.inc(i, "V", sems, 1, label="p:vecd")
            for c in range(NCH + 2):
                if c >= 2:
                    cc = c - 2
                    C.wait(v, f"a1:replmu:{cc}", sems)
                    C.wait(v, f"a1:repls:{cc}", sems)
                    pm, ps_ = PB[2 + (cc % 2) * 2], PB[3 + (cc % 2) * 2]
                    for s in range(NS):
                        v.tensor_tensor(tn1[s % 2][:], chunk(x_sb[s], cc),
                                        pm[:], ALU.subtract)
                        v.tensor_tensor(tn1[s % 2][:], tn1[s % 2][:], ps_[:],
                                        ALU.mult)
                        v.tensor_scalar(tn1[s % 2][:], tn1[s % 2][:],
                                        col(vc["gam"], s), col(vc["bet"], s),
                                        ALU.mult, ALU.add)
                        k = cc * NS + s
                        i = v.tensor_reduce(mslots[:, k:k + 1], tn1[s % 2][:],
                                            AXX, ALU.max,
                                            apply_absolute_value=True)
                        C.inc(i, "V", sems, 1, label=f"a1:sl:{cc}:{s}")
                if c <= 3:
                    for k in range(4 * c, 4 * c + 4):
                        _wpass1(k)
                if 1 <= c <= NCH:
                    cc = c - 1
                    C.wait(v, f"a1:sg:{cc}", sems)
                    if cc >= 1:
                        C.wait(v, f"a1:sr:{cc - 1}", sems)
                    i = v.reciprocal(dvt, dvin)
                    nv = C.inc(i, "V", sems, 1)
                    v.wait_ge(sems["V"], nv)
                    i = v.tensor_scalar(dvout, dvt, 1.0, None, ALU.mult)
                    C.inc(i, "V", sems, 1, label=f"a1:rc:{cc}")
                if c == 4:
                    def _ss(i):
                        nv = C.inc(i, "V", sems, 1)
                        v.wait_ge(sems["V"], nv)
                        return nv
                    C.wait(v, "p:wsmm", sems)
                    _ss(v.tensor_copy(wred[0:1, 0:1], PB[7][0:1, 2:3]))
                    _ss(v.tensor_scalar(beta_c[0:1, :], wred[0:1, 0:1],
                                        1.0 / (D * D), None, ALU.mult))
                    _ss(v.tensor_scalar(stg_b, beta_c[0:1, :], 0.5, None,
                                        ALU.mult))
                    _ss(v.tensor_scalar(inv_t[0:1, :], beta_c[0:1, :], 1e-5,
                                        None, ALU.add))
                    _ss(v.reciprocal(mx2[0:1, :], inv_t[0:1, :]))
                    i = v.tensor_scalar(stg_a, mx2[0:1, :], 1.0, None, ALU.mult)
                    nv = C.inc(i, "V", sems, 1, label="p:invrow")
                    v.wait_ge(sems["V"], nv)
                    C.wait(v, "p:invmm", sems)
                    i = v.tensor_copy(cpk[:, 7:9], PB[7][:, 0:2])
                    nv = C.inc(i, "V", sems, 1, label="p:inv")
                    v.wait_ge(sems["V"], nv)
                if 5 <= c <= 8:
                    for k in range(4 * (c - 5), 4 * (c - 4)):
                        C.wait(v, f"p:w2:{k}", sems)
                        v.tensor_scalar(wstage[:], wstage[:], col(inv_c, 0),
                                        None, ALU.mult)
                        half = w1[k // 2][:, (k % 2) * TC:(k % 2 + 1) * TC]
                        v.tensor_scalar(wsb[:], wstage[:], -0.5, None, ALU.is_le)
                        v.tensor_scalar(half, wstage[:], 0.5, None, ALU.is_ge)
                        i = v.tensor_tensor(half, half, wsb[:], ALU.subtract)
                        C.inc(i, "V", sems, 1, label=f"p:wq:{k}")
                if c <= NCH - 1:
                    C.wait(v, f"a1:stat:{c}", sems)
                    if c >= 1:
                        C.wait(v, f"a1:replmu:{c - 1}", sems)
                        C.wait(v, f"a1:mst:{c - 1}", sems)
                    v.tensor_scalar(mrow, PB[0][0:1, :], INV_D, None, ALU.mult)
                    i = v.tensor_scalar(vrow, PB[1][0:1, :], FFN_A, FFN_B,
                                        ALU.mult, ALU.add)
                    nv = C.inc(i, "V", sems, 1)
                    v.wait_ge(sems["V"], nv)
                    i = v.scalar_tensor_tensor(qrow, mrow, FFN_M, mrow,
                                               ALU.mult, ALU.mult)
                    nv = C.inc(i, "V", sems, 1)
                    v.wait_ge(sems["V"], nv)
                    i = v.tensor_tensor(vrow, vrow, qrow, ALU.subtract)
                    C.inc(i, "V", sems, 1, label=f"a1:math1:{c}")
            v.wait_ge(sems["V"], C.cnt["V"])
            i = v.tensor_reduce(mx1[:], mslots[:], AXX, ALU.max)
            nv = C.inc(i, "V", sems, 1, label="a1:slots")
            v.wait_ge(sems["V"], nv)
            C.wait(v, "a1:gmax", sems)
            i = v.tensor_scalar(stg_a, gsl1[0:1, 0:1], THR_C, None, ALU.mult)
            nv = C.inc(i, "V", sems, 1)
            v.wait_ge(sems["V"], nv)
            i = v.tensor_scalar(stg_b, stg_a, -1.0, None, ALU.mult)
            nv = C.inc(i, "V", sems, 1, label="thr1v")
            v.wait_ge(sems["V"], nv)

        @blk.scalar
        def _(sc):
            def _casts(c):
                C.wait(sc, f"lxf:{c}", sems)
                for s in range(NS):
                    if c >= 2:
                        C.wait(sc, f"a1:statmm:{c - 2}:{s}", sems)
                    sc.copy(bufX[c % 2][s][:], chunk(x_sb[s], c))
                    if c >= 1:
                        C.wait(sc, f"a1:statmm:{c - 1}:{s}", sems)
                    i = sc.activation(sqv[s][:], bufX[c % 2][s][:],
                                      ACT.Square)
                    C.inc(i, "S", sems, 1, label=f"a1:cast:{c}:{s}")

            def _dance_back(cc):
                C.wait(sc, f"a1:rc:{cc}", sems)
                if cc >= 1:
                    C.wait(sc, f"a1:repls:{cc - 1}", sems)
                    C.wait(sc, f"a1:sst:{cc - 1}", sems)
                sc.dma_start(out=srow_s, in_=dvout,
                             allow_slow_non_contiguous=True
                             ).then_inc(sems["RS"], 16)
                rs = C.inc(None, "RS", sems, 16)
                sc.wait_ge(sems["RS"], rs)
                i = sc.nop()
                C.inc(i, "S", sems, 1, label=f"a1:sr:{cc}")

            _casts(0)
            C.wait(sc, "vecf", sems)
            i = sc.activation(a_cols[:], vc["la"][:], ACT.Exp)
            C.inc(i, "S", sems, 1, label="p:exp")
            for c in range(NCH):
                if c + 1 <= NCH - 1:
                    _casts(c + 1)
                if c >= 1:
                    _dance_back(c - 1)
                C.wait(sc, f"a1:math1:{c}", sems)
                i = sc.activation(vrow, vrow, ACT.Sqrt)
                ns = C.inc(i, "S", sems, 1)
                sc.wait_ge(sems["S"], ns)
                sc.dma_start(out=dvin, in_=vrow,
                             allow_slow_non_contiguous=True
                             ).then_inc(sems["RS"], 16)
                rs = C.inc(None, "RS", sems, 16)
                sc.wait_ge(sems["RS"], rs)
                i = sc.nop()
                C.inc(i, "S", sems, 1, label=f"a1:sg:{c}")
            _dance_back(NCH - 1)

        @blk.tensor
        def _(t):
            C.wait(t, "p:ones", sems)
            C.wait(t, "p:ones1", sems)
            for c in range(NCH + 2):
                if c >= 2 and c - 2 <= NCH - 1:
                    cc = c - 2
                    C.wait(t, f"a1:sr:{cc}", sems)
                    if cc >= 2:
                        C.wait(t, f"a1:sl:{cc - 2}:{NS - 1}", sems)
                    i = t.matmul(PB[3 + (cc % 2) * 2][:], ones1, srow_s,
                                 start=True, stop=True)
                    C.inc(i, "T", sems, 1, label=f"a1:repls:{cc}")
                if c <= NCH - 1:
                    for s in range(NS):
                        C.wait(t, f"a1:cast:{c}:{s}", sems)
                        if c >= 1 and s == 0:
                            C.wait(t, f"a1:math1:{c - 1}", sems)
                        t.matmul(PB[0][:], ones_bf[:], bufX[c % 2][s][:],
                                 start=(s == 0), stop=(s == NS - 1))
                        i = t.matmul(PB[1][:], ones_bf[:], sqv[s][:],
                                     start=(s == 0), stop=(s == NS - 1))
                        C.inc(i, "T", sems, 1, label=f"a1:statmm:{c}:{s}")
                    C.mark("T", f"a1:stat:{c}")
                if c == 3:
                    C.wait(t, f"p:wr1:{2 * NS - 1}", sems)
                    i = t.matmul(PB[7][0:1, 2:3], wsum, cpk[:, 15:16],
                                 start=True, stop=True)
                    C.inc(i, "T", sems, 1, label="p:wsmm")
                if c == 4:
                    C.wait(t, "p:invrow", sems)
                    i = t.matmul(PB[7][:, 0:2], ones1, cpk[0:1, 13:15],
                                 start=True, stop=True)
                    C.inc(i, "T", sems, 1, label="p:invmm")
                if c >= 1 and c - 1 <= NCH - 1:
                    cc = c - 1
                    C.wait(t, f"a1:math1:{cc}", sems)
                    if cc >= 2:
                        C.wait(t, f"a1:sl:{cc - 2}:{NS - 1}", sems)
                    i = t.matmul(PB[2 + (cc % 2) * 2][:], ones1, mrow,
                                 start=True, stop=True)
                    C.inc(i, "T", sems, 1, label=f"a1:replmu:{cc}")

        for e in ("V", "S", "T", "G"):
            C.mark(e, f"end:A1:{e}")

        def barrier(eng, sweep):
            for e in ("V", "S", "T", "G"):
                C.wait(eng, f"end:{sweep}:{e}", sems)

        blk.__exit__(None, None, None)
        blk2 = nc.Block(); blk2.__enter__()
        mrowB = scr3[0:1, :]
        vrowB = vrow
        qrowB = qrow

        @blk2.sync
        def _(sy):
            for c in range(NCH):
                if c == 0:
                    C.wait(sy, f"a1:sst:{NCH - 1}", sems)
                    C.wait(sy, f"a1:mst:{NCH - 1}", sems)
                else:
                    C.wait(sy, f"b1:repl:{c - 1}", sems)
                sy.dma_start(out=srow[:], in_=cmp_s["x"][:, 4 * c:4 * c + 4]
                             ).then_inc(sems["RQ"], 16)
                C.inc(None, "RQ", sems, 16)
                sy.dma_start(out=m2row[:], in_=cmp_m2["x"][:, 4 * c:4 * c + 4]
                             ).then_inc(sems["RQ"], 16)
                C.inc(None, "RQ", sems, 16, label=f"b1:rows:{c}")
                if c >= 1:
                    C.wait(sy, f"b1:math1:{c - 1}", sems)
                    sy.dma_start(out=cmp_mu[:, 4 * (c - 1):4 * (c - 1) + 4],
                                 in_=mrowB).then_inc(sems["RM"], 16)
                    C.inc(None, "RM", sems, 16, label=f"b1:mst:{c - 1}")
                    if DEBUG:
                        C.wait(sy, f"b1:dbgq:{c - 1}", sems)
                        sy.dma_start(out=dbgr[:, 4 * (c - 1):4 * (c - 1) + 4],
                                     in_=qrowB).then_inc(sems["RQ"], 16)
                        C.inc(None, "RQ", sems, 16, label=f"b1:dbgst:{c - 1}")
            C.wait(sy, f"b1:math1:{NCH - 1}", sems)
            sy.dma_start(out=cmp_mu[:, 4 * (NCH - 1):4 * NCH],
                         in_=mrowB).then_inc(sems["RM"], 16)
            C.inc(None, "RM", sems, 16, label=f"b1:mst:{NCH - 1}")
            if DEBUG:
                C.wait(sy, f"b1:dbgq:{NCH - 1}", sems)
                sy.dma_start(out=dbgr[:, 4 * (NCH - 1):4 * NCH],
                             in_=qrowB).then_inc(sems["RQ"], 16)
                C.inc(None, "RQ", sems, 16, label=f"b1:dbgst:{NCH - 1}")

        @blk2.tensor
        def _(t):
            barrier(t, "A1")
            C.wait(t, "thr1v", sems)
            i = t.matmul(PB[6][:, 0:2], ones1, cpk[0:1, 13:15],
                         start=True, stop=True)
            C.inc(i, "T", sems, 1, label="thr1mm")
            for c in range(NCH + 1):
                if c <= NCH - 1:
                    C.wait(t, f"b1:rows:{c}", sems)
                    if c >= 2:
                        C.wait(t, f"b1:xq:{c - 2}", sems)
                    r1, r2 = PB[2 + (c % 2) * 2], PB[3 + (c % 2) * 2]
                    t.matmul(r1[:], ones1, srow[:], start=True, stop=True)
                    i = t.matmul(r2[:], ones1, m2row[:], start=True, stop=True)
                    C.inc(i, "T", sems, 1, label=f"b1:repl:{c}")
                if c >= 1:
                    cc = c - 1
                    C.wait(t, f"b1:xq:{cc}", sems)
                    i = None
                    for e in range(NS):
                        k = cc * NS + e
                        if k >= 2:
                            C.wait(t, f"b1:evac:{(k - 2) // NS}:{(k - 2) % NS}",
                                   sems)
                        for s in range(NS):
                            i = t.matmul(PB[6 + e % 2][:],
                                         w1[s][:, e * 128:(e + 1) * 128],
                                         bufX[cc % 2][s][:], start=(s == 0),
                                         stop=(s == NS - 1))
                        C.inc(i, "T", sems, 1, label=f"b1:mm:{cc}:{e}")
                    for s in range(NS):
                        C.wait(t, f"b1:cast1:{cc}:{s}", sems)
                        if cc >= 1 and s == 0:
                            C.wait(t, f"b1:math1:{cc - 1}", sems)
                        t.matmul(PB[0][:], ones_bf[:], bufX[cc % 2][s][:],
                                 start=(s == 0), stop=(s == NS - 1))
                        i = t.matmul(PB[1][:], ones_bf[:], sqt[s % 2][:],
                                     start=(s == 0), stop=(s == NS - 1))
                        C.inc(i, "T", sems, 1, label=f"b1:statmm:{cc}:{s}")
                    C.mark("T", f"b1:stat:{cc}")
            C.mark("T", "b1:mmall")

        @blk2.vector
        def _(v):
            barrier(v, "A1")
            C.wait(v, "thr1mm", sems)
            i = v.tensor_copy(cpk[:, 9:11], PB[6][:, 0:2])
            nv = C.inc(i, "V", sems, 1)
            v.wait_ge(sems["V"], nv)
            for c in range(NCH + 2):
                if c <= NCH - 1 or c == NCH:
                    cc = c - 1
                    if c <= NCH - 1:
                        C.wait(v, f"b1:repl:{c}", sems)
                        if c >= 2:
                            C.wait(v, f"b1:statmm:{c - 2}:{NS - 1}", sems)
                            if DEBUG and c - 2 == 4:
                                C.wait(v, f"b1:dbgbx:4", sems)
                    r1, r2 = PB[2 + (c % 2) * 2], PB[3 + (c % 2) * 2]
                    i = None
                    for s in range(NS):
                        if c <= NCH - 1:
                            v.tensor_tensor(b_sb[:], chunk(x_sb[s], c), r2[:],
                                            ALU.subtract)
                            v.tensor_tensor(s_rep[:], b_sb[:], r1[:], ALU.mult)
                            v.tensor_scalar(b_sb[:], s_rep[:],
                                            col(vc["gam"], s),
                                            col(vc["bet"], s),
                                            ALU.mult, ALU.add)
                            i = v.tensor_scalar(bufX[c % 2][s][:], b_sb[:],
                                                col(thr1, 0), col(nthr1, 0),
                                                ALU.min, ALU.max)
                            if s == NS - 1:
                                C.inc(i, "V", sems, 1, label=f"b1:xq:{c}")
                        if 1 <= c:
                            C.wait(v, f"b1:mm:{cc}:{s}", sems)
                            i = v.scalar_tensor_tensor(chunk(x_sb[s], cc),
                                                       PB[6 + s % 2][:],
                                                       col(hbeta_c, 0),
                                                       chunk(x_sb[s], cc),
                                                       ALU.mult, ALU.add)
                            C.inc(i, "V", sems, 1, label=f"b1:evac:{cc}:{s}")
                if 2 <= c <= NCH + 1:
                    cc = c - 2
                    C.wait(v, f"b1:sg:{cc}", sems)
                    if cc >= 1:
                        C.wait(v, f"b1:sr:{cc - 1}", sems)
                    i = v.reciprocal(dvt, dvin)
                    nv = C.inc(i, "V", sems, 1)
                    v.wait_ge(sems["V"], nv)
                    i = v.tensor_scalar(dvout, dvt, 1.0, None, ALU.mult)
                    C.inc(i, "V", sems, 1, label=f"b1:rc:{cc}")
                if 1 <= c <= NCH:
                    cc = c - 1
                    C.wait(v, f"b1:stat:{cc}", sems)
                    if cc >= 1:
                        C.wait(v, f"b1:mst:{cc - 1}", sems)
                    v.tensor_scalar(mrowB, PB[0][0:1, :], INV_D, None, ALU.mult)
                    i = v.tensor_scalar(vrowB, PB[1][0:1, :], SSM_A, SSM_B,
                                        ALU.mult, ALU.add)
                    nv = C.inc(i, "V", sems, 1)
                    v.wait_ge(sems["V"], nv)
                    i = v.scalar_tensor_tensor(qrowB, mrowB, SSM_M, mrowB,
                                               ALU.mult, ALU.mult)
                    nv = C.inc(i, "V", sems, 1)
                    v.wait_ge(sems["V"], nv)
                    i = v.tensor_tensor(vrowB, vrowB, qrowB, ALU.subtract)
                    C.inc(i, "V", sems, 1, label=f"b1:math1:{cc}")
                    if DEBUG:
                        if cc >= 1:
                            C.wait(v, f"b1:dbgst:{cc - 1}", sems)
                        i = v.tensor_copy(qrowB, PB[0][0:1, :])
                        C.inc(i, "V", sems, 1, label=f"b1:dbgq:{cc}")
            C.wait(v, f"b1:sr:{NCH - 1}", sems)
            C.wait(v, f"b1:mst:{NCH - 1}", sems)
            i = v.tensor_tensor(cmp_m2["x1"][:], cmp_mu[:], cmp_s["x1"][:],
                                ALU.mult)
            nv = C.inc(i, "V", sems, 1, label="b1:m2c")
            v.wait_ge(sems["V"], nv)

        @blk2.scalar
        def _(sc):
            barrier(sc, "A1")
            for c in range(1, NCH + 2):
                if c - 1 <= NCH - 1:
                    cc = c - 1
                    for s in range(NS):
                        k = cc * NS + s
                        C.wait(sc, f"b1:evac:{cc}:{s}", sems)
                        C.wait(sc, f"b1:mm:{cc}:{NS - 1}", sems)
                        sc.copy(bufX[cc % 2][s][:], chunk(x_sb[s], cc))
                        if k >= 1:
                            C.wait(sc,
                                   f"b1:statmm:{(k - 1) // NS}:{(k - 1) % NS}",
                                   sems)
                        i = sc.activation(sqt[s % 2][:], bufX[cc % 2][s][:],
                                          ACT.Square)
                        C.inc(i, "S", sems, 1, label=f"b1:cast1:{cc}:{s}")
                if c >= 2:
                    cc = c - 2
                    C.wait(sc, f"b1:rc:{cc}", sems)
                    sc.dma_start(out=cmp_s["x1"][:, 4 * cc:4 * cc + 4],
                                 in_=dvout, allow_slow_non_contiguous=True
                                 ).then_inc(sems["RS"], 16)
                    rs = C.inc(None, "RS", sems, 16)
                    sc.wait_ge(sems["RS"], rs)
                    i = sc.nop()
                    C.inc(i, "S", sems, 1, label=f"b1:sr:{cc}")
                if c - 1 <= NCH - 1:
                    cc = c - 1
                    C.wait(sc, f"b1:math1:{cc}", sems)
                    i = sc.activation(vrowB, vrowB, ACT.Sqrt)
                    ns = C.inc(i, "S", sems, 1)
                    sc.wait_ge(sems["S"], ns)
                    sc.dma_start(out=dvin, in_=vrowB,
                                 allow_slow_non_contiguous=True
                                 ).then_inc(sems["RS"], 16)
                    rs = C.inc(None, "RS", sems, 16)
                    sc.wait_ge(sems["RS"], rs)
                    i = sc.nop()
                    C.inc(i, "S", sems, 1, label=f"b1:sg:{cc}")

        @blk2.gpsimd
        def _(gp):
            barrier(gp, "A1")
            if DEBUG:
                C.wait(gp, "b1:stat:4", sems)
                for s in range(NS):
                    gp.dma_start(out=dbga[:, s * TC:(s + 1) * TC],
                                 in_=bufX[0][s][:]).then_inc(sems["GQ"], 16)
                    C.inc(None, "GQ", sems, 16,
                          label=f"b1:dbgbx:4" if s == NS - 1 else None)
            C.wait(gp, "b1:mmall", sems)
            for s in range(NS):
                gp.dma_start(out=w1[s][:], in_=cmat[s * 128:(s + 1) * 128, :]
                             ).then_inc(sems["GQ"], 16)
                C.inc(None, "GQ", sems, 16)
            gp.wait_ge(sems["GQ"], C.cnt["GQ"])
            i = gp.nop()
            C.inc(i, "F", sems, 1, label="lcmatf")

        for e in ("V", "S", "T", "G"):
            C.mark(e, f"end:B1:{e}")

        blk2.__exit__(None, None, None)
        blk3 = nc.Block(); blk3.__enter__()
        mrowS = m2row[0:1, :]
        vrowS = scr2[0:1, :]
        qrowS = wstage[0:1, :]

        @blk3.sync
        def _(sy):
            if DEBUG:
                for e in ("V", "S", "T", "G"):
                    C.wait(sy, f"end:B1:{e}", sems)
                for s in range(NS):
                    sy.dma_start(out=dbg1[s * 128:(s + 1) * 128, :],
                                 in_=x_sb[s][:]).then_inc(sems["RQ"], 16)
                    C.inc(None, "RQ", sems, 16)
            def _ssm_mst(cc):
                C.wait(sy, f"ssm:math1:{cc}", sems)
                sy.dma_start(out=cmp_m2["x2"][:, 4 * cc:4 * cc + 4],
                             in_=mrowS).then_inc(sems["RM"], 16)
                C.inc(None, "RM", sems, 16, label=f"ssm:mst:{cc}")

            for c in range(NCH):
                if c == 0:
                    C.wait(sy, f"b1:sr:{NCH - 1}", sems)
                    C.wait(sy, "b1:m2c", sems)
                else:
                    C.wait(sy, f"ssm:repl:{c - 1}", sems)
                if c >= 3:
                    C.wait(sy, f"ssm:rep3:{c - 3}", sems)
                sy.dma_start(out=srow[:], in_=cmp_s["x1"][:, 4 * c:4 * c + 4]
                             ).then_inc(sems["RQ"], 16)
                C.inc(None, "RQ", sems, 16, label=f"ssm:rows:{c}")
                if c >= 2:
                    _ssm_mst(c - 2)
            _ssm_mst(NCH - 2)
            _ssm_mst(NCH - 1)

        @blk3.gpsimd
        def _(gp):
            barrier(gp, "B1")
            for c in range(NCH + 2):
                if c <= NCH - 1:
                    if c > 0:
                        C.wait(gp, f"ssm:umm:{c - 1}:{NS - 1}", sems)
                        C.wait(gp, f"ssm:ymm:{c - 1}:{NS - 1}", sems)
                    gp.dma_start(out=murow_b[:], in_=cmp_mu[:, 4 * c:4 * c + 4]
                                 ).then_inc(sems["GQ"], 16)
                    C.inc(None, "GQ", sems, 16)
                    gp.dma_start(out=m2row_b[:],
                                 in_=cmp_m2["x1"][:, 4 * c:4 * c + 4]
                                 ).then_inc(sems["GQ"], 16)
                    C.inc(None, "GQ", sems, 16, label=f"ssm:browz:{c}")
                if c >= 2:
                    cc = c - 2
                    for s in range(NS):
                        k = cc * NS + s
                        C.wait(gp, f"ssm:ab:{cc}:{s}", sems)
                        if cc == 0 and s == 0:
                            C.wait(gp, "thr1v", sems)
                        i = gp.tensor_reduce(gsl2[0:1, k:k + 1], tn1[s % 2][:],
                                             mybir.AxisListType.XYZWC, ALU.max)
                        C.inc(i, "G", sems, 1, label=f"ssm:red:{cc}:{s}")
                if c <= NCH - 1:
                    for f in range(NS):
                        C.wait(gp, f"ssm:xy:{c}:{f}", sems)
                        i = gp.tensor_tensor(chunk(x_sb[f], c),
                                             chunk(x_sb[f], c),
                                             (scr1 if f % 2 == 0 else scr3)[:],
                                             ALU.add)
                        C.inc(i, "G", sems, 1, label=f"ssm:xt4:{c}:{f}")
            C.wait(gp, "ssm:ulast", sems)
            gp.wait_ge(sems["GQ"], C.cnt["GQ"])
            for s in range(NS):
                gp.dma_start(out=w2[s][:], in_=wq_scr[s * 128:(s + 1) * 128, :]
                             ).then_inc(sems["GQ"], 16)
                C.inc(None, "GQ", sems, 16)
            gp.wait_ge(sems["GQ"], C.cnt["GQ"])
            i = gp.nop()
            C.inc(i, "F", sems, 1, label="lwq2f")

        @blk3.tensor
        def _(t):
            barrier(t, "B1")
            i = None
            for e in range(NS):
                if e == 0:
                    C.wait(t, "lwinf", sems)
                for s in range(NS):
                    i = t.matmul(PB[2][:, e:e + 1],
                                 w2[s][:, e * 128:(e + 1) * 128],
                                 onescol_b[:], start=(s == 0),
                                 stop=(s == NS - 1))
                C.inc(i, "T", sems, 1, label=f"ssm:ws:{e}")
            for c in range(NCH):
                C.wait(t, f"ssm:rows:{c}", sems)
                if c == 0:
                    C.wait(t, f"ssm:wsc:{NS - 1}", sems)
                if c >= 2:
                    C.wait(t, f"ssm:srepdone:{c - 2}", sems)
                i = t.matmul(PB[2 + c % 2][:], ones1, srow[:],
                             start=True, stop=True)
                C.inc(i, "T", sems, 1, label=f"ssm:repl:{c}")
                for s in range(NS):
                    C.wait(t, f"ssm:cast1:{c}:{s}", sems)
                C.wait(t, f"ssm:browz:{c}", sems)
                if c == 0:
                    C.wait(t, "p:vecws", sems)
                for e in range(NS):
                    k = c * NS + e
                    if e == 4 and c >= 2:
                        cc = c - 2
                        C.wait(t, f"ssm:sr:{cc}", sems)
                        C.wait(t, f"ssm:math1:{c - 1}", sems)
                        t.matmul(PB[0][:], ones1, srow[:], start=True,
                                 stop=True)
                        i = t.matmul(PB[1][:], ones1, mrowS, start=True,
                                     stop=True)
                        C.inc(i, "T", sems, 1, label=f"ssm:rep3:{cc}")
                    if k >= 2:
                        C.wait(t, f"ssm:b:{(k - 2) // NS}:{(k - 2) % NS}", sems)
                    for s in range(NS):
                        t.matmul(PB[4 + e % 2][:],
                                 w2[s][:, e * 128:(e + 1) * 128],
                                 bufX[c % 2][s][:], start=(s == 0), stop=False)
                    i = t.matmul(PB[4 + e % 2][:], nwrow[e][:], murow_b[:],
                                 start=False, stop=True)
                    C.inc(i, "T", sems, 1, label=f"ssm:umm:{c}:{e}")
                if c == NCH - 1:
                    C.mark("T", "ssm:ulast")
                for f in range(NS):
                    k = c * NS + f
                    C.wait(t, f"ssm:hhat:{c}:{NS - 1}", sems)
                    if k >= 2:
                        C.wait(t, f"ssm:xy:{(k - 2) // NS}:{(k - 2) % NS}",
                               sems)
                    if c == 0 and f == 0:
                        C.wait(t, "lcmatf", sems)
                    for e in range(NS):
                        t.matmul(PB[6 + f % 2][:],
                                 w1[e][:, f * 128:(f + 1) * 128],
                                 bufH[e][:], start=(e == 0), stop=False)
                    i = t.matmul(PB[6 + f % 2][:], ndrow[f][:], m2row_b[:],
                                 start=False, stop=True)
                    C.inc(i, "T", sems, 1, label=f"ssm:ymm:{c}:{f}")
                for s in range(NS):
                    C.wait(t, f"ssm:cast2:{c}:{s}", sems)
                    if c >= 2 and s == 0:
                        C.wait(t, f"ssm:xn2:{c - 2}:{NS - 1}", sems)
                    t.matmul(PB[0][:], ones_bf[:], bufX[c % 2][s][:],
                             start=(s == 0), stop=(s == NS - 1))
                    i = t.matmul(PB[1][:], ones_bf[:], sqt[s % 2][:],
                                 start=(s == 0), stop=(s == NS - 1))
                    C.inc(i, "T", sems, 1, label=f"ssm:statmm:{c}:{s}")
                C.mark("T", f"ssm:stat:{c}")
            for cc in (NCH - 2, NCH - 1):
                C.wait(t, f"ssm:sr:{cc}", sems)
                C.wait(t, f"ssm:math1:{NCH - 1}", sems)
                if cc == NCH - 2:
                    C.wait(t, f"ssm:xn2:{NCH - 3}:{NS - 1}", sems)
                else:
                    C.wait(t, f"ssm:xn2:{NCH - 2}:{NS - 1}", sems)
                t.matmul(PB[0][:], ones1, srow[:], start=True, stop=True)
                i = t.matmul(PB[1][:], ones1, mrowS, start=True, stop=True)
                C.inc(i, "T", sems, 1, label=f"ssm:rep3:{cc}")

        @blk3.vector
        def _(v):
            barrier(v, "B1")
            for e in range(NS):
                C.wait(v, f"ssm:ws:{e}", sems)
                i = v.tensor_scalar(col(ws_cols, e), PB[2][:, e:e + 1], -1.0,
                                    None, ALU.mult)
                C.inc(i, "V", sems, 1, label=f"ssm:wsc:{e}")
            for c in range(NCH + 2):
                if c <= NCH - 1:
                    C.wait(v, f"ssm:repl:{c}", sems)
                    i = v.tensor_copy(s_rep[:], PB[2 + c % 2][:])
                    nv = C.inc(i, "V", sems, 1, label=f"ssm:srepdone:{c}")
                    v.wait_ge(sems["V"], nv)
                if 2 <= c <= NCH + 1:
                    cc = c - 2
                    C.wait(v, f"ssm:sg:{cc}", sems)
                    if cc >= 1:
                        C.wait(v, f"ssm:sr:{cc - 1}", sems)
                    i = v.reciprocal(dvt, dvin)
                    nv = C.inc(i, "V", sems, 1)
                    v.wait_ge(sems["V"], nv)
                    i = v.tensor_scalar(dvout, dvt, 1.0, None, ALU.mult)
                    C.inc(i, "V", sems, 1, label=f"ssm:rc:{cc}")
                if 1 <= c <= NCH:
                    cc = c - 1
                    C.wait(v, f"ssm:stat:{cc}", sems)
                    if cc >= 1:
                        C.wait(v, f"ssm:mst:{cc - 1}", sems)
                    v.tensor_scalar(mrowS, PB[0][0:1, :], INV_D, None, ALU.mult)
                    i = v.tensor_scalar(vrowS, PB[1][0:1, :], FFN_A, FFN_B,
                                        ALU.mult, ALU.add)
                    nv = C.inc(i, "V", sems, 1)
                    v.wait_ge(sems["V"], nv)
                    i = v.scalar_tensor_tensor(qrowS, mrowS, FFN_M, mrowS,
                                               ALU.mult, ALU.mult)
                    nv = C.inc(i, "V", sems, 1)
                    v.wait_ge(sems["V"], nv)
                    i = v.tensor_tensor(vrowS, vrowS, qrowS, ALU.subtract)
                    C.inc(i, "V", sems, 1, label=f"ssm:math1:{cc}")
                if c <= NCH - 1:
                    for e in range(NS):
                        C.wait(v, f"ssm:umm:{c}:{e}", sems)
                        if e >= 1:
                            C.wait(v, f"ssm:hcast:{c}:{e - 1}", sems)
                        elif c >= 1:
                            C.wait(v, f"ssm:hcast:{c - 1}:{NS - 1}", sems)
                        v.scalar_tensor_tensor(b_sb[:], PB[4 + e % 2][:],
                                               col(vc["bvec"], e), s_rep[:],
                                               ALU.mult, ALU.mult)
                        i = v.tensor_scalar(b_sb[:], b_sb[:], col(bb_cols, e),
                                            None, ALU.add)
                        C.inc(i, "V", sems, 1, label=f"ssm:b:{c}:{e}")
                        a_bc = bass.AP(vct, NS + e, [[6 * NS, 128], [0, TC]])
                        v.tensor_tensor_scan(h_sb[:], a_bc, b_sb[:],
                                             col(carry, e), ALU.mult, ALU.add)
                        i = v.tensor_copy(col(carry, e), h_sb[:, TC - 1:TC])
                        nv = C.inc(i, "V", sems, 1, label=f"ssm:scan:{c}:{e}")
                        v.wait_ge(sems["V"], nv)
                if 2 <= c <= NCH + 1:
                    cc = c - 2
                    C.wait(v, f"ssm:rep3:{cc}", sems)
                    for s in range(NS):
                        k = cc * NS + s
                        if k >= 2:
                            C.wait(v, f"ssm:red:{(k - 2) // NS}:{(k - 2) % NS}",
                                   sems)
                        C.wait(v, f"ssm:xt4:{cc}:{s}", sems)
                        v.tensor_tensor(tn1[s % 2][:], chunk(x_sb[s], cc),
                                        PB[1][:], ALU.subtract)
                        v.tensor_tensor(tn1[s % 2][:], tn1[s % 2][:], PB[0][:],
                                        ALU.mult)
                        i = v.tensor_scalar(tn1[s % 2][:], tn1[s % 2][:],
                                            col(vc["gam"], s), col(vc["bet"], s),
                                            ALU.mult, ALU.add)
                        C.inc(i, "V", sems, 1, label=f"ssm:xn2:{cc}:{s}")
                if c <= NCH - 1:
                    for f in range(NS):
                        C.wait(v, f"ssm:ymm:{c}:{f}", sems)
                        if f >= 2:
                            C.wait(v, f"ssm:xt4:{c}:{f - 2}", sems)
                        elif c >= 1:
                            C.wait(v, f"ssm:xt4:{c - 1}:{f + NS - 2}", sems)
                        t4buf = scr1 if f % 2 == 0 else scr3
                        v.scalar_tensor_tensor(t4buf[:], chunk(x_sb[f], c),
                                               col(vc["dd"], f), s_rep[:],
                                               ALU.mult, ALU.mult)
                        i = v.tensor_tensor(chunk(x_sb[f], c),
                                            chunk(x_sb[f], c),
                                            PB[6 + f % 2][:], ALU.add)
                        C.inc(i, "V", sems, 1, label=f"ssm:xy:{c}:{f}")
            C.wait(v, f"ssm:red:{NCH - 1}:{NS - 1}", sems)
            i = v.tensor_reduce(mx1[0:1, :], gsl1[0:1, 0:64], AXX, ALU.max)
            nv = C.inc(i, "V", sems, 1)
            v.wait_ge(sems["V"], nv)
            i = v.tensor_scalar(stg_a, mx1[0:1, :], THR_C, None, ALU.mult)
            nv = C.inc(i, "V", sems, 1)
            v.wait_ge(sems["V"], nv)
            i = v.tensor_scalar(stg_b, stg_a, -1.0, None, ALU.mult)
            nv = C.inc(i, "V", sems, 1, label="thr2v")
            v.wait_ge(sems["V"], nv)

        @blk3.scalar
        def _(sc):
            barrier(sc, "B1")
            C.wait(sc, "p:vecd", sems)
            for e in range(NS):
                C.wait(sc, f"ssm:wsc:{e}", sems)
                sc.dma_start(out=nwrow[e][:], in_=col(ws_cols, e),
                             allow_slow_non_contiguous=True
                             ).then_inc(sems["RS"], 16)
                C.inc(None, "RS", sems, 16)
                sc.dma_start(out=ndrow[e][:], in_=col(nd_colsb, e),
                             allow_slow_non_contiguous=True
                             ).then_inc(sems["RS"], 16)
                C.inc(None, "RS", sems, 16, label=f"ssm:rowdma:{e}")
            sc.wait_ge(sems["RS"], C.cnt["RS"])
            i = sc.nop()
            C.inc(i, "S", sems, 1, label="p:vecws")
            for s in range(NS):
                i = sc.copy(bufX[0][s][:], chunk(x_sb[s], 0))
                C.inc(i, "S", sems, 1, label=f"ssm:cast1:0:{s}")
            for c in range(NCH + 2):
                if c + 1 <= NCH - 1:
                    for s in range(NS):
                        if c + 1 >= 2:
                            C.wait(sc, f"ssm:statmm:{c - 1}:{s}", sems)
                        i = sc.copy(bufX[(c + 1) % 2][s][:],
                                    chunk(x_sb[s], c + 1))
                        C.inc(i, "S", sems, 1, label=f"ssm:cast1:{c + 1}:{s}")
                if 2 <= c <= NCH + 1:
                    cc = c - 2
                    C.wait(sc, f"ssm:rc:{cc}", sems)
                    if cc + 2 <= NCH - 1:
                        C.wait(sc, f"ssm:repl:{cc + 2}", sems)
                    elif cc >= 1:
                        C.wait(sc, f"ssm:rep3:{cc - 1}", sems)
                    sc.dma_start(out=srow[:], in_=dvout,
                                 allow_slow_non_contiguous=True
                                 ).then_inc(sems["RS"], 16)
                    rs = C.inc(None, "RS", sems, 16)
                    sc.wait_ge(sems["RS"], rs)
                    sc.dma_start(out=cmp_s["x2"][:, 4 * cc:4 * cc + 4],
                                 in_=dvout, allow_slow_non_contiguous=True
                                 ).then_inc(sems["RS"], 16)
                    rs = C.inc(None, "RS", sems, 16)
                    sc.wait_ge(sems["RS"], rs)
                    i = sc.nop()
                    C.inc(i, "S", sems, 1, label=f"ssm:sr:{cc}")
                if 1 <= c <= NCH:
                    cc = c - 1
                    C.wait(sc, f"ssm:math1:{cc}", sems)
                    i = sc.activation(vrowS, vrowS, ACT.Sqrt)
                    ns = C.inc(i, "S", sems, 1)
                    sc.wait_ge(sems["S"], ns)
                    sc.dma_start(out=dvin, in_=vrowS,
                                 allow_slow_non_contiguous=True
                                 ).then_inc(sems["RS"], 16)
                    rs = C.inc(None, "RS", sems, 16)
                    sc.wait_ge(sems["RS"], rs)
                    i = sc.nop()
                    C.inc(i, "S", sems, 1, label=f"ssm:sg:{cc}")
                if c <= NCH - 1:
                    for e in range(NS):
                        C.wait(sc, f"ssm:scan:{c}:{e}", sems)
                        if c >= 1 and e == 0:
                            C.wait(sc, f"ssm:ymm:{c - 1}:{NS - 1}", sems)
                        i = sc.copy(bufH[e][:], h_sb[:])
                        C.inc(i, "S", sems, 1, label=f"ssm:hhat:{c}:{e}")
                        C.mark("S", f"ssm:hcast:{c}:{e}")
                if 2 <= c <= NCH + 1:
                    cc = c - 2
                    for s in range(NS):
                        C.wait(sc, f"ssm:xn2:{cc}:{s}", sems)
                        i = sc.activation(tn1[s % 2][:], tn1[s % 2][:], ACT.Abs)
                        C.inc(i, "S", sems, 1, label=f"ssm:ab:{cc}:{s}")
                if c <= NCH - 1:
                    for s in range(NS):
                        k = c * NS + s
                        C.wait(sc, f"ssm:xt4:{c}:{s}", sems)
                        C.wait(sc, f"ssm:umm:{c}:{NS - 1}", sems)
                        sc.copy(bufX[c % 2][s][:], chunk(x_sb[s], c))
                        if k >= 1:
                            C.wait(sc,
                                   f"ssm:statmm:{(k - 1) // NS}:{(k - 1) % NS}",
                                   sems)
                        i = sc.activation(sqt[s % 2][:], bufX[c % 2][s][:],
                                          ACT.Square)
                        C.inc(i, "S", sems, 1, label=f"ssm:cast2:{c}:{s}")

        for e in ("V", "S", "T", "G"):
            C.mark(e, f"end:SSM:{e}")

        blk3.__exit__(None, None, None)
        blk4 = nc.Block(); blk4.__enter__()

        @blk4.sync
        def _(sy):
            if DEBUG:
                for e in ("V", "S", "T", "G"):
                    C.wait(sy, f"end:SSM:{e}", sems)
                for s in range(NS):
                    sy.dma_start(out=dbg2[s * 128:(s + 1) * 128, :],
                                 in_=x_sb[s][:]).then_inc(sems["RQ"], 16)
                    C.inc(None, "RQ", sems, 16)
            def _b2_store(c):
                for e in range(NS):
                    C.wait(sy, f"b2:evac:{c}:{e}", sems)
                    sy.dma_start(out=outT[e * 128:(e + 1) * 128,
                                          c * TC:(c + 1) * TC],
                                 in_=chunk(x_sb[e], c)).then_inc(sems["ST"], 16)
                    C.inc(None, "ST", sems, 16, label=f"st:{c}:{e}")

            for c in range(NCH):
                if c == 0:
                    C.wait(sy, f"ssm:sr:{NCH - 1}", sems)
                    C.wait(sy, f"ssm:mst:{NCH - 1}", sems)
                    C.wait(sy, f"ssm:rep3:{NCH - 1}", sems)
                else:
                    C.wait(sy, f"b2:repl:{c - 1}", sems)
                sy.dma_start(out=srow[:], in_=cmp_s["x2"][:, 4 * c:4 * c + 4]
                             ).then_inc(sems["RQ"], 16)
                C.inc(None, "RQ", sems, 16)
                sy.dma_start(out=m2row[:], in_=cmp_m2["x2"][:, 4 * c:4 * c + 4]
                             ).then_inc(sems["RQ"], 16)
                C.inc(None, "RQ", sems, 16, label=f"b2:rows:{c}")
                if c >= 1:
                    _b2_store(c - 1)
            _b2_store(NCH - 1)

        def _b2_big(t, c):
            C.wait(t, f"b2:xq:{c}", sems)
            i = None
            for e in range(NS):
                k = c * NS + e
                if k >= 2:
                    C.wait(t, f"b2:evac:{(k - 2) // NS}:{(k - 2) % NS}", sems)
                for s in range(NS):
                    i = t.matmul(PB[6 + e % 2][:], w2[s][:, e * 128:(e + 1) * 128],
                                 bufX[c % 2][s][:], start=(s == 0),
                                 stop=(s == NS - 1))
                C.inc(i, "T", sems, 1, label=f"b2:mm:{c}:{e}")

        @blk4.tensor
        def _(t):
            barrier(t, "SSM")
            C.wait(t, "thr2v", sems)
            i = t.matmul(PB[6][:, 0:2], ones1, cpk[0:1, 13:15],
                         start=True, stop=True)
            C.inc(i, "T", sems, 1, label="thr2mm")
            C.wait(t, "lwq2f", sems)
            for c in range(NCH):
                C.wait(t, f"b2:rows:{c}", sems)
                if c >= 2:
                    C.wait(t, f"b2:xq:{c - 2}", sems)
                r1, r2 = PB[2 + (c % 2) * 2], PB[3 + (c % 2) * 2]
                t.matmul(r1[:], ones1, srow[:], start=True, stop=True)
                i = t.matmul(r2[:], ones1, m2row[:], start=True, stop=True)
                C.inc(i, "T", sems, 1, label=f"b2:repl:{c}")
                C.mark("T", f"b2:replq:{c}")
                if c >= 1:
                    _b2_big(t, c - 1)
            _b2_big(t, NCH - 1)

        @blk4.vector
        def _(v):
            barrier(v, "SSM")
            C.wait(v, "thr2mm", sems)
            i = v.tensor_copy(cpk[:, 11:13], PB[6][:, 0:2])
            nv = C.inc(i, "V", sems, 1)
            v.wait_ge(sems["V"], nv)
            for c in range(NCH + 1):
                cc = c - 1
                if c <= NCH - 1:
                    C.wait(v, f"b2:repl:{c}", sems)
                    if c >= 2:
                        C.wait(v, f"b2:mm:{c - 2}:{NS - 1}", sems)
                r1, r2 = PB[2 + (c % 2) * 2], PB[3 + (c % 2) * 2]
                i = None
                for s in range(NS):
                    if c <= NCH - 1:
                        v.tensor_tensor(b_sb[:], chunk(x_sb[s], c), r2[:],
                                        ALU.subtract)
                        v.tensor_tensor(s_rep[:], b_sb[:], r1[:], ALU.mult)
                        v.tensor_scalar(b_sb[:], s_rep[:], col(vc["gam"], s),
                                        col(vc["bet"], s), ALU.mult, ALU.add)
                        i = v.tensor_scalar(bufX[c % 2][s][:], b_sb[:],
                                            col(thr2, 0), col(nthr2, 0),
                                            ALU.min, ALU.max)
                        if s == NS - 1:
                            C.inc(i, "V", sems, 1, label=f"b2:xq:{c}")
                    if c >= 1:
                        C.wait(v, f"b2:mm:{cc}:{s}", sems)
                        i = v.scalar_tensor_tensor(chunk(x_sb[s], cc),
                                                   PB[6 + s % 2][:],
                                                   col(hbeta_c, 0),
                                                   chunk(x_sb[s], cc),
                                                   ALU.mult, ALU.add)
                        C.inc(i, "V", sems, 1, label=f"b2:evac:{cc}:{s}")

        @blk4.gpsimd
        def _(gp):
            barrier(gp, "SSM")
            gp.wait_ge(sems["ST"], 16 * NCH * NS)
            if DEBUG:
                w = NCH * 4
                for j, tt in enumerate([cmp_s["x"], cmp_m2["x"], cmp_s["x1"],
                                        cmp_m2["x1"], cmp_mu, cmp_s["x2"],
                                        cmp_m2["x2"]]):
                    gp.dma_start(out=dbgs[:, j * w:(j + 1) * w], in_=tt[:]
                                 ).then_inc(sems["GQ"], 16)
                    C.inc(None, "GQ", sems, 16)
                gp.dma_start(out=dbgs[:, 7 * w:7 * w + 16], in_=cpk[:]
                             ).then_inc(sems["GQ"], 16)
                C.inc(None, "GQ", sems, 16)
                gp.dma_start(out=dbgs[:, 7 * w + 16:7 * w + 16 + NS],
                             in_=carry[:]).then_inc(sems["GQ"], 16)
                C.inc(None, "GQ", sems, 16)
                gp.wait_ge(sems["GQ"], C.cnt["GQ"])

        blk4.__exit__(None, None, None)

    return C.out


def _build():
    nc1 = bass.Bass("TRN2", target_bir_lowering=False, detect_race_conditions=False)
    labels = _emit(nc1, {}, strict=False)
    nc = bass.Bass("TRN2", target_bir_lowering=False, detect_race_conditions=False)
    labels2 = _emit(nc, labels, strict=True)
    return nc


_BUILT = {}


def kernel(L0_attn_x, L0_attn_W_in, L0_attn_b_in, L0_attn_log_A, L0_attn_B_vec,
           L0_attn_C, L0_attn_D, L0_ffn_weight, L0_ffn_ln_gamma, L0_ffn_ln_beta):
    if "nc" not in _BUILT:
        _BUILT["nc"] = _build()
    nc = _BUILT["nc"]
    vecs = np.stack([
        np.asarray(L0_attn_b_in), np.asarray(L0_attn_log_A),
        np.asarray(L0_attn_B_vec), np.asarray(L0_attn_D),
        np.asarray(L0_ffn_ln_gamma), np.asarray(L0_ffn_ln_beta)
    ]).astype(np.float32, copy=False)
    vecs = np.ascontiguousarray(
        vecs.reshape(6, 8, 128).transpose(2, 0, 1).reshape(128, 48))
    wffn = np.ascontiguousarray(np.asarray(L0_ffn_weight, dtype=np.float32))
    win = np.ascontiguousarray(np.asarray(L0_attn_W_in, dtype=np.float32))
    cmat = np.ascontiguousarray(np.asarray(L0_attn_C, dtype=np.float32))
    x = np.asarray(L0_attn_x)
    in_maps = [{
        "xT": np.ascontiguousarray(x[i].T),
        "wffn": wffn, "win": win, "cmat": cmat, "vecs": vecs,
    } for i in range(x.shape[0])]
    res = run_bass_kernel_spmd(nc, in_maps, core_ids=list(range(len(in_maps))))
    return np.stack([np.ascontiguousarray(r["outT"].T) for r in res.results])


# version 17
# speedup vs baseline: 1.2760x; 1.0915x over previous
from contextlib import ExitStack

import numpy as np
import concourse.bass as bass
import concourse.mybir as mybir
from concourse.bass_isa import ReduceOp
from concourse.bass_utils import run_bass_kernel_spmd

F32 = mybir.dt.float32
BF16 = mybir.dt.bfloat16
ALU = mybir.AluOpType
ACT = mybir.ActivationFunctionType
AXX = mybir.AxisListType.X

D = 1024
NS = 8
TC = 512
T = 4096
NCH = T // TC
QB = 128.0
THR_C = (QB - 1e-5) / QB
INV_D = 1.0 / D
FFN_A = 1.001 / D
FFN_B = 1e-8
FFN_M = 1.001
SSM_A = 1.0 / D
SSM_B = 1e-5
SSM_M = 1.0


class Ctr:
    def __init__(self, labels, strict):
        self.labels = labels
        self.strict = strict
        self.cnt = {}
        self.out = {}

    def inc(self, instr, semname, sems, n=1, label=None):
        c = self.cnt.get(semname, 0) + n
        self.cnt[semname] = c
        if label is not None:
            self.out[label] = (semname, c)
        if instr is not None and n > 0:
            instr.then_inc(sems[semname], n)
        return c

    def mark(self, semname, label):
        self.out[label] = (semname, self.cnt.get(semname, 0))

    def wait(self, eng, label, sems):
        if label not in self.labels:
            if self.strict:
                raise KeyError(f"unknown label {label!r}")
            return
        semname, val = self.labels[label]
        if val > 0:
            eng.wait_ge(sems[semname], val)


DEBUG = False


def _emit(nc, labels, strict):
    C = Ctr(labels, strict)

    xT_h = nc.dram_tensor("xT", [D, T], F32, kind="ExternalInput")
    wffn_h = nc.dram_tensor("wffn", [D, D], F32, kind="ExternalInput")
    win_h = nc.dram_tensor("win", [D, D], F32, kind="ExternalInput")
    cmat_h = nc.dram_tensor("cmat", [D, D], F32, kind="ExternalInput")
    vecs_h = nc.dram_tensor("vecs", [128, 6 * NS], F32, kind="ExternalInput")
    outT_h = nc.dram_tensor("outT", [D, T], F32, kind="ExternalOutput")
    wqs_h = nc.dram_tensor("wq_scr", [D, D], BF16, kind="Internal")
    xT, wffn, win, cmat = xT_h.ap(), wffn_h.ap(), win_h.ap(), cmat_h.ap()
    outT, wq_scr = outT_h.ap(), wqs_h.ap()
    if DEBUG:
        dbg1 = nc.dram_tensor("dbg1", [D, T], F32, kind="ExternalOutput").ap()
        dbg2 = nc.dram_tensor("dbg2", [D, T], F32, kind="ExternalOutput").ap()
        dbgs = nc.dram_tensor("dbgs", [128, 7 * NCH * 4 + 16 + NS], F32,
                              kind="ExternalOutput").ap()
        dbga = nc.dram_tensor("dbga", [128, 8 * TC], F32, kind="ExternalOutput").ap()
        dbgr = nc.dram_tensor("dbgr", [128, NCH * 4], F32, kind="ExternalOutput").ap()

    with ExitStack() as st:
        def sb(name, shape, dt=F32):
            return st.enter_context(nc.sbuf_tensor(name, shape, dt))

        x_sb = [sb(f"x{s}", [128, T]) for s in range(NS)]
        w1 = [sb(f"w1_{s}", [128, D], BF16) for s in range(NS)]
        w2 = [sb(f"w2_{s}", [128, D], BF16) for s in range(NS)]
        bufX = [[sb(f"bx{p}_{s}", [128, TC], BF16) for s in range(NS)]
                for p in range(2)]
        bufH = [sb(f"bh{s}", [128, TC], BF16) for s in range(NS)]
        sqt_t = sb("sqt", [128, TC], BF16)
        sqt = [sqt_t, sqt_t]
        tn1_t = sb("tn1", [128, TC], BF16)
        tn1_b = sb("tn1b", [128, TC], BF16)
        tn1 = [tn1_t, tn1_b]

        scr1 = sb("scr1", [128, TC]); scr2 = sb("scr2", [128, TC])
        scr3 = sb("scr3", [128, TC])
        b_sb = sb("b_sb", [128, TC]); h_sb = b_sb
        s_rep = sb("s_rep", [128, TC])
        srow = sb("srow", [1, TC]); m2row = sb("m2row", [1, TC])
        mrow = m2row[0:1, :]
        srow_s = srow[0:1, :]
        vrow = scr2[0:1, :]
        qrow = scr1[0:1, :]
        mslots = sb("mslots", [128, NCH * NS], BF16)
        sqv = [sqt_t] + [bufH[i] for i in range(1, NS)]
        p2rb = sb("p2rb", [33, TC], BF16)
        murow_b = p2rb[0:1, :]
        m2row_b = p2rb[32:33, :]
        cmp_s = {k: sb(f"cs_{k}", [128, NCH * 4]) for k in ("x", "x1")}
        cmp_m2 = {k: sb(f"cm_{k}", [128, NCH * 4]) for k in ("x", "x1")}
        cmp_s["x2"] = cmp_s["x"]
        cmp_m2["x2"] = cmp_m2["x"]
        cmp_mu = cmp_m2["x"]
        wstage = sb("wst", [128, TC])
        wsb = bufH[0]
        vct = sb("vct", [128, 6 * NS])
        vc = {k: vct[:, j * NS:(j + 1) * NS] for j, k in
              enumerate(("bin", "la", "bvec", "dd", "gam", "bet"))}
        a_cols = vc["la"]
        bb_cols = vc["bin"]
        nd_colsb = sb("nd_colsb", [128, NS], BF16)
        ws_cols = sb("ws_cols", [128, NS], BF16)
        vcc1 = sb("vcc1", [128, 8]); vcc2 = sb("vcc2", [128, 4])
        dvin = vcc1[:, 0:4]; dvout = vcc1[:, 4:8]; dvt = vcc2[:, 0:4]
        gsl1 = mslots[0:1, 0:NCH * NS]
        gsl2 = gsl1
        p2r = sb("p2r", [33, D], BF16)
        nwrow = [p2r[0:1, e * 128:(e + 1) * 128] for e in range(NS)]
        ndrow = [p2r[32:33, e * 128:(e + 1) * 128] for e in range(NS)]
        ones_bf = sb("ones_bf", [128, 128], BF16)
        ones1_t = sb("ones1", [1, 128])
        ones1 = ones1_t[:]
        onescol_b = ones_bf[:, 0:1]
        carry = sb("carry", [128, NS])
        cpk = sb("cpk", [128, 16])
        (mx1, mx2, gred, _unused_wsum, wred, beta_c, inv_t, inv_c, hbeta_c,
         thr1, nthr1, thr2, nthr2) = [cpk[:, j:j + 1] for j in range(13)]
        wsum = cpk[:, 3:4]
        stg_a = cpk[0:1, 13:14]
        stg_b = cpk[0:1, 14:15]

        PB = [st.enter_context(nc.psum_tensor(f"pb{i}", [128, TC], F32))
              for i in range(8)]

        semnames = ["LX", "LW", "GQ", "RQ", "RS", "ST", "V", "S", "T", "G", "F", "RM"]
        sems = {n: st.enter_context(nc.semaphore(n)) for n in semnames}

        def col(tile, j):
            try:
                return tile[:, j:j + 1]
            except Exception:
                return tile

        def bcast(tile, j, n=TC):
            free = int(tile.shape[1])
            return bass.AP(tile, j, [[free, 128], [0, n]])

        def chunk(tile, c):
            return tile[:, c * TC:(c + 1) * TC]

        blk = nc.Block(); blk.__enter__()

        @blk.sync
        def _(sy):
            sy.dma_start(out=vct[:], in_=vecs_h.ap()[:]).then_inc(sems["RQ"], 16)
            rq = C.inc(None, "RQ", sems, 16)
            sy.wait_ge(sems["RQ"], rq)
            i = sy.nop()
            C.inc(i, "F", sems, 1, label="vecf")
            for c in range(NCH):
                for s in range(NS):
                    sy.dma_start(out=chunk(x_sb[s], c),
                                 in_=xT[s * 128:(s + 1) * 128, c * TC:(c + 1) * TC]
                                 ).then_inc(sems["LX"], 16)
                    C.inc(None, "LX", sems, 16)
                if c >= 1:
                    sy.wait_ge(sems["LX"], 16 * NS * c)
                    i = sy.nop()
                    C.inc(i, "F", sems, 1, label=f"lxf:{c - 1}")
            sy.wait_ge(sems["LX"], 16 * NS * NCH)
            i = sy.nop()
            C.inc(i, "F", sems, 1, label=f"lxf:{NCH - 1}")
            def _wload(k):
                s, hf = k // 2, k % 2
                if k == 0:
                    C.wait(sy, f"p:wr1:{2 * NS - 1}", sems)
                else:
                    C.wait(sy, f"p:wq:{k - 1}", sems)
                sy.dma_start(out=wstage[:],
                             in_=wffn[s * 128:(s + 1) * 128,
                                      hf * TC:(hf + 1) * TC]
                             ).then_inc(sems["LW"], 16)
                C.inc(None, "LW", sems, 16, label=f"p:w2:{k}")

            wl_sched = {4: [0, 1, 2, 3, 4], 5: [5, 6, 7, 8],
                        6: [9, 10, 11, 12], 7: [13, 14, 15]}
            for c in range(NCH):
                C.wait(sy, f"a1:math1:{c}", sems)
                sy.dma_start(out=cmp_m2["x"][:, 4 * c:4 * c + 4],
                             in_=mrow).then_inc(sems["RM"], 16)
                C.inc(None, "RM", sems, 16, label=f"a1:mst:{c}")
                for k in wl_sched.get(c, []):
                    _wload(k)
                C.wait(sy, f"a1:sr:{c}", sems)
                sy.dma_start(out=cmp_s["x"][:, 4 * c:4 * c + 4],
                             in_=srow_s).then_inc(sems["RQ"], 16)
                C.inc(None, "RQ", sems, 16, label=f"a1:sst:{c}")

        @blk.gpsimd
        def _(gp):
            for c in range(NCH + 2):
                if c <= 3:
                    for k in range(4 * c, 4 * c + 4):
                        s, hf = k // 2, k % 2
                        if k >= 1:
                            C.wait(gp, f"p:wr1:{k - 1}", sems)
                        gp.dma_start(out=wstage[:],
                                     in_=wffn[s * 128:(s + 1) * 128,
                                              hf * TC:(hf + 1) * TC]
                                     ).then_inc(sems["GQ"], 16)
                        C.inc(None, "GQ", sems, 16, label=f"p:w1:{k}")
                if c >= 2:
                    cc = c - 2
                    for s in range(NS):
                        k = cc * NS + s
                        C.wait(gp, f"a1:ab:{cc}:{s}", sems)
                        i = gp.tensor_reduce(gsl1[0:1, k:k + 1], tn1[s % 2][:],
                                             mybir.AxisListType.XYZWC, ALU.max)
                        C.inc(i, "G", sems, 1, label=f"a1:red:{cc}:{s}")
            for s in range(NS):
                C.wait(gp, f"p:wq:{2 * s + 1}", sems)
                gp.dma_start(out=wq_scr[s * 128:(s + 1) * 128, :],
                             in_=w1[s][:]).then_inc(sems["GQ"], 16)
                C.inc(None, "GQ", sems, 16, label=f"p:wqs:{s}")
            for s in range(NS):
                gp.dma_start(out=w2[s][:], in_=win[s * 128:(s + 1) * 128, :]
                             ).then_inc(sems["GQ"], 16)
                C.inc(None, "GQ", sems, 16)
            gp.wait_ge(sems["GQ"], C.cnt["GQ"])
            i = gp.nop()
            C.inc(i, "F", sems, 1, label="lwinf")
        @blk.vector
        def _(v):
            i = v.memset(carry[:], 0.0)
            C.inc(i, "V", sems, 1)
            i = v.memset(ones_bf[:], 1.0)
            C.inc(i, "V", sems, 1, label="p:ones")
            v.memset(ones1_t[:], 1.0)
            i = v.memset(cpk[:, 15:16], 1.0)
            C.inc(i, "V", sems, 1, label="p:ones1")
            v.wait_ge(sems["V"], C.cnt["V"])

            def _wpass1(k):
                C.wait(v, f"p:w1:{k}", sems)
                if k == 0:
                    i = v.tensor_reduce(wsum[:], wstage[:], AXX, ALU.add,
                                        apply_absolute_value=True)
                    nv = C.inc(i, "V", sems, 1, label=f"p:wr1:{k}")
                    v.wait_ge(sems["V"], nv)
                else:
                    i = v.tensor_reduce(wred[:], wstage[:], AXX, ALU.add,
                                        apply_absolute_value=True)
                    nv = C.inc(i, "V", sems, 1)
                    v.wait_ge(sems["V"], nv)
                    i = v.tensor_tensor(wsum[:], wsum[:], wred[:], ALU.add)
                    nv = C.inc(i, "V", sems, 1, label=f"p:wr1:{k}")
                    v.wait_ge(sems["V"], nv)

            C.wait(v, "p:exp", sems)
            v.tensor_scalar(a_cols[:], a_cols[:], -1.0, None, ALU.mult)
            C.wait(v, "vecf", sems)
            v.tensor_tensor(bb_cols[:], vc["bin"][:], vc["bvec"][:], ALU.mult)
            i = v.tensor_scalar(nd_colsb[:], vc["dd"][:], -1.0, None, ALU.mult)
            C.inc(i, "V", sems, 1, label="p:vecd")
            for c in range(NCH + 2):
                if c <= 3:
                    for k in range(4 * c, 4 * c + 4):
                        _wpass1(k)
                if c >= 2:
                    cc = c - 2
                    C.wait(v, f"a1:replmu:{cc}", sems)
                    C.wait(v, f"a1:repls:{cc}", sems)
                    pm, ps_ = PB[2 + (cc % 2) * 2], PB[3 + (cc % 2) * 2]
                    for s in range(NS):
                        k = cc * NS + s
                        if k >= 2:
                            C.wait(v, f"a1:red:{(k - 2) // NS}:{(k - 2) % NS}",
                                   sems)
                        v.tensor_tensor(tn1[s % 2][:], chunk(x_sb[s], cc),
                                        pm[:], ALU.subtract)
                        i = v.tensor_tensor(tn1[s % 2][:], tn1[s % 2][:],
                                            ps_[:], ALU.mult)
                        C.inc(i, "V", sems, 1, label=f"a1:sl:{cc}:{s}")
                if 1 <= c <= NCH:
                    cc = c - 1
                    C.wait(v, f"a1:sg:{cc}", sems)
                    if cc >= 1:
                        C.wait(v, f"a1:sr:{cc - 1}", sems)
                    i = v.reciprocal(dvt, dvin)
                    nv = C.inc(i, "V", sems, 1)
                    v.wait_ge(sems["V"], nv)
                    i = v.tensor_scalar(dvout, dvt, 1.0, None, ALU.mult)
                    C.inc(i, "V", sems, 1, label=f"a1:rc:{cc}")
                if c == 4:
                    def _ss(i):
                        nv = C.inc(i, "V", sems, 1)
                        v.wait_ge(sems["V"], nv)
                        return nv
                    C.wait(v, "p:wsmm", sems)
                    _ss(v.tensor_copy(wred[0:1, 0:1], PB[7][0:1, 2:3]))
                    _ss(v.tensor_scalar(beta_c[0:1, :], wred[0:1, 0:1],
                                        1.0 / (D * D), None, ALU.mult))
                    _ss(v.tensor_scalar(stg_b, beta_c[0:1, :], 0.5, None,
                                        ALU.mult))
                    _ss(v.tensor_scalar(inv_t[0:1, :], beta_c[0:1, :], 1e-5,
                                        None, ALU.add))
                    _ss(v.reciprocal(mx2[0:1, :], inv_t[0:1, :]))
                    i = v.tensor_scalar(stg_a, mx2[0:1, :], 1.0, None, ALU.mult)
                    nv = C.inc(i, "V", sems, 1, label="p:invrow")
                    v.wait_ge(sems["V"], nv)
                    C.wait(v, "p:invmm", sems)
                    i = v.tensor_copy(cpk[:, 7:9], PB[7][:, 0:2])
                    nv = C.inc(i, "V", sems, 1, label="p:inv")
                    v.wait_ge(sems["V"], nv)
                if 5 <= c <= 8:
                    for k in range(4 * (c - 5), 4 * (c - 4)):
                        C.wait(v, f"p:w2:{k}", sems)
                        v.tensor_scalar(wstage[:], wstage[:], col(inv_c, 0),
                                        None, ALU.mult)
                        half = w1[k // 2][:, (k % 2) * TC:(k % 2 + 1) * TC]
                        v.tensor_scalar(wsb[:], wstage[:], -0.5, None, ALU.is_le)
                        v.tensor_scalar(half, wstage[:], 0.5, None, ALU.is_ge)
                        i = v.tensor_tensor(half, half, wsb[:], ALU.subtract)
                        C.inc(i, "V", sems, 1, label=f"p:wq:{k}")
                if c <= NCH - 1:
                    C.wait(v, f"a1:stat:{c}", sems)
                    if c >= 1:
                        C.wait(v, f"a1:replmu:{c - 1}", sems)
                        C.wait(v, f"a1:mst:{c - 1}", sems)
                    v.tensor_scalar(mrow, PB[0][0:1, :], INV_D, None, ALU.mult)
                    i = v.tensor_scalar(vrow, PB[1][0:1, :], FFN_A, FFN_B,
                                        ALU.mult, ALU.add)
                    nv = C.inc(i, "V", sems, 1)
                    v.wait_ge(sems["V"], nv)
                    i = v.scalar_tensor_tensor(qrow, mrow, FFN_M, mrow,
                                               ALU.mult, ALU.mult)
                    nv = C.inc(i, "V", sems, 1)
                    v.wait_ge(sems["V"], nv)
                    i = v.tensor_tensor(vrow, vrow, qrow, ALU.subtract)
                    C.inc(i, "V", sems, 1, label=f"a1:math1:{c}")
            v.wait_ge(sems["V"], C.cnt["V"])
            C.wait(v, f"a1:red:{NCH - 1}:{NS - 1}", sems)
            i = v.tensor_reduce(mx1[0:1, :], gsl1[0:1, 0:NCH * NS], AXX,
                                ALU.max)
            nv = C.inc(i, "V", sems, 1, label="a1:slots")
            v.wait_ge(sems["V"], nv)
            i = v.tensor_scalar(stg_a, mx1[0:1, :], THR_C, None, ALU.mult)
            nv = C.inc(i, "V", sems, 1)
            v.wait_ge(sems["V"], nv)
            i = v.tensor_scalar(stg_b, stg_a, -1.0, None, ALU.mult)
            nv = C.inc(i, "V", sems, 1, label="thr1v")
            v.wait_ge(sems["V"], nv)

        @blk.scalar
        def _(sc):
            def _casts(c):
                C.wait(sc, f"lxf:{c}", sems)
                for s in range(NS):
                    if c >= 2:
                        C.wait(sc, f"a1:statmm:{c - 2}:{s}", sems)
                    sc.copy(bufX[c % 2][s][:], chunk(x_sb[s], c))
                    if c >= 1:
                        C.wait(sc, f"a1:statmm:{c - 1}:{s}", sems)
                    i = sc.activation(sqv[s][:], bufX[c % 2][s][:],
                                      ACT.Square)
                    C.inc(i, "S", sems, 1, label=f"a1:cast:{c}:{s}")

            def _dance_back(cc):
                C.wait(sc, f"a1:rc:{cc}", sems)
                if cc >= 1:
                    C.wait(sc, f"a1:repls:{cc - 1}", sems)
                    C.wait(sc, f"a1:sst:{cc - 1}", sems)
                sc.dma_start(out=srow_s, in_=dvout,
                             allow_slow_non_contiguous=True
                             ).then_inc(sems["RS"], 16)
                rs = C.inc(None, "RS", sems, 16)
                sc.wait_ge(sems["RS"], rs)
                i = sc.nop()
                C.inc(i, "S", sems, 1, label=f"a1:sr:{cc}")

            _casts(0)
            C.wait(sc, "vecf", sems)
            i = sc.activation(a_cols[:], vc["la"][:], ACT.Exp)
            C.inc(i, "S", sems, 1, label="p:exp")
            for c in range(NCH + 2):
                if c >= NCH:
                    cc = c - 2
                    for s in range(NS):
                        C.wait(sc, f"a1:sl:{cc}:{s}", sems)
                        i = sc.activation(tn1[s % 2][:], tn1[s % 2][:],
                                          ACT.Abs,
                                          bias=col(vc["bet"], s),
                                          scale=col(vc["gam"], s))
                        C.inc(i, "S", sems, 1, label=f"a1:ab:{cc}:{s}")
                    if c - 1 <= NCH - 1:
                        _dance_back(c - 1)
                    continue
                if c + 1 <= NCH - 1:
                    _casts(c + 1)
                if c >= 2:
                    cc = c - 2
                    for s in range(NS):
                        C.wait(sc, f"a1:sl:{cc}:{s}", sems)
                        i = sc.activation(tn1[s % 2][:], tn1[s % 2][:],
                                          ACT.Abs,
                                          bias=col(vc["bet"], s),
                                          scale=col(vc["gam"], s))
                        C.inc(i, "S", sems, 1, label=f"a1:ab:{cc}:{s}")
                if c >= 1:
                    _dance_back(c - 1)
                C.wait(sc, f"a1:math1:{c}", sems)
                i = sc.activation(vrow, vrow, ACT.Sqrt)
                ns = C.inc(i, "S", sems, 1)
                sc.wait_ge(sems["S"], ns)
                sc.dma_start(out=dvin, in_=vrow,
                             allow_slow_non_contiguous=True
                             ).then_inc(sems["RS"], 16)
                rs = C.inc(None, "RS", sems, 16)
                sc.wait_ge(sems["RS"], rs)
                i = sc.nop()
                C.inc(i, "S", sems, 1, label=f"a1:sg:{c}")

        @blk.tensor
        def _(t):
            C.wait(t, "p:ones", sems)
            C.wait(t, "p:ones1", sems)
            for c in range(NCH + 2):
                if c >= 2 and c - 2 <= NCH - 1:
                    cc = c - 2
                    C.wait(t, f"a1:sr:{cc}", sems)
                    if cc >= 2:
                        C.wait(t, f"a1:sl:{cc - 2}:{NS - 1}", sems)
                    i = t.matmul(PB[3 + (cc % 2) * 2][:], ones1, srow_s,
                                 start=True, stop=True)
                    C.inc(i, "T", sems, 1, label=f"a1:repls:{cc}")
                if c <= NCH - 1:
                    for s in range(NS):
                        C.wait(t, f"a1:cast:{c}:{s}", sems)
                        if c >= 1 and s == 0:
                            C.wait(t, f"a1:math1:{c - 1}", sems)
                        t.matmul(PB[0][:], ones_bf[:], bufX[c % 2][s][:],
                                 start=(s == 0), stop=(s == NS - 1))
                        i = t.matmul(PB[1][:], ones_bf[:], sqv[s][:],
                                     start=(s == 0), stop=(s == NS - 1))
                        C.inc(i, "T", sems, 1, label=f"a1:statmm:{c}:{s}")
                    C.mark("T", f"a1:stat:{c}")
                if c == 3:
                    C.wait(t, f"p:wr1:{2 * NS - 1}", sems)
                    i = t.matmul(PB[7][0:1, 2:3], wsum, cpk[:, 15:16],
                                 start=True, stop=True)
                    C.inc(i, "T", sems, 1, label="p:wsmm")
                if c == 4:
                    C.wait(t, "p:invrow", sems)
                    i = t.matmul(PB[7][:, 0:2], ones1, cpk[0:1, 13:15],
                                 start=True, stop=True)
                    C.inc(i, "T", sems, 1, label="p:invmm")
                if c >= 1 and c - 1 <= NCH - 1:
                    cc = c - 1
                    C.wait(t, f"a1:math1:{cc}", sems)
                    if cc >= 2:
                        C.wait(t, f"a1:sl:{cc - 2}:{NS - 1}", sems)
                    i = t.matmul(PB[2 + (cc % 2) * 2][:], ones1, mrow,
                                 start=True, stop=True)
                    C.inc(i, "T", sems, 1, label=f"a1:replmu:{cc}")

        for e in ("V", "S", "T", "G"):
            C.mark(e, f"end:A1:{e}")

        def barrier(eng, sweep):
            for e in ("V", "S", "T", "G"):
                C.wait(eng, f"end:{sweep}:{e}", sems)

        blk.__exit__(None, None, None)
        blk2 = nc.Block(); blk2.__enter__()
        mrowB = scr3[0:1, :]
        vrowB = vrow
        qrowB = qrow

        @blk2.sync
        def _(sy):
            def _b1_mst(cc):
                C.wait(sy, f"b1:math1:{cc}", sems)
                sy.dma_start(out=cmp_mu[:, 4 * cc:4 * cc + 4],
                             in_=mrowB).then_inc(sems["RM"], 16)
                C.inc(None, "RM", sems, 16, label=f"b1:mst:{cc}")

            for c in range(NCH):
                if c == 0:
                    C.wait(sy, f"a1:sst:{NCH - 1}", sems)
                    C.wait(sy, f"a1:mst:{NCH - 1}", sems)
                else:
                    C.wait(sy, f"b1:repl:{c - 1}", sems)
                sy.dma_start(out=srow[:], in_=cmp_s["x"][:, 4 * c:4 * c + 4]
                             ).then_inc(sems["RQ"], 16)
                C.inc(None, "RQ", sems, 16)
                sy.dma_start(out=m2row[:], in_=cmp_m2["x"][:, 4 * c:4 * c + 4]
                             ).then_inc(sems["RQ"], 16)
                C.inc(None, "RQ", sems, 16, label=f"b1:rows:{c}")
                if c >= 3:
                    _b1_mst(c - 3)
            for cc in range(NCH - 3, NCH):
                _b1_mst(cc)

        @blk2.tensor
        def _(t):
            barrier(t, "A1")
            C.wait(t, "thr1v", sems)
            i = t.matmul(PB[6][:, 0:2], ones1, cpk[0:1, 13:15],
                         start=True, stop=True)
            C.inc(i, "T", sems, 1, label="thr1mm")
            for c in range(NCH + 2):
                if c <= NCH - 1:
                    C.wait(t, f"b1:rows:{c}", sems)
                    if c >= 2:
                        C.wait(t, f"b1:xq:{c - 2}", sems)
                    r1, r2 = PB[2 + (c % 2) * 2], PB[3 + (c % 2) * 2]
                    t.matmul(r1[:], ones1, srow[:], start=True, stop=True)
                    i = t.matmul(r2[:], ones1, m2row[:], start=True, stop=True)
                    C.inc(i, "T", sems, 1, label=f"b1:repl:{c}")
                if c >= 2:
                    cc = c - 2
                    for s in range(NS):
                        C.wait(t, f"b1:cast1:{cc}:{s}", sems)
                        if cc >= 1 and s == 0:
                            C.wait(t, f"b1:math1:{cc - 1}", sems)
                        t.matmul(PB[0][:], ones_bf[:], bufX[cc % 2][s][:],
                                 start=(s == 0), stop=(s == NS - 1))
                        i = t.matmul(PB[1][:], ones_bf[:], sqv[s][:],
                                     start=(s == 0), stop=(s == NS - 1))
                        C.inc(i, "T", sems, 1, label=f"b1:statmm:{cc}:{s}")
                    C.mark("T", f"b1:stat:{cc}")
                if 1 <= c <= NCH:
                    cc = c - 1
                    C.wait(t, f"b1:xq:{cc}", sems)
                    i = None
                    for e in range(NS):
                        k = cc * NS + e
                        if k >= 2:
                            C.wait(t, f"b1:evac:{(k - 2) // NS}:{(k - 2) % NS}",
                                   sems)
                        for s in range(NS):
                            i = t.matmul(PB[6 + e % 2][:],
                                         w1[s][:, e * 128:(e + 1) * 128],
                                         bufX[cc % 2][s][:], start=(s == 0),
                                         stop=(s == NS - 1))
                        C.inc(i, "T", sems, 1, label=f"b1:mm:{cc}:{e}")
            C.mark("T", "b1:mmall")

        @blk2.vector
        def _(v):
            barrier(v, "A1")
            C.wait(v, "thr1mm", sems)
            i = v.tensor_copy(cpk[:, 9:11], PB[6][:, 0:2])
            nv = C.inc(i, "V", sems, 1)
            v.wait_ge(sems["V"], nv)
            for c in range(NCH + 3):
                if c <= NCH:
                    cc = c - 1
                    if c <= NCH - 1:
                        C.wait(v, f"b1:repl:{c}", sems)
                        if c >= 2:
                            C.wait(v, f"b1:statmm:{c - 2}:{NS - 1}", sems)
                            if DEBUG and c - 2 == 4:
                                C.wait(v, f"b1:dbgbx:4", sems)
                    r1, r2 = PB[2 + (c % 2) * 2], PB[3 + (c % 2) * 2]
                    i = None
                    for s in range(NS):
                        if c <= NCH - 1:
                            v.tensor_tensor(b_sb[:], chunk(x_sb[s], c), r2[:],
                                            ALU.subtract)
                            v.tensor_tensor(s_rep[:], b_sb[:], r1[:], ALU.mult)
                            v.tensor_scalar(b_sb[:], s_rep[:],
                                            col(vc["gam"], s),
                                            col(vc["bet"], s),
                                            ALU.mult, ALU.add)
                            i = v.tensor_scalar(bufX[c % 2][s][:], b_sb[:],
                                                col(thr1, 0), col(nthr1, 0),
                                                ALU.min, ALU.max)
                            if s == NS - 1:
                                C.inc(i, "V", sems, 1, label=f"b1:xq:{c}")
                        if 1 <= c:
                            C.wait(v, f"b1:mm:{cc}:{s}", sems)
                            i = v.scalar_tensor_tensor(chunk(x_sb[s], cc),
                                                       PB[6 + s % 2][:],
                                                       col(hbeta_c, 0),
                                                       chunk(x_sb[s], cc),
                                                       ALU.mult, ALU.add)
                            C.inc(i, "V", sems, 1, label=f"b1:evac:{cc}:{s}")
                if 3 <= c <= NCH + 2:
                    cc = c - 3
                    C.wait(v, f"b1:sg:{cc}", sems)
                    if cc >= 1:
                        C.wait(v, f"b1:sr:{cc - 1}", sems)
                    i = v.reciprocal(dvt, dvin)
                    nv = C.inc(i, "V", sems, 1)
                    v.wait_ge(sems["V"], nv)
                    i = v.tensor_scalar(dvout, dvt, 1.0, None, ALU.mult)
                    C.inc(i, "V", sems, 1, label=f"b1:rc:{cc}")
                if 2 <= c <= NCH + 1:
                    cc = c - 2
                    C.wait(v, f"b1:stat:{cc}", sems)
                    if cc >= 1:
                        C.wait(v, f"b1:mst:{cc - 1}", sems)
                    v.tensor_scalar(mrowB, PB[0][0:1, :], INV_D, None, ALU.mult)
                    i = v.tensor_scalar(vrowB, PB[1][0:1, :], SSM_A, SSM_B,
                                        ALU.mult, ALU.add)
                    nv = C.inc(i, "V", sems, 1)
                    v.wait_ge(sems["V"], nv)
                    i = v.scalar_tensor_tensor(qrowB, mrowB, SSM_M, mrowB,
                                               ALU.mult, ALU.mult)
                    nv = C.inc(i, "V", sems, 1)
                    v.wait_ge(sems["V"], nv)
                    i = v.tensor_tensor(vrowB, vrowB, qrowB, ALU.subtract)
                    C.inc(i, "V", sems, 1, label=f"b1:math1:{cc}")
                    if DEBUG:
                        if cc >= 1:
                            C.wait(v, f"b1:dbgst:{cc - 1}", sems)
                        i = v.tensor_copy(qrowB, PB[0][0:1, :])
                        C.inc(i, "V", sems, 1, label=f"b1:dbgq:{cc}")
            C.wait(v, f"b1:sr:{NCH - 1}", sems)
            C.wait(v, f"b1:mst:{NCH - 1}", sems)
            i = v.tensor_tensor(cmp_m2["x1"][:], cmp_mu[:], cmp_s["x1"][:],
                                ALU.mult)
            nv = C.inc(i, "V", sems, 1, label="b1:m2c")
            v.wait_ge(sems["V"], nv)

        @blk2.scalar
        def _(sc):
            barrier(sc, "A1")
            for c in range(1, NCH + 4):
                if 1 <= c <= NCH:
                    cc = c - 1
                    for s in range(NS):
                        C.wait(sc, f"b1:evac:{cc}:{s}", sems)
                        C.wait(sc, f"b1:mm:{cc}:{NS - 1}", sems)
                        sc.copy(bufX[cc % 2][s][:], chunk(x_sb[s], cc))
                        if cc >= 1:
                            C.wait(sc, f"b1:statmm:{cc - 1}:{s}", sems)
                        i = sc.activation(sqv[s][:], bufX[cc % 2][s][:],
                                          ACT.Square)
                        C.inc(i, "S", sems, 1, label=f"b1:cast1:{cc}:{s}")
                if 3 <= c <= NCH + 2:
                    cc = c - 3
                    C.wait(sc, f"b1:rc:{cc}", sems)
                    sc.dma_start(out=cmp_s["x1"][:, 4 * cc:4 * cc + 4],
                                 in_=dvout, allow_slow_non_contiguous=True
                                 ).then_inc(sems["RS"], 16)
                    rs = C.inc(None, "RS", sems, 16)
                    sc.wait_ge(sems["RS"], rs)
                    i = sc.nop()
                    C.inc(i, "S", sems, 1, label=f"b1:sr:{cc}")
                if 2 <= c <= NCH + 1:
                    cc = c - 2
                    C.wait(sc, f"b1:math1:{cc}", sems)
                    i = sc.activation(vrowB, vrowB, ACT.Sqrt)
                    ns = C.inc(i, "S", sems, 1)
                    sc.wait_ge(sems["S"], ns)
                    sc.dma_start(out=dvin, in_=vrowB,
                                 allow_slow_non_contiguous=True
                                 ).then_inc(sems["RS"], 16)
                    rs = C.inc(None, "RS", sems, 16)
                    sc.wait_ge(sems["RS"], rs)
                    i = sc.nop()
                    C.inc(i, "S", sems, 1, label=f"b1:sg:{cc}")

        @blk2.gpsimd
        def _(gp):
            barrier(gp, "A1")
            if DEBUG:
                C.wait(gp, "b1:stat:4", sems)
                for s in range(NS):
                    gp.dma_start(out=dbga[:, s * TC:(s + 1) * TC],
                                 in_=bufX[0][s][:]).then_inc(sems["GQ"], 16)
                    C.inc(None, "GQ", sems, 16,
                          label=f"b1:dbgbx:4" if s == NS - 1 else None)
            C.wait(gp, "b1:mmall", sems)
            for s in range(NS):
                gp.dma_start(out=w1[s][:], in_=cmat[s * 128:(s + 1) * 128, :]
                             ).then_inc(sems["GQ"], 16)
                C.inc(None, "GQ", sems, 16)
            gp.wait_ge(sems["GQ"], C.cnt["GQ"])
            i = gp.nop()
            C.inc(i, "F", sems, 1, label="lcmatf")

        for e in ("V", "S", "T", "G"):
            C.mark(e, f"end:B1:{e}")

        blk2.__exit__(None, None, None)
        blk3 = nc.Block(); blk3.__enter__()
        mrowS = m2row[0:1, :]
        vrowS = scr2[0:1, :]
        qrowS = wstage[0:1, :]

        @blk3.sync
        def _(sy):
            if DEBUG:
                for e in ("V", "S", "T", "G"):
                    C.wait(sy, f"end:B1:{e}", sems)
                for s in range(NS):
                    sy.dma_start(out=dbg1[s * 128:(s + 1) * 128, :],
                                 in_=x_sb[s][:]).then_inc(sems["RQ"], 16)
                    C.inc(None, "RQ", sems, 16)
            def _ssm_mst(cc):
                C.wait(sy, f"ssm:math1:{cc}", sems)
                sy.dma_start(out=cmp_m2["x2"][:, 4 * cc:4 * cc + 4],
                             in_=mrowS).then_inc(sems["RM"], 16)
                C.inc(None, "RM", sems, 16, label=f"ssm:mst:{cc}")

            for c in range(NCH):
                if c == 0:
                    C.wait(sy, f"b1:sr:{NCH - 1}", sems)
                    C.wait(sy, "b1:m2c", sems)
                else:
                    C.wait(sy, f"ssm:repl:{c - 1}", sems)
                if c >= 3:
                    C.wait(sy, f"ssm:rep3:{c - 3}", sems)
                sy.dma_start(out=srow[:], in_=cmp_s["x1"][:, 4 * c:4 * c + 4]
                             ).then_inc(sems["RQ"], 16)
                C.inc(None, "RQ", sems, 16, label=f"ssm:rows:{c}")
                if c >= 2:
                    _ssm_mst(c - 2)
            _ssm_mst(NCH - 2)
            _ssm_mst(NCH - 1)

        @blk3.gpsimd
        def _(gp):
            barrier(gp, "B1")
            for c in range(NCH + 2):
                if c <= NCH - 1:
                    if c > 0:
                        C.wait(gp, f"ssm:umm:{c - 1}:{NS - 1}", sems)
                        C.wait(gp, f"ssm:ymm:{c - 1}:{NS - 1}", sems)
                    gp.dma_start(out=murow_b[:], in_=cmp_mu[:, 4 * c:4 * c + 4]
                                 ).then_inc(sems["GQ"], 16)
                    C.inc(None, "GQ", sems, 16)
                    gp.dma_start(out=m2row_b[:],
                                 in_=cmp_m2["x1"][:, 4 * c:4 * c + 4]
                                 ).then_inc(sems["GQ"], 16)
                    C.inc(None, "GQ", sems, 16, label=f"ssm:browz:{c}")
                if c >= 2:
                    cc = c - 2
                    for s in range(NS):
                        k = cc * NS + s
                        C.wait(gp, f"ssm:ab:{cc}:{s}", sems)
                        if cc == 0 and s == 0:
                            C.wait(gp, "thr1v", sems)
                        i = gp.tensor_reduce(gsl2[0:1, k:k + 1], tn1[s % 2][:],
                                             mybir.AxisListType.XYZWC, ALU.max)
                        C.inc(i, "G", sems, 1, label=f"ssm:red:{cc}:{s}")
                if c <= NCH - 1:
                    for f in range(NS):
                        C.wait(gp, f"ssm:xy:{c}:{f}", sems)
                        i = gp.tensor_tensor(chunk(x_sb[f], c),
                                             chunk(x_sb[f], c),
                                             (scr1 if f % 2 == 0 else scr3)[:],
                                             ALU.add)
                        C.inc(i, "G", sems, 1, label=f"ssm:xt4:{c}:{f}")
            C.wait(gp, "ssm:ulast", sems)
            gp.wait_ge(sems["GQ"], C.cnt["GQ"])
            for s in range(NS):
                gp.dma_start(out=w2[s][:], in_=wq_scr[s * 128:(s + 1) * 128, :]
                             ).then_inc(sems["GQ"], 16)
                C.inc(None, "GQ", sems, 16)
            gp.wait_ge(sems["GQ"], C.cnt["GQ"])
            i = gp.nop()
            C.inc(i, "F", sems, 1, label="lwq2f")

        @blk3.tensor
        def _(t):
            barrier(t, "B1")
            i = None
            for e in range(NS):
                if e == 0:
                    C.wait(t, "lwinf", sems)
                for s in range(NS):
                    i = t.matmul(PB[2][:, e:e + 1],
                                 w2[s][:, e * 128:(e + 1) * 128],
                                 onescol_b[:], start=(s == 0),
                                 stop=(s == NS - 1))
                C.inc(i, "T", sems, 1, label=f"ssm:ws:{e}")
            for c in range(NCH + 1):
                if c <= NCH - 1:
                    C.wait(t, f"ssm:rows:{c}", sems)
                    if c == 0:
                        C.wait(t, f"ssm:wsc:{NS - 1}", sems)
                    if c >= 2:
                        C.wait(t, f"ssm:srepdone:{c - 2}", sems)
                    i = t.matmul(PB[2 + c % 2][:], ones1, srow[:],
                                 start=True, stop=True)
                    C.inc(i, "T", sems, 1, label=f"ssm:repl:{c}")
                if c >= 1:
                    cc = c - 1
                    for s in range(NS):
                        C.wait(t, f"ssm:cpy:{cc}:{s}", sems)
                        if cc >= 2 and s == 0:
                            C.wait(t, f"ssm:xn2:{cc - 2}:{NS - 1}", sems)
                        i = t.matmul(PB[0][:], ones_bf[:], bufX[cc % 2][s][:],
                                     start=(s == 0), stop=(s == NS - 1))
                        C.inc(i, "T", sems, 1, label=f"ssm:st0:{cc}:{s}")
                    for s in range(NS):
                        C.wait(t, f"ssm:cast2:{cc}:{s}", sems)
                        i = t.matmul(PB[1][:], ones_bf[:], bufX[cc % 2][s][:],
                                     start=(s == 0), stop=(s == NS - 1))
                        C.inc(i, "T", sems, 1, label=f"ssm:statmm:{cc}:{s}")
                    C.mark("T", f"ssm:stat:{cc}")
                if c > NCH - 1:
                    continue
                for s in range(NS):
                    C.wait(t, f"ssm:cast1:{c}:{s}", sems)
                C.wait(t, f"ssm:browz:{c}", sems)
                if c == 0:
                    C.wait(t, "p:vecws", sems)
                for e in range(NS):
                    k = c * NS + e
                    if e == 4 and c >= 2:
                        cc = c - 2
                        C.wait(t, f"ssm:sr:{cc}", sems)
                        C.wait(t, f"ssm:math1:{c - 1}", sems)
                        t.matmul(PB[0][:], ones1, srow[:], start=True,
                                 stop=True)
                        i = t.matmul(PB[1][:], ones1, mrowS, start=True,
                                     stop=True)
                        C.inc(i, "T", sems, 1, label=f"ssm:rep3:{cc}")
                    if k >= 2:
                        C.wait(t, f"ssm:b:{(k - 2) // NS}:{(k - 2) % NS}", sems)
                    for s in range(NS):
                        t.matmul(PB[4 + e % 2][:],
                                 w2[s][:, e * 128:(e + 1) * 128],
                                 bufX[c % 2][s][:], start=(s == 0), stop=False)
                    i = t.matmul(PB[4 + e % 2][:], nwrow[e][:], murow_b[:],
                                 start=False, stop=True)
                    C.inc(i, "T", sems, 1, label=f"ssm:umm:{c}:{e}")
                if c == NCH - 1:
                    C.mark("T", "ssm:ulast")
                for f in range(NS):
                    k = c * NS + f
                    C.wait(t, f"ssm:hhat:{c}:{NS - 1}", sems)
                    if k >= 2:
                        C.wait(t, f"ssm:xy:{(k - 2) // NS}:{(k - 2) % NS}",
                               sems)
                    if c == 0 and f == 0:
                        C.wait(t, "lcmatf", sems)
                    for e in range(NS):
                        t.matmul(PB[6 + f % 2][:],
                                 w1[e][:, f * 128:(f + 1) * 128],
                                 bufH[e][:], start=(e == 0), stop=False)
                    i = t.matmul(PB[6 + f % 2][:], ndrow[f][:], m2row_b[:],
                                 start=False, stop=True)
                    C.inc(i, "T", sems, 1, label=f"ssm:ymm:{c}:{f}")
            for cc in (NCH - 2, NCH - 1):
                C.wait(t, f"ssm:sr:{cc}", sems)
                C.wait(t, f"ssm:math1:{NCH - 1}", sems)
                if cc == NCH - 2:
                    C.wait(t, f"ssm:xn2:{NCH - 3}:{NS - 1}", sems)
                else:
                    C.wait(t, f"ssm:xn2:{NCH - 2}:{NS - 1}", sems)
                t.matmul(PB[0][:], ones1, srow[:], start=True, stop=True)
                i = t.matmul(PB[1][:], ones1, mrowS, start=True, stop=True)
                C.inc(i, "T", sems, 1, label=f"ssm:rep3:{cc}")

        @blk3.vector
        def _(v):
            barrier(v, "B1")
            for e in range(NS):
                C.wait(v, f"ssm:ws:{e}", sems)
                i = v.tensor_scalar(col(ws_cols, e), PB[2][:, e:e + 1], -1.0,
                                    None, ALU.mult)
                C.inc(i, "V", sems, 1, label=f"ssm:wsc:{e}")
            for c in range(NCH + 2):
                if c <= NCH - 1:
                    C.wait(v, f"ssm:repl:{c}", sems)
                    i = v.tensor_copy(s_rep[:], PB[2 + c % 2][:])
                    nv = C.inc(i, "V", sems, 1, label=f"ssm:srepdone:{c}")
                    v.wait_ge(sems["V"], nv)
                if 2 <= c <= NCH + 1:
                    cc = c - 2
                    C.wait(v, f"ssm:sg:{cc}", sems)
                    if cc >= 1:
                        C.wait(v, f"ssm:sr:{cc - 1}", sems)
                    i = v.reciprocal(dvt, dvin)
                    nv = C.inc(i, "V", sems, 1)
                    v.wait_ge(sems["V"], nv)
                    i = v.tensor_scalar(dvout, dvt, 1.0, None, ALU.mult)
                    C.inc(i, "V", sems, 1, label=f"ssm:rc:{cc}")
                if 1 <= c <= NCH:
                    cc = c - 1
                    C.wait(v, f"ssm:stat:{cc}", sems)
                    if cc >= 1:
                        C.wait(v, f"ssm:mst:{cc - 1}", sems)
                    v.tensor_scalar(mrowS, PB[0][0:1, :], INV_D, None, ALU.mult)
                    i = v.tensor_scalar(vrowS, PB[1][0:1, :], FFN_A, FFN_B,
                                        ALU.mult, ALU.add)
                    nv = C.inc(i, "V", sems, 1)
                    v.wait_ge(sems["V"], nv)
                    i = v.scalar_tensor_tensor(qrowS, mrowS, FFN_M, mrowS,
                                               ALU.mult, ALU.mult)
                    nv = C.inc(i, "V", sems, 1)
                    v.wait_ge(sems["V"], nv)
                    i = v.tensor_tensor(vrowS, vrowS, qrowS, ALU.subtract)
                    C.inc(i, "V", sems, 1, label=f"ssm:math1:{cc}")
                if c <= NCH - 1:
                    for e in range(NS):
                        C.wait(v, f"ssm:umm:{c}:{e}", sems)
                        if e >= 1:
                            C.wait(v, f"ssm:hcast:{c}:{e - 1}", sems)
                        elif c >= 1:
                            C.wait(v, f"ssm:hcast:{c - 1}:{NS - 1}", sems)
                        v.scalar_tensor_tensor(b_sb[:], PB[4 + e % 2][:],
                                               col(vc["bvec"], e), s_rep[:],
                                               ALU.mult, ALU.mult)
                        i = v.tensor_scalar(b_sb[:], b_sb[:], col(bb_cols, e),
                                            None, ALU.add)
                        C.inc(i, "V", sems, 1, label=f"ssm:b:{c}:{e}")
                        a_bc = bass.AP(vct, NS + e, [[6 * NS, 128], [0, TC]])
                        v.tensor_tensor_scan(h_sb[:], a_bc, b_sb[:],
                                             col(carry, e), ALU.mult, ALU.add)
                        i = v.tensor_copy(col(carry, e), h_sb[:, TC - 1:TC])
                        nv = C.inc(i, "V", sems, 1, label=f"ssm:scan:{c}:{e}")
                        v.wait_ge(sems["V"], nv)
                if 2 <= c <= NCH + 1:
                    cc = c - 2
                    C.wait(v, f"ssm:rep3:{cc}", sems)
                    for s in range(NS):
                        k = cc * NS + s
                        if k >= 2:
                            C.wait(v, f"ssm:red:{(k - 2) // NS}:{(k - 2) % NS}",
                                   sems)
                        C.wait(v, f"ssm:xt4:{cc}:{s}", sems)
                        v.tensor_tensor(tn1[s % 2][:], chunk(x_sb[s], cc),
                                        PB[1][:], ALU.subtract)
                        i = v.tensor_tensor(tn1[s % 2][:], tn1[s % 2][:],
                                            PB[0][:], ALU.mult)
                        C.inc(i, "V", sems, 1, label=f"ssm:xn2:{cc}:{s}")
                if c <= NCH - 1:
                    for f in range(NS):
                        C.wait(v, f"ssm:ymm:{c}:{f}", sems)
                        if f >= 2:
                            C.wait(v, f"ssm:xt4:{c}:{f - 2}", sems)
                        elif c >= 1:
                            C.wait(v, f"ssm:xt4:{c - 1}:{f + NS - 2}", sems)
                        t4buf = scr1 if f % 2 == 0 else scr3
                        v.scalar_tensor_tensor(t4buf[:], chunk(x_sb[f], c),
                                               col(vc["dd"], f), s_rep[:],
                                               ALU.mult, ALU.mult)
                        i = v.tensor_tensor(chunk(x_sb[f], c),
                                            chunk(x_sb[f], c),
                                            PB[6 + f % 2][:], ALU.add)
                        C.inc(i, "V", sems, 1, label=f"ssm:xy:{c}:{f}")
            C.wait(v, f"ssm:red:{NCH - 1}:{NS - 1}", sems)
            i = v.tensor_reduce(mx1[0:1, :], gsl1[0:1, 0:64], AXX, ALU.max)
            nv = C.inc(i, "V", sems, 1)
            v.wait_ge(sems["V"], nv)
            i = v.tensor_scalar(stg_a, mx1[0:1, :], THR_C, None, ALU.mult)
            nv = C.inc(i, "V", sems, 1)
            v.wait_ge(sems["V"], nv)
            i = v.tensor_scalar(stg_b, stg_a, -1.0, None, ALU.mult)
            nv = C.inc(i, "V", sems, 1, label="thr2v")
            v.wait_ge(sems["V"], nv)

        @blk3.scalar
        def _(sc):
            barrier(sc, "B1")
            C.wait(sc, "p:vecd", sems)
            for e in range(NS):
                C.wait(sc, f"ssm:wsc:{e}", sems)
                sc.dma_start(out=nwrow[e][:], in_=col(ws_cols, e),
                             allow_slow_non_contiguous=True
                             ).then_inc(sems["RS"], 16)
                C.inc(None, "RS", sems, 16)
                sc.dma_start(out=ndrow[e][:], in_=col(nd_colsb, e),
                             allow_slow_non_contiguous=True
                             ).then_inc(sems["RS"], 16)
                C.inc(None, "RS", sems, 16, label=f"ssm:rowdma:{e}")
            sc.wait_ge(sems["RS"], C.cnt["RS"])
            i = sc.nop()
            C.inc(i, "S", sems, 1, label="p:vecws")
            for s in range(NS):
                i = sc.copy(bufX[0][s][:], chunk(x_sb[s], 0))
                C.inc(i, "S", sems, 1, label=f"ssm:cast1:0:{s}")
            for c in range(NCH + 2):
                if 1 <= c <= NCH:
                    cc = c - 1
                    for s in range(NS):
                        C.wait(sc, f"ssm:st0:{cc}:{s}", sems)
                        i = sc.activation(bufX[cc % 2][s][:],
                                          bufX[cc % 2][s][:], ACT.Square)
                        C.inc(i, "S", sems, 1, label=f"ssm:cast2:{cc}:{s}")
                if c + 1 <= NCH - 1:
                    for s in range(NS):
                        if c + 1 >= 2:
                            C.wait(sc, f"ssm:statmm:{c - 1}:{s}", sems)
                        i = sc.copy(bufX[(c + 1) % 2][s][:],
                                    chunk(x_sb[s], c + 1))
                        C.inc(i, "S", sems, 1, label=f"ssm:cast1:{c + 1}:{s}")
                if 2 <= c <= NCH + 1:
                    cc = c - 2
                    C.wait(sc, f"ssm:rc:{cc}", sems)
                    if cc + 2 <= NCH - 1:
                        C.wait(sc, f"ssm:repl:{cc + 2}", sems)
                    elif cc >= 1:
                        C.wait(sc, f"ssm:rep3:{cc - 1}", sems)
                    sc.dma_start(out=srow[:], in_=dvout,
                                 allow_slow_non_contiguous=True
                                 ).then_inc(sems["RS"], 16)
                    rs = C.inc(None, "RS", sems, 16)
                    sc.wait_ge(sems["RS"], rs)
                    sc.dma_start(out=cmp_s["x2"][:, 4 * cc:4 * cc + 4],
                                 in_=dvout, allow_slow_non_contiguous=True
                                 ).then_inc(sems["RS"], 16)
                    rs = C.inc(None, "RS", sems, 16)
                    sc.wait_ge(sems["RS"], rs)
                    i = sc.nop()
                    C.inc(i, "S", sems, 1, label=f"ssm:sr:{cc}")
                if 1 <= c <= NCH:
                    cc = c - 1
                    C.wait(sc, f"ssm:math1:{cc}", sems)
                    i = sc.activation(vrowS, vrowS, ACT.Sqrt)
                    ns = C.inc(i, "S", sems, 1)
                    sc.wait_ge(sems["S"], ns)
                    sc.dma_start(out=dvin, in_=vrowS,
                                 allow_slow_non_contiguous=True
                                 ).then_inc(sems["RS"], 16)
                    rs = C.inc(None, "RS", sems, 16)
                    sc.wait_ge(sems["RS"], rs)
                    i = sc.nop()
                    C.inc(i, "S", sems, 1, label=f"ssm:sg:{cc}")
                if c <= NCH - 1:
                    for e in range(NS):
                        C.wait(sc, f"ssm:scan:{c}:{e}", sems)
                        if c >= 1 and e == 0:
                            C.wait(sc, f"ssm:ymm:{c - 1}:{NS - 1}", sems)
                        i = sc.copy(bufH[e][:], h_sb[:])
                        C.inc(i, "S", sems, 1, label=f"ssm:hhat:{c}:{e}")
                        C.mark("S", f"ssm:hcast:{c}:{e}")
                if 2 <= c <= NCH + 1:
                    cc = c - 2
                    for s in range(NS):
                        C.wait(sc, f"ssm:xn2:{cc}:{s}", sems)
                        i = sc.activation(tn1[s % 2][:], tn1[s % 2][:], ACT.Abs,
                                          bias=col(vc["bet"], s),
                                          scale=col(vc["gam"], s))
                        C.inc(i, "S", sems, 1, label=f"ssm:ab:{cc}:{s}")
                if c <= NCH - 1:
                    for s in range(NS):
                        C.wait(sc, f"ssm:xt4:{c}:{s}", sems)
                        C.wait(sc, f"ssm:umm:{c}:{NS - 1}", sems)
                        i = sc.copy(bufX[c % 2][s][:], chunk(x_sb[s], c))
                        C.inc(i, "S", sems, 1, label=f"ssm:cpy:{c}:{s}")

        for e in ("V", "S", "T", "G"):
            C.mark(e, f"end:SSM:{e}")

        blk3.__exit__(None, None, None)
        blk4 = nc.Block(); blk4.__enter__()

        @blk4.sync
        def _(sy):
            if DEBUG:
                for e in ("V", "S", "T", "G"):
                    C.wait(sy, f"end:SSM:{e}", sems)
                for s in range(NS):
                    sy.dma_start(out=dbg2[s * 128:(s + 1) * 128, :],
                                 in_=x_sb[s][:]).then_inc(sems["RQ"], 16)
                    C.inc(None, "RQ", sems, 16)
            def _b2_store(c):
                for e in range(NS):
                    C.wait(sy, f"b2:evac:{c}:{e}", sems)
                    sy.dma_start(out=outT[e * 128:(e + 1) * 128,
                                          c * TC:(c + 1) * TC],
                                 in_=chunk(x_sb[e], c)).then_inc(sems["ST"], 16)
                    C.inc(None, "ST", sems, 16, label=f"st:{c}:{e}")

            for c in range(NCH):
                if c == 0:
                    C.wait(sy, f"ssm:sr:{NCH - 1}", sems)
                    C.wait(sy, f"ssm:mst:{NCH - 1}", sems)
                    C.wait(sy, f"ssm:rep3:{NCH - 1}", sems)
                else:
                    C.wait(sy, f"b2:repl:{c - 1}", sems)
                sy.dma_start(out=srow[:], in_=cmp_s["x2"][:, 4 * c:4 * c + 4]
                             ).then_inc(sems["RQ"], 16)
                C.inc(None, "RQ", sems, 16)
                sy.dma_start(out=m2row[:], in_=cmp_m2["x2"][:, 4 * c:4 * c + 4]
                             ).then_inc(sems["RQ"], 16)
                C.inc(None, "RQ", sems, 16, label=f"b2:rows:{c}")
                if c >= 1:
                    _b2_store(c - 1)
            _b2_store(NCH - 1)

        def _b2_big(t, c):
            C.wait(t, f"b2:xq:{c}", sems)
            i = None
            for e in range(NS):
                k = c * NS + e
                if k >= 2:
                    C.wait(t, f"b2:evac:{(k - 2) // NS}:{(k - 2) % NS}", sems)
                for s in range(NS):
                    i = t.matmul(PB[6 + e % 2][:], w2[s][:, e * 128:(e + 1) * 128],
                                 bufX[c % 2][s][:], start=(s == 0),
                                 stop=(s == NS - 1))
                C.inc(i, "T", sems, 1, label=f"b2:mm:{c}:{e}")

        @blk4.tensor
        def _(t):
            barrier(t, "SSM")
            C.wait(t, "thr2v", sems)
            i = t.matmul(PB[6][:, 0:2], ones1, cpk[0:1, 13:15],
                         start=True, stop=True)
            C.inc(i, "T", sems, 1, label="thr2mm")
            C.wait(t, "lwq2f", sems)
            for c in range(NCH):
                C.wait(t, f"b2:rows:{c}", sems)
                if c >= 2:
                    C.wait(t, f"b2:xq:{c - 2}", sems)
                r1, r2 = PB[2 + (c % 2) * 2], PB[3 + (c % 2) * 2]
                t.matmul(r1[:], ones1, srow[:], start=True, stop=True)
                i = t.matmul(r2[:], ones1, m2row[:], start=True, stop=True)
                C.inc(i, "T", sems, 1, label=f"b2:repl:{c}")
                C.mark("T", f"b2:replq:{c}")
                if c >= 1:
                    _b2_big(t, c - 1)
            _b2_big(t, NCH - 1)

        @blk4.vector
        def _(v):
            barrier(v, "SSM")
            C.wait(v, "thr2mm", sems)
            i = v.tensor_copy(cpk[:, 11:13], PB[6][:, 0:2])
            nv = C.inc(i, "V", sems, 1)
            v.wait_ge(sems["V"], nv)
            for c in range(NCH + 1):
                cc = c - 1
                if c <= NCH - 1:
                    C.wait(v, f"b2:repl:{c}", sems)
                    if c >= 2:
                        C.wait(v, f"b2:mm:{c - 2}:{NS - 1}", sems)
                r1, r2 = PB[2 + (c % 2) * 2], PB[3 + (c % 2) * 2]
                i = None
                for s in range(NS):
                    if c <= NCH - 1:
                        v.tensor_tensor(b_sb[:], chunk(x_sb[s], c), r2[:],
                                        ALU.subtract)
                        v.tensor_tensor(s_rep[:], b_sb[:], r1[:], ALU.mult)
                        v.tensor_scalar(b_sb[:], s_rep[:], col(vc["gam"], s),
                                        col(vc["bet"], s), ALU.mult, ALU.add)
                        i = v.tensor_scalar(bufX[c % 2][s][:], b_sb[:],
                                            col(thr2, 0), col(nthr2, 0),
                                            ALU.min, ALU.max)
                        if s == NS - 1:
                            C.inc(i, "V", sems, 1, label=f"b2:xq:{c}")
                    if c >= 1:
                        C.wait(v, f"b2:mm:{cc}:{s}", sems)
                        i = v.scalar_tensor_tensor(chunk(x_sb[s], cc),
                                                   PB[6 + s % 2][:],
                                                   col(hbeta_c, 0),
                                                   chunk(x_sb[s], cc),
                                                   ALU.mult, ALU.add)
                        C.inc(i, "V", sems, 1, label=f"b2:evac:{cc}:{s}")

        @blk4.gpsimd
        def _(gp):
            barrier(gp, "SSM")
            gp.wait_ge(sems["ST"], 16 * NCH * NS)
            if DEBUG:
                w = NCH * 4
                for j, tt in enumerate([cmp_s["x"], cmp_m2["x"], cmp_s["x1"],
                                        cmp_m2["x1"], cmp_mu, cmp_s["x2"],
                                        cmp_m2["x2"]]):
                    gp.dma_start(out=dbgs[:, j * w:(j + 1) * w], in_=tt[:]
                                 ).then_inc(sems["GQ"], 16)
                    C.inc(None, "GQ", sems, 16)
                gp.dma_start(out=dbgs[:, 7 * w:7 * w + 16], in_=cpk[:]
                             ).then_inc(sems["GQ"], 16)
                C.inc(None, "GQ", sems, 16)
                gp.dma_start(out=dbgs[:, 7 * w + 16:7 * w + 16 + NS],
                             in_=carry[:]).then_inc(sems["GQ"], 16)
                C.inc(None, "GQ", sems, 16)
                gp.wait_ge(sems["GQ"], C.cnt["GQ"])

        blk4.__exit__(None, None, None)

    return C.out


def _build():
    nc1 = bass.Bass("TRN2", target_bir_lowering=False, detect_race_conditions=False)
    labels = _emit(nc1, {}, strict=False)
    nc = bass.Bass("TRN2", target_bir_lowering=False, detect_race_conditions=False)
    labels2 = _emit(nc, labels, strict=True)
    return nc


_BUILT = {}


def kernel(L0_attn_x, L0_attn_W_in, L0_attn_b_in, L0_attn_log_A, L0_attn_B_vec,
           L0_attn_C, L0_attn_D, L0_ffn_weight, L0_ffn_ln_gamma, L0_ffn_ln_beta):
    if "nc" not in _BUILT:
        _BUILT["nc"] = _build()
    nc = _BUILT["nc"]
    vecs = np.stack([
        np.asarray(L0_attn_b_in), np.asarray(L0_attn_log_A),
        np.asarray(L0_attn_B_vec), np.asarray(L0_attn_D),
        np.asarray(L0_ffn_ln_gamma), np.asarray(L0_ffn_ln_beta)
    ]).astype(np.float32, copy=False)
    vecs = np.ascontiguousarray(
        vecs.reshape(6, 8, 128).transpose(2, 0, 1).reshape(128, 48))
    wffn = np.ascontiguousarray(np.asarray(L0_ffn_weight, dtype=np.float32))
    win = np.ascontiguousarray(np.asarray(L0_attn_W_in, dtype=np.float32))
    cmat = np.ascontiguousarray(np.asarray(L0_attn_C, dtype=np.float32))
    x = np.asarray(L0_attn_x)
    in_maps = [{
        "xT": np.ascontiguousarray(x[i].T),
        "wffn": wffn, "win": win, "cmat": cmat, "vecs": vecs,
    } for i in range(x.shape[0])]
    res = run_bass_kernel_spmd(nc, in_maps, core_ids=list(range(len(in_maps))))
    return np.stack([np.ascontiguousarray(r["outT"].T) for r in res.results])


# version 18
# speedup vs baseline: 1.3042x; 1.0221x over previous
from contextlib import ExitStack

import numpy as np
import concourse.bass as bass
import concourse.mybir as mybir
from concourse.bass_isa import ReduceOp
from concourse.bass_utils import run_bass_kernel_spmd

F32 = mybir.dt.float32
BF16 = mybir.dt.bfloat16
ALU = mybir.AluOpType
ACT = mybir.ActivationFunctionType
AXX = mybir.AxisListType.X

D = 1024
NS = 8
TC = 512
T = 4096
NCH = T // TC
QB = 128.0
THR_C = (QB - 1e-5) / QB
INV_D = 1.0 / D
FFN_A = 1.001 / D
FFN_B = 1e-8
FFN_M = 1.001
SSM_A = 1.0 / D
SSM_B = 1e-5
SSM_M = 1.0


class Ctr:
    def __init__(self, labels, strict):
        self.labels = labels
        self.strict = strict
        self.cnt = {}
        self.out = {}

    def inc(self, instr, semname, sems, n=1, label=None):
        c = self.cnt.get(semname, 0) + n
        self.cnt[semname] = c
        if label is not None:
            self.out[label] = (semname, c)
        if instr is not None and n > 0:
            instr.then_inc(sems[semname], n)
        return c

    def mark(self, semname, label):
        self.out[label] = (semname, self.cnt.get(semname, 0))

    def wait(self, eng, label, sems):
        if label not in self.labels:
            if self.strict:
                raise KeyError(f"unknown label {label!r}")
            return
        semname, val = self.labels[label]
        if val > 0:
            eng.wait_ge(sems[semname], val)


DEBUG = False


def _emit(nc, labels, strict):
    C = Ctr(labels, strict)

    xT_h = nc.dram_tensor("xT", [D, T], F32, kind="ExternalInput")
    wffn_h = nc.dram_tensor("wffn", [D, D], F32, kind="ExternalInput")
    win_h = nc.dram_tensor("win", [D, D], F32, kind="ExternalInput")
    cmat_h = nc.dram_tensor("cmat", [D, D], F32, kind="ExternalInput")
    vecs_h = nc.dram_tensor("vecs", [128, 6 * NS], F32, kind="ExternalInput")
    outT_h = nc.dram_tensor("outT", [D, T], F32, kind="ExternalOutput")
    wqs_h = nc.dram_tensor("wq_scr", [D, D], BF16, kind="Internal")
    xT, wffn, win, cmat = xT_h.ap(), wffn_h.ap(), win_h.ap(), cmat_h.ap()
    outT, wq_scr = outT_h.ap(), wqs_h.ap()
    if DEBUG:
        dbg1 = nc.dram_tensor("dbg1", [D, T], F32, kind="ExternalOutput").ap()
        dbg2 = nc.dram_tensor("dbg2", [D, T], F32, kind="ExternalOutput").ap()
        dbgs = nc.dram_tensor("dbgs", [128, 7 * NCH * 4 + 16 + NS], F32,
                              kind="ExternalOutput").ap()
        dbga = nc.dram_tensor("dbga", [128, 8 * TC], F32, kind="ExternalOutput").ap()
        dbgr = nc.dram_tensor("dbgr", [128, NCH * 4], F32, kind="ExternalOutput").ap()

    with ExitStack() as st:
        def sb(name, shape, dt=F32):
            return st.enter_context(nc.sbuf_tensor(name, shape, dt))

        x_sb = [sb(f"x{s}", [128, T]) for s in range(NS)]
        w1 = [sb(f"w1_{s}", [128, D], BF16) for s in range(NS)]
        w2 = [sb(f"w2_{s}", [128, D], BF16) for s in range(NS)]
        bufX = [[sb(f"bx{p}_{s}", [128, TC], BF16) for s in range(NS)]
                for p in range(2)]
        bufH = [sb(f"bh{s}", [128, TC], BF16) for s in range(NS)]
        sqt_t = sb("sqt", [128, TC], BF16)
        sqt = [sqt_t, sqt_t]
        tn1_t = sb("tn1", [128, TC], BF16)
        tn1_b = sb("tn1b", [128, TC], BF16)
        tn1 = [tn1_t, tn1_b]

        scr1 = sb("scr1", [128, TC]); scr2 = sb("scr2", [128, TC])
        scr3 = sb("scr3", [128, TC])
        b_sb = sb("b_sb", [128, TC]); h_sb = b_sb
        s_rep = sb("s_rep", [128, TC])
        srow = sb("srow", [1, TC]); m2row = sb("m2row", [1, TC])
        mrow = m2row[0:1, :]
        srow_s = srow[0:1, :]
        vrow = scr2[0:1, :]
        qrow = scr1[0:1, :]
        mslots = sb("mslots", [128, NCH * NS], BF16)
        sqv = [sqt_t] + [bufH[i] for i in range(1, NS)]
        p2rb = sb("p2rb", [33, TC], BF16)
        murow_b = p2rb[0:1, :]
        m2row_b = p2rb[32:33, :]
        cmp_s = {k: sb(f"cs_{k}", [128, NCH * 4]) for k in ("x", "x1")}
        cmp_m2 = {k: sb(f"cm_{k}", [128, NCH * 4]) for k in ("x", "x1")}
        cmp_s["x2"] = cmp_s["x"]
        cmp_m2["x2"] = cmp_m2["x"]
        cmp_mu = cmp_m2["x"]
        wstage = sb("wst", [128, TC])
        wsb = bufH[0]
        vct = sb("vct", [128, 6 * NS])
        vc = {k: vct[:, j * NS:(j + 1) * NS] for j, k in
              enumerate(("bin", "la", "bvec", "dd", "gam", "bet"))}
        a_cols = vc["la"]
        bb_cols = vc["bin"]
        nd_colsb = sb("nd_colsb", [128, NS], BF16)
        ws_cols = sb("ws_cols", [128, NS], BF16)
        vcc1 = sb("vcc1", [128, 8]); vcc2 = sb("vcc2", [128, 4])
        dvin = vcc1[:, 0:4]; dvout = vcc1[:, 4:8]; dvt = vcc2[:, 0:4]
        gsl1 = mslots[0:1, 0:NCH * NS]
        gsl2 = gsl1
        p2r = sb("p2r", [33, D], BF16)
        nwrow = [p2r[0:1, e * 128:(e + 1) * 128] for e in range(NS)]
        ndrow = [p2r[32:33, e * 128:(e + 1) * 128] for e in range(NS)]
        ones_bf = sb("ones_bf", [128, 128], BF16)
        ones1_t = sb("ones1", [1, 128])
        ones1 = ones1_t[:]
        onescol_b = ones_bf[:, 0:1]
        carry = sb("carry", [128, NS])
        cpk = sb("cpk", [128, 16])
        (mx1, mx2, gred, _unused_wsum, wred, beta_c, inv_t, inv_c, hbeta_c,
         thr1, nthr1, thr2, nthr2) = [cpk[:, j:j + 1] for j in range(13)]
        wsum = cpk[:, 3:4]
        stg_a = cpk[0:1, 13:14]
        stg_b = cpk[0:1, 14:15]

        PB = [st.enter_context(nc.psum_tensor(f"pb{i}", [128, TC], F32))
              for i in range(8)]

        semnames = ["LX", "LW", "GQ", "RQ", "RS", "ST", "V", "S", "T", "G", "F", "RM"]
        sems = {n: st.enter_context(nc.semaphore(n)) for n in semnames}

        def col(tile, j):
            try:
                return tile[:, j:j + 1]
            except Exception:
                return tile

        def bcast(tile, j, n=TC):
            free = int(tile.shape[1])
            return bass.AP(tile, j, [[free, 128], [0, n]])

        def chunk(tile, c):
            return tile[:, c * TC:(c + 1) * TC]

        blk = nc.Block(); blk.__enter__()

        @blk.sync
        def _(sy):
            sy.dma_start(out=vct[:], in_=vecs_h.ap()[:]).then_inc(sems["RQ"], 16)
            rq = C.inc(None, "RQ", sems, 16)
            sy.wait_ge(sems["RQ"], rq)
            i = sy.nop()
            C.inc(i, "F", sems, 1, label="vecf")
            for c in range(NCH):
                for s in range(NS):
                    sy.dma_start(out=chunk(x_sb[s], c),
                                 in_=xT[s * 128:(s + 1) * 128, c * TC:(c + 1) * TC]
                                 ).then_inc(sems["LX"], 16)
                    C.inc(None, "LX", sems, 16)
                if c >= 1:
                    sy.wait_ge(sems["LX"], 16 * NS * c)
                    i = sy.nop()
                    C.inc(i, "F", sems, 1, label=f"lxf:{c - 1}")
            sy.wait_ge(sems["LX"], 16 * NS * NCH)
            i = sy.nop()
            C.inc(i, "F", sems, 1, label=f"lxf:{NCH - 1}")
            def _wload(k):
                s, hf = k // 2, k % 2
                if k == 0:
                    C.wait(sy, f"p:wr1:{2 * NS - 1}", sems)
                else:
                    C.wait(sy, f"p:wq:{k - 1}", sems)
                sy.dma_start(out=wstage[:],
                             in_=wffn[s * 128:(s + 1) * 128,
                                      hf * TC:(hf + 1) * TC]
                             ).then_inc(sems["LW"], 16)
                C.inc(None, "LW", sems, 16, label=f"p:w2:{k}")

            wl_sched = {4: [0, 1, 2, 3, 4], 5: [5, 6, 7, 8],
                        6: [9, 10, 11, 12], 7: [13, 14, 15]}
            for c in range(NCH):
                C.wait(sy, f"a1:math1:{c}", sems)
                sy.dma_start(out=cmp_m2["x"][:, 4 * c:4 * c + 4],
                             in_=mrow).then_inc(sems["RM"], 16)
                C.inc(None, "RM", sems, 16, label=f"a1:mst:{c}")
                for k in wl_sched.get(c, []):
                    _wload(k)
                C.wait(sy, f"a1:sr:{c}", sems)
                sy.dma_start(out=cmp_s["x"][:, 4 * c:4 * c + 4],
                             in_=srow_s).then_inc(sems["RQ"], 16)
                C.inc(None, "RQ", sems, 16, label=f"a1:sst:{c}")

        @blk.gpsimd
        def _(gp):
            for k in range(2 * NS):
                s, hf = k // 2, k % 2
                if k >= 1:
                    C.wait(gp, f"p:wr1:{k - 1}", sems)
                gp.dma_start(out=wstage[:],
                             in_=wffn[s * 128:(s + 1) * 128,
                                      hf * TC:(hf + 1) * TC]
                             ).then_inc(sems["GQ"], 16)
                C.inc(None, "GQ", sems, 16, label=f"p:w1:{k}")
            for s in range(NS):
                C.wait(gp, f"p:wq:{2 * s + 1}", sems)
                gp.dma_start(out=wq_scr[s * 128:(s + 1) * 128, :],
                             in_=w1[s][:]).then_inc(sems["GQ"], 16)
                C.inc(None, "GQ", sems, 16, label=f"p:wqs:{s}")
            for s in range(NS):
                gp.dma_start(out=w2[s][:], in_=win[s * 128:(s + 1) * 128, :]
                             ).then_inc(sems["GQ"], 16)
                C.inc(None, "GQ", sems, 16)
            gp.wait_ge(sems["GQ"], C.cnt["GQ"])
            i = gp.nop()
            C.inc(i, "F", sems, 1, label="lwinf")
            C.wait(gp, "a1:slots", sems)
            i = gp.tensor_reduce(gsl1[0:1, 0:1], mx1[:],
                                 mybir.AxisListType.XYZWC, ALU.max)
            C.inc(i, "G", sems, 1, label="a1:gmax")
        @blk.vector
        def _(v):
            i = v.memset(carry[:], 0.0)
            C.inc(i, "V", sems, 1)
            i = v.memset(ones_bf[:], 1.0)
            C.inc(i, "V", sems, 1, label="p:ones")
            v.memset(ones1_t[:], 1.0)
            i = v.memset(cpk[:, 15:16], 1.0)
            C.inc(i, "V", sems, 1, label="p:ones1")
            v.wait_ge(sems["V"], C.cnt["V"])

            def _wpass1(k):
                C.wait(v, f"p:w1:{k}", sems)
                if k == 0:
                    i = v.tensor_reduce(wsum[:], wstage[:], AXX, ALU.add,
                                        apply_absolute_value=True)
                    nv = C.inc(i, "V", sems, 1, label=f"p:wr1:{k}")
                    v.wait_ge(sems["V"], nv)
                else:
                    i = v.tensor_reduce(wred[:], wstage[:], AXX, ALU.add,
                                        apply_absolute_value=True)
                    nv = C.inc(i, "V", sems, 1)
                    v.wait_ge(sems["V"], nv)
                    i = v.tensor_tensor(wsum[:], wsum[:], wred[:], ALU.add)
                    nv = C.inc(i, "V", sems, 1, label=f"p:wr1:{k}")
                    v.wait_ge(sems["V"], nv)

            C.wait(v, "p:exp", sems)
            v.tensor_scalar(a_cols[:], a_cols[:], -1.0, None, ALU.mult)
            C.wait(v, "vecf", sems)
            v.tensor_tensor(bb_cols[:], vc["bin"][:], vc["bvec"][:], ALU.mult)
            i = v.tensor_scalar(nd_colsb[:], vc["dd"][:], -1.0, None, ALU.mult)
            C.inc(i, "V", sems, 1, label="p:vecd")
            for c in range(NCH + 2):
                if c >= 2:
                    cc = c - 2
                    C.wait(v, f"a1:replmu:{cc}", sems)
                    C.wait(v, f"a1:repls:{cc}", sems)
                    pm, ps_ = PB[2 + (cc % 2) * 2], PB[3 + (cc % 2) * 2]
                    for s in range(NS):
                        v.tensor_tensor(tn1[s % 2][:], chunk(x_sb[s], cc),
                                        pm[:], ALU.subtract)
                        v.tensor_tensor(tn1[s % 2][:], tn1[s % 2][:], ps_[:],
                                        ALU.mult)
                        v.tensor_scalar(tn1[s % 2][:], tn1[s % 2][:],
                                        col(vc["gam"], s), col(vc["bet"], s),
                                        ALU.mult, ALU.add)
                        k = cc * NS + s
                        i = v.tensor_reduce(mslots[:, k:k + 1], tn1[s % 2][:],
                                            AXX, ALU.max,
                                            apply_absolute_value=True)
                        C.inc(i, "V", sems, 1, label=f"a1:sl:{cc}:{s}")
                if c <= 3:
                    for k in range(4 * c, 4 * c + 4):
                        _wpass1(k)
                if 1 <= c <= NCH:
                    cc = c - 1
                    C.wait(v, f"a1:sg:{cc}", sems)
                    if cc >= 1:
                        C.wait(v, f"a1:sr:{cc - 1}", sems)
                    i = v.reciprocal(dvt, dvin)
                    nv = C.inc(i, "V", sems, 1)
                    v.wait_ge(sems["V"], nv)
                    i = v.tensor_scalar(dvout, dvt, 1.0, None, ALU.mult)
                    C.inc(i, "V", sems, 1, label=f"a1:rc:{cc}")
                if c == 4:
                    def _ss(i):
                        nv = C.inc(i, "V", sems, 1)
                        v.wait_ge(sems["V"], nv)
                        return nv
                    C.wait(v, "p:wsmm", sems)
                    _ss(v.tensor_copy(wred[0:1, 0:1], PB[7][0:1, 2:3]))
                    _ss(v.tensor_scalar(beta_c[0:1, :], wred[0:1, 0:1],
                                        1.0 / (D * D), None, ALU.mult))
                    _ss(v.tensor_scalar(stg_b, beta_c[0:1, :], 0.5, None,
                                        ALU.mult))
                    _ss(v.tensor_scalar(inv_t[0:1, :], beta_c[0:1, :], 1e-5,
                                        None, ALU.add))
                    _ss(v.reciprocal(mx2[0:1, :], inv_t[0:1, :]))
                    i = v.tensor_scalar(stg_a, mx2[0:1, :], 1.0, None, ALU.mult)
                    nv = C.inc(i, "V", sems, 1, label="p:invrow")
                    v.wait_ge(sems["V"], nv)
                    C.wait(v, "p:invmm", sems)
                    i = v.tensor_copy(cpk[:, 7:9], PB[7][:, 0:2])
                    nv = C.inc(i, "V", sems, 1, label="p:inv")
                    v.wait_ge(sems["V"], nv)
                if 5 <= c <= 8:
                    for k in range(4 * (c - 5), 4 * (c - 4)):
                        C.wait(v, f"p:w2:{k}", sems)
                        v.tensor_scalar(wstage[:], wstage[:], col(inv_c, 0),
                                        None, ALU.mult)
                        half = w1[k // 2][:, (k % 2) * TC:(k % 2 + 1) * TC]
                        v.tensor_scalar(wsb[:], wstage[:], -0.5, None, ALU.is_le)
                        v.tensor_scalar(half, wstage[:], 0.5, None, ALU.is_ge)
                        i = v.tensor_tensor(half, half, wsb[:], ALU.subtract)
                        C.inc(i, "V", sems, 1, label=f"p:wq:{k}")
                if c <= NCH - 1:
                    C.wait(v, f"a1:stat:{c}", sems)
                    if c >= 1:
                        C.wait(v, f"a1:replmu:{c - 1}", sems)
                        C.wait(v, f"a1:mst:{c - 1}", sems)
                    v.tensor_scalar(mrow, PB[0][0:1, :], INV_D, None, ALU.mult)
                    i = v.tensor_scalar(vrow, PB[1][0:1, :], FFN_A, FFN_B,
                                        ALU.mult, ALU.add)
                    nv = C.inc(i, "V", sems, 1)
                    v.wait_ge(sems["V"], nv)
                    i = v.scalar_tensor_tensor(qrow, mrow, FFN_M, mrow,
                                               ALU.mult, ALU.mult)
                    nv = C.inc(i, "V", sems, 1)
                    v.wait_ge(sems["V"], nv)
                    i = v.tensor_tensor(vrow, vrow, qrow, ALU.subtract)
                    C.inc(i, "V", sems, 1, label=f"a1:math1:{c}")
            v.wait_ge(sems["V"], C.cnt["V"])
            i = v.tensor_reduce(mx1[:], mslots[:], AXX, ALU.max)
            nv = C.inc(i, "V", sems, 1, label="a1:slots")
            v.wait_ge(sems["V"], nv)
            C.wait(v, "a1:gmax", sems)
            i = v.tensor_scalar(stg_a, gsl1[0:1, 0:1], THR_C, None, ALU.mult)
            nv = C.inc(i, "V", sems, 1)
            v.wait_ge(sems["V"], nv)
            i = v.tensor_scalar(stg_b, stg_a, -1.0, None, ALU.mult)
            nv = C.inc(i, "V", sems, 1, label="thr1v")
            v.wait_ge(sems["V"], nv)

        @blk.scalar
        def _(sc):
            def _casts(c):
                C.wait(sc, f"lxf:{c}", sems)
                for s in range(NS):
                    if c >= 2:
                        C.wait(sc, f"a1:statmm:{c - 2}:{s}", sems)
                    sc.copy(bufX[c % 2][s][:], chunk(x_sb[s], c))
                    if c >= 1:
                        C.wait(sc, f"a1:statmm:{c - 1}:{s}", sems)
                    i = sc.activation(sqv[s][:], bufX[c % 2][s][:],
                                      ACT.Square)
                    C.inc(i, "S", sems, 1, label=f"a1:cast:{c}:{s}")

            def _dance_back(cc):
                C.wait(sc, f"a1:rc:{cc}", sems)
                if cc >= 1:
                    C.wait(sc, f"a1:repls:{cc - 1}", sems)
                    C.wait(sc, f"a1:sst:{cc - 1}", sems)
                sc.dma_start(out=srow_s, in_=dvout,
                             allow_slow_non_contiguous=True
                             ).then_inc(sems["RS"], 16)
                rs = C.inc(None, "RS", sems, 16)
                sc.wait_ge(sems["RS"], rs)
                i = sc.nop()
                C.inc(i, "S", sems, 1, label=f"a1:sr:{cc}")

            _casts(0)
            C.wait(sc, "vecf", sems)
            i = sc.activation(a_cols[:], vc["la"][:], ACT.Exp)
            C.inc(i, "S", sems, 1, label="p:exp")
            for c in range(NCH):
                if c + 1 <= NCH - 1:
                    _casts(c + 1)
                if c >= 1:
                    _dance_back(c - 1)
                C.wait(sc, f"a1:math1:{c}", sems)
                i = sc.activation(vrow, vrow, ACT.Sqrt)
                ns = C.inc(i, "S", sems, 1)
                sc.wait_ge(sems["S"], ns)
                sc.dma_start(out=dvin, in_=vrow,
                             allow_slow_non_contiguous=True
                             ).then_inc(sems["RS"], 16)
                rs = C.inc(None, "RS", sems, 16)
                sc.wait_ge(sems["RS"], rs)
                i = sc.nop()
                C.inc(i, "S", sems, 1, label=f"a1:sg:{c}")
            _dance_back(NCH - 1)

        @blk.tensor
        def _(t):
            C.wait(t, "p:ones", sems)
            C.wait(t, "p:ones1", sems)
            for c in range(NCH + 2):
                if c >= 2 and c - 2 <= NCH - 1:
                    cc = c - 2
                    C.wait(t, f"a1:sr:{cc}", sems)
                    if cc >= 2:
                        C.wait(t, f"a1:sl:{cc - 2}:{NS - 1}", sems)
                    i = t.matmul(PB[3 + (cc % 2) * 2][:], ones1, srow_s,
                                 start=True, stop=True)
                    C.inc(i, "T", sems, 1, label=f"a1:repls:{cc}")
                if c <= NCH - 1:
                    for s in range(NS):
                        C.wait(t, f"a1:cast:{c}:{s}", sems)
                        if c >= 1 and s == 0:
                            C.wait(t, f"a1:math1:{c - 1}", sems)
                        t.matmul(PB[0][:], ones_bf[:], bufX[c % 2][s][:],
                                 start=(s == 0), stop=(s == NS - 1))
                        i = t.matmul(PB[1][:], ones_bf[:], sqv[s][:],
                                     start=(s == 0), stop=(s == NS - 1))
                        C.inc(i, "T", sems, 1, label=f"a1:statmm:{c}:{s}")
                    C.mark("T", f"a1:stat:{c}")
                if c == 3:
                    C.wait(t, f"p:wr1:{2 * NS - 1}", sems)
                    i = t.matmul(PB[7][0:1, 2:3], wsum, cpk[:, 15:16],
                                 start=True, stop=True)
                    C.inc(i, "T", sems, 1, label="p:wsmm")
                if c == 4:
                    C.wait(t, "p:invrow", sems)
                    i = t.matmul(PB[7][:, 0:2], ones1, cpk[0:1, 13:15],
                                 start=True, stop=True)
                    C.inc(i, "T", sems, 1, label="p:invmm")
                if c >= 1 and c - 1 <= NCH - 1:
                    cc = c - 1
                    C.wait(t, f"a1:math1:{cc}", sems)
                    if cc >= 2:
                        C.wait(t, f"a1:sl:{cc - 2}:{NS - 1}", sems)
                    i = t.matmul(PB[2 + (cc % 2) * 2][:], ones1, mrow,
                                 start=True, stop=True)
                    C.inc(i, "T", sems, 1, label=f"a1:replmu:{cc}")

        for e in ("V", "S", "T", "G"):
            C.mark(e, f"end:A1:{e}")

        def barrier(eng, sweep):
            for e in ("V", "S", "T", "G"):
                C.wait(eng, f"end:{sweep}:{e}", sems)

        blk.__exit__(None, None, None)
        blk2 = nc.Block(); blk2.__enter__()
        mrowB = scr3[0:1, :]
        vrowB = vrow
        qrowB = qrow

        @blk2.sync
        def _(sy):
            def _b1_mst(cc):
                C.wait(sy, f"b1:math1:{cc}", sems)
                sy.dma_start(out=cmp_mu[:, 4 * cc:4 * cc + 4],
                             in_=mrowB).then_inc(sems["RM"], 16)
                C.inc(None, "RM", sems, 16, label=f"b1:mst:{cc}")

            for c in range(NCH):
                if c == 0:
                    C.wait(sy, f"a1:sst:{NCH - 1}", sems)
                    C.wait(sy, f"a1:mst:{NCH - 1}", sems)
                else:
                    C.wait(sy, f"b1:repl:{c - 1}", sems)
                sy.dma_start(out=srow[:], in_=cmp_s["x"][:, 4 * c:4 * c + 4]
                             ).then_inc(sems["RQ"], 16)
                C.inc(None, "RQ", sems, 16)
                sy.dma_start(out=m2row[:], in_=cmp_m2["x"][:, 4 * c:4 * c + 4]
                             ).then_inc(sems["RQ"], 16)
                C.inc(None, "RQ", sems, 16, label=f"b1:rows:{c}")
                if c >= 3:
                    _b1_mst(c - 3)
            for cc in range(NCH - 3, NCH):
                _b1_mst(cc)

        @blk2.tensor
        def _(t):
            barrier(t, "A1")
            C.wait(t, "thr1v", sems)
            i = t.matmul(PB[6][:, 0:2], ones1, cpk[0:1, 13:15],
                         start=True, stop=True)
            C.inc(i, "T", sems, 1, label="thr1mm")
            for c in range(NCH + 2):
                if c <= NCH - 1:
                    C.wait(t, f"b1:rows:{c}", sems)
                    if c >= 2:
                        C.wait(t, f"b1:xq:{c - 2}", sems)
                    r1, r2 = PB[2 + (c % 2) * 2], PB[3 + (c % 2) * 2]
                    t.matmul(r1[:], ones1, srow[:], start=True, stop=True)
                    i = t.matmul(r2[:], ones1, m2row[:], start=True, stop=True)
                    C.inc(i, "T", sems, 1, label=f"b1:repl:{c}")
                if c >= 2:
                    cc = c - 2
                    for s in range(NS):
                        C.wait(t, f"b1:cast1:{cc}:{s}", sems)
                        if cc >= 1 and s == 0:
                            C.wait(t, f"b1:math1:{cc - 1}", sems)
                        t.matmul(PB[0][:], ones_bf[:], bufX[cc % 2][s][:],
                                 start=(s == 0), stop=(s == NS - 1))
                        i = t.matmul(PB[1][:], ones_bf[:], sqv[s][:],
                                     start=(s == 0), stop=(s == NS - 1))
                        C.inc(i, "T", sems, 1, label=f"b1:statmm:{cc}:{s}")
                    C.mark("T", f"b1:stat:{cc}")
                if 1 <= c <= NCH:
                    cc = c - 1
                    C.wait(t, f"b1:xq:{cc}", sems)
                    i = None
                    for e in range(NS):
                        k = cc * NS + e
                        if k >= 2:
                            C.wait(t, f"b1:evac:{(k - 2) // NS}:{(k - 2) % NS}",
                                   sems)
                        for s in range(NS):
                            i = t.matmul(PB[6 + e % 2][:],
                                         w1[s][:, e * 128:(e + 1) * 128],
                                         bufX[cc % 2][s][:], start=(s == 0),
                                         stop=(s == NS - 1))
                        C.inc(i, "T", sems, 1, label=f"b1:mm:{cc}:{e}")
            C.mark("T", "b1:mmall")

        @blk2.vector
        def _(v):
            barrier(v, "A1")
            C.wait(v, "thr1mm", sems)
            i = v.tensor_copy(cpk[:, 9:11], PB[6][:, 0:2])
            nv = C.inc(i, "V", sems, 1)
            v.wait_ge(sems["V"], nv)
            for c in range(NCH + 3):
                if c <= NCH:
                    cc = c - 1
                    if c <= NCH - 1:
                        C.wait(v, f"b1:repl:{c}", sems)
                        if c >= 2:
                            C.wait(v, f"b1:statmm:{c - 2}:{NS - 1}", sems)
                            if DEBUG and c - 2 == 4:
                                C.wait(v, f"b1:dbgbx:4", sems)
                    r1, r2 = PB[2 + (c % 2) * 2], PB[3 + (c % 2) * 2]
                    i = None
                    for s in range(NS):
                        if c <= NCH - 1:
                            v.tensor_tensor(b_sb[:], chunk(x_sb[s], c), r2[:],
                                            ALU.subtract)
                            v.tensor_tensor(s_rep[:], b_sb[:], r1[:], ALU.mult)
                            v.tensor_scalar(b_sb[:], s_rep[:],
                                            col(vc["gam"], s),
                                            col(vc["bet"], s),
                                            ALU.mult, ALU.add)
                            i = v.tensor_scalar(bufX[c % 2][s][:], b_sb[:],
                                                col(thr1, 0), col(nthr1, 0),
                                                ALU.min, ALU.max)
                            if s == NS - 1:
                                C.inc(i, "V", sems, 1, label=f"b1:xq:{c}")
                        if 1 <= c:
                            C.wait(v, f"b1:mm:{cc}:{s}", sems)
                            i = v.scalar_tensor_tensor(chunk(x_sb[s], cc),
                                                       PB[6 + s % 2][:],
                                                       col(hbeta_c, 0),
                                                       chunk(x_sb[s], cc),
                                                       ALU.mult, ALU.add)
                            C.inc(i, "V", sems, 1, label=f"b1:evac:{cc}:{s}")
                if 3 <= c <= NCH + 2:
                    cc = c - 3
                    C.wait(v, f"b1:sg:{cc}", sems)
                    if cc >= 1:
                        C.wait(v, f"b1:sr:{cc - 1}", sems)
                    i = v.reciprocal(dvt, dvin)
                    nv = C.inc(i, "V", sems, 1)
                    v.wait_ge(sems["V"], nv)
                    i = v.tensor_scalar(dvout, dvt, 1.0, None, ALU.mult)
                    C.inc(i, "V", sems, 1, label=f"b1:rc:{cc}")
                if 2 <= c <= NCH + 1:
                    cc = c - 2
                    C.wait(v, f"b1:stat:{cc}", sems)
                    if cc >= 1:
                        C.wait(v, f"b1:mst:{cc - 1}", sems)
                    v.tensor_scalar(mrowB, PB[0][0:1, :], INV_D, None, ALU.mult)
                    i = v.tensor_scalar(vrowB, PB[1][0:1, :], SSM_A, SSM_B,
                                        ALU.mult, ALU.add)
                    nv = C.inc(i, "V", sems, 1)
                    v.wait_ge(sems["V"], nv)
                    i = v.scalar_tensor_tensor(qrowB, mrowB, SSM_M, mrowB,
                                               ALU.mult, ALU.mult)
                    nv = C.inc(i, "V", sems, 1)
                    v.wait_ge(sems["V"], nv)
                    i = v.tensor_tensor(vrowB, vrowB, qrowB, ALU.subtract)
                    C.inc(i, "V", sems, 1, label=f"b1:math1:{cc}")
                    if DEBUG:
                        if cc >= 1:
                            C.wait(v, f"b1:dbgst:{cc - 1}", sems)
                        i = v.tensor_copy(qrowB, PB[0][0:1, :])
                        C.inc(i, "V", sems, 1, label=f"b1:dbgq:{cc}")
            C.wait(v, f"b1:sr:{NCH - 1}", sems)
            C.wait(v, f"b1:mst:{NCH - 1}", sems)
            i = v.tensor_tensor(cmp_m2["x1"][:], cmp_mu[:], cmp_s["x1"][:],
                                ALU.mult)
            nv = C.inc(i, "V", sems, 1, label="b1:m2c")
            v.wait_ge(sems["V"], nv)

        @blk2.scalar
        def _(sc):
            barrier(sc, "A1")
            for c in range(1, NCH + 4):
                if 1 <= c <= NCH:
                    cc = c - 1
                    for s in range(NS):
                        C.wait(sc, f"b1:evac:{cc}:{s}", sems)
                        C.wait(sc, f"b1:mm:{cc}:{NS - 1}", sems)
                        sc.copy(bufX[cc % 2][s][:], chunk(x_sb[s], cc))
                        if cc >= 1:
                            C.wait(sc, f"b1:statmm:{cc - 1}:{s}", sems)
                        i = sc.activation(sqv[s][:], bufX[cc % 2][s][:],
                                          ACT.Square)
                        C.inc(i, "S", sems, 1, label=f"b1:cast1:{cc}:{s}")
                if 3 <= c <= NCH + 2:
                    cc = c - 3
                    C.wait(sc, f"b1:rc:{cc}", sems)
                    sc.dma_start(out=cmp_s["x1"][:, 4 * cc:4 * cc + 4],
                                 in_=dvout, allow_slow_non_contiguous=True
                                 ).then_inc(sems["RS"], 16)
                    rs = C.inc(None, "RS", sems, 16)
                    sc.wait_ge(sems["RS"], rs)
                    i = sc.nop()
                    C.inc(i, "S", sems, 1, label=f"b1:sr:{cc}")
                if 2 <= c <= NCH + 1:
                    cc = c - 2
                    C.wait(sc, f"b1:math1:{cc}", sems)
                    i = sc.activation(vrowB, vrowB, ACT.Sqrt)
                    ns = C.inc(i, "S", sems, 1)
                    sc.wait_ge(sems["S"], ns)
                    sc.dma_start(out=dvin, in_=vrowB,
                                 allow_slow_non_contiguous=True
                                 ).then_inc(sems["RS"], 16)
                    rs = C.inc(None, "RS", sems, 16)
                    sc.wait_ge(sems["RS"], rs)
                    i = sc.nop()
                    C.inc(i, "S", sems, 1, label=f"b1:sg:{cc}")

        @blk2.gpsimd
        def _(gp):
            barrier(gp, "A1")
            if DEBUG:
                C.wait(gp, "b1:stat:4", sems)
                for s in range(NS):
                    gp.dma_start(out=dbga[:, s * TC:(s + 1) * TC],
                                 in_=bufX[0][s][:]).then_inc(sems["GQ"], 16)
                    C.inc(None, "GQ", sems, 16,
                          label=f"b1:dbgbx:4" if s == NS - 1 else None)
            C.wait(gp, "b1:mmall", sems)
            for s in range(NS):
                gp.dma_start(out=w1[s][:], in_=cmat[s * 128:(s + 1) * 128, :]
                             ).then_inc(sems["GQ"], 16)
                C.inc(None, "GQ", sems, 16)
            gp.wait_ge(sems["GQ"], C.cnt["GQ"])
            i = gp.nop()
            C.inc(i, "F", sems, 1, label="lcmatf")

        for e in ("V", "S", "T", "G"):
            C.mark(e, f"end:B1:{e}")

        blk2.__exit__(None, None, None)
        blk3 = nc.Block(); blk3.__enter__()
        mrowS = m2row[0:1, :]
        vrowS = scr2[0:1, :]
        qrowS = wstage[0:1, :]

        @blk3.sync
        def _(sy):
            if DEBUG:
                for e in ("V", "S", "T", "G"):
                    C.wait(sy, f"end:B1:{e}", sems)
                for s in range(NS):
                    sy.dma_start(out=dbg1[s * 128:(s + 1) * 128, :],
                                 in_=x_sb[s][:]).then_inc(sems["RQ"], 16)
                    C.inc(None, "RQ", sems, 16)
            def _ssm_mst(cc):
                C.wait(sy, f"ssm:math1:{cc}", sems)
                sy.dma_start(out=cmp_m2["x2"][:, 4 * cc:4 * cc + 4],
                             in_=mrowS).then_inc(sems["RM"], 16)
                C.inc(None, "RM", sems, 16, label=f"ssm:mst:{cc}")

            for c in range(NCH):
                if c == 0:
                    C.wait(sy, f"b1:sr:{NCH - 1}", sems)
                    C.wait(sy, "b1:m2c", sems)
                else:
                    C.wait(sy, f"ssm:repl:{c - 1}", sems)
                if c >= 3:
                    C.wait(sy, f"ssm:rep3:{c - 3}", sems)
                sy.dma_start(out=srow[:], in_=cmp_s["x1"][:, 4 * c:4 * c + 4]
                             ).then_inc(sems["RQ"], 16)
                C.inc(None, "RQ", sems, 16, label=f"ssm:rows:{c}")
                if c >= 2:
                    _ssm_mst(c - 2)
            _ssm_mst(NCH - 2)
            _ssm_mst(NCH - 1)

        @blk3.gpsimd
        def _(gp):
            barrier(gp, "B1")
            for c in range(NCH + 2):
                if c <= NCH - 1:
                    if c > 0:
                        C.wait(gp, f"ssm:umm:{c - 1}:{NS - 1}", sems)
                        C.wait(gp, f"ssm:ymm:{c - 1}:{NS - 1}", sems)
                    gp.dma_start(out=murow_b[:], in_=cmp_mu[:, 4 * c:4 * c + 4]
                                 ).then_inc(sems["GQ"], 16)
                    C.inc(None, "GQ", sems, 16)
                    gp.dma_start(out=m2row_b[:],
                                 in_=cmp_m2["x1"][:, 4 * c:4 * c + 4]
                                 ).then_inc(sems["GQ"], 16)
                    C.inc(None, "GQ", sems, 16, label=f"ssm:browz:{c}")
                if c >= 2:
                    cc = c - 2
                    for s in range(NS):
                        k = cc * NS + s
                        C.wait(gp, f"ssm:ab:{cc}:{s}", sems)
                        if cc == 0 and s == 0:
                            C.wait(gp, "thr1v", sems)
                        i = gp.tensor_reduce(gsl2[0:1, k:k + 1], tn1[s % 2][:],
                                             mybir.AxisListType.XYZWC, ALU.max)
                        C.inc(i, "G", sems, 1, label=f"ssm:red:{cc}:{s}")
                if c <= NCH - 1:
                    for f in range(NS):
                        C.wait(gp, f"ssm:xy:{c}:{f}", sems)
                        i = gp.tensor_tensor(chunk(x_sb[f], c),
                                             chunk(x_sb[f], c),
                                             (scr1 if f % 2 == 0 else scr3)[:],
                                             ALU.add)
                        C.inc(i, "G", sems, 1, label=f"ssm:xt4:{c}:{f}")
            C.wait(gp, "ssm:ulast", sems)
            gp.wait_ge(sems["GQ"], C.cnt["GQ"])
            for s in range(NS):
                gp.dma_start(out=w2[s][:], in_=wq_scr[s * 128:(s + 1) * 128, :]
                             ).then_inc(sems["GQ"], 16)
                C.inc(None, "GQ", sems, 16)
            gp.wait_ge(sems["GQ"], C.cnt["GQ"])
            i = gp.nop()
            C.inc(i, "F", sems, 1, label="lwq2f")

        @blk3.tensor
        def _(t):
            barrier(t, "B1")
            i = None
            for e in range(NS):
                if e == 0:
                    C.wait(t, "lwinf", sems)
                for s in range(NS):
                    i = t.matmul(PB[2][:, e:e + 1],
                                 w2[s][:, e * 128:(e + 1) * 128],
                                 onescol_b[:], start=(s == 0),
                                 stop=(s == NS - 1))
                C.inc(i, "T", sems, 1, label=f"ssm:ws:{e}")
            for c in range(NCH + 1):
                if c <= NCH - 1:
                    C.wait(t, f"ssm:rows:{c}", sems)
                    if c == 0:
                        C.wait(t, f"ssm:wsc:{NS - 1}", sems)
                    if c >= 2:
                        C.wait(t, f"ssm:srepdone:{c - 2}", sems)
                    i = t.matmul(PB[2 + c % 2][:], ones1, srow[:],
                                 start=True, stop=True)
                    C.inc(i, "T", sems, 1, label=f"ssm:repl:{c}")
                if c >= 1:
                    cc = c - 1
                    for s in range(NS):
                        C.wait(t, f"ssm:cpy:{cc}:{s}", sems)
                        if cc >= 2 and s == 0:
                            C.wait(t, f"ssm:xn2:{cc - 2}:{NS - 1}", sems)
                        i = t.matmul(PB[0][:], ones_bf[:], bufX[cc % 2][s][:],
                                     start=(s == 0), stop=(s == NS - 1))
                        C.inc(i, "T", sems, 1, label=f"ssm:st0:{cc}:{s}")
                    for s in range(NS):
                        C.wait(t, f"ssm:cast2:{cc}:{s}", sems)
                        i = t.matmul(PB[1][:], ones_bf[:], bufX[cc % 2][s][:],
                                     start=(s == 0), stop=(s == NS - 1))
                        C.inc(i, "T", sems, 1, label=f"ssm:statmm:{cc}:{s}")
                    C.mark("T", f"ssm:stat:{cc}")
                if c > NCH - 1:
                    continue
                for s in range(NS):
                    C.wait(t, f"ssm:cast1:{c}:{s}", sems)
                C.wait(t, f"ssm:browz:{c}", sems)
                if c == 0:
                    C.wait(t, "p:vecws", sems)
                for e in range(NS):
                    k = c * NS + e
                    if e == 4 and c >= 2:
                        cc = c - 2
                        C.wait(t, f"ssm:sr:{cc}", sems)
                        C.wait(t, f"ssm:math1:{c - 1}", sems)
                        t.matmul(PB[0][:], ones1, srow[:], start=True,
                                 stop=True)
                        i = t.matmul(PB[1][:], ones1, mrowS, start=True,
                                     stop=True)
                        C.inc(i, "T", sems, 1, label=f"ssm:rep3:{cc}")
                    if k >= 2:
                        C.wait(t, f"ssm:b:{(k - 2) // NS}:{(k - 2) % NS}", sems)
                    for s in range(NS):
                        t.matmul(PB[4 + e % 2][:],
                                 w2[s][:, e * 128:(e + 1) * 128],
                                 bufX[c % 2][s][:], start=(s == 0), stop=False)
                    i = t.matmul(PB[4 + e % 2][:], nwrow[e][:], murow_b[:],
                                 start=False, stop=True)
                    C.inc(i, "T", sems, 1, label=f"ssm:umm:{c}:{e}")
                if c == NCH - 1:
                    C.mark("T", "ssm:ulast")
                for f in range(NS):
                    k = c * NS + f
                    C.wait(t, f"ssm:hhat:{c}:{NS - 1}", sems)
                    if k >= 2:
                        C.wait(t, f"ssm:xy:{(k - 2) // NS}:{(k - 2) % NS}",
                               sems)
                    if c == 0 and f == 0:
                        C.wait(t, "lcmatf", sems)
                    for e in range(NS):
                        t.matmul(PB[6 + f % 2][:],
                                 w1[e][:, f * 128:(f + 1) * 128],
                                 bufH[e][:], start=(e == 0), stop=False)
                    i = t.matmul(PB[6 + f % 2][:], ndrow[f][:], m2row_b[:],
                                 start=False, stop=True)
                    C.inc(i, "T", sems, 1, label=f"ssm:ymm:{c}:{f}")
            for cc in (NCH - 2, NCH - 1):
                C.wait(t, f"ssm:sr:{cc}", sems)
                C.wait(t, f"ssm:math1:{NCH - 1}", sems)
                if cc == NCH - 2:
                    C.wait(t, f"ssm:xn2:{NCH - 3}:{NS - 1}", sems)
                else:
                    C.wait(t, f"ssm:xn2:{NCH - 2}:{NS - 1}", sems)
                t.matmul(PB[0][:], ones1, srow[:], start=True, stop=True)
                i = t.matmul(PB[1][:], ones1, mrowS, start=True, stop=True)
                C.inc(i, "T", sems, 1, label=f"ssm:rep3:{cc}")

        @blk3.vector
        def _(v):
            barrier(v, "B1")
            for e in range(NS):
                C.wait(v, f"ssm:ws:{e}", sems)
                i = v.tensor_scalar(col(ws_cols, e), PB[2][:, e:e + 1], -1.0,
                                    None, ALU.mult)
                C.inc(i, "V", sems, 1, label=f"ssm:wsc:{e}")
            for c in range(NCH + 2):
                if c <= NCH - 1:
                    C.wait(v, f"ssm:repl:{c}", sems)
                    i = v.tensor_copy(s_rep[:], PB[2 + c % 2][:])
                    nv = C.inc(i, "V", sems, 1, label=f"ssm:srepdone:{c}")
                    v.wait_ge(sems["V"], nv)
                if 2 <= c <= NCH + 1:
                    cc = c - 2
                    C.wait(v, f"ssm:sg:{cc}", sems)
                    if cc >= 1:
                        C.wait(v, f"ssm:sr:{cc - 1}", sems)
                    i = v.reciprocal(dvt, dvin)
                    nv = C.inc(i, "V", sems, 1)
                    v.wait_ge(sems["V"], nv)
                    i = v.tensor_scalar(dvout, dvt, 1.0, None, ALU.mult)
                    C.inc(i, "V", sems, 1, label=f"ssm:rc:{cc}")
                if 1 <= c <= NCH:
                    cc = c - 1
                    C.wait(v, f"ssm:stat:{cc}", sems)
                    if cc >= 1:
                        C.wait(v, f"ssm:mst:{cc - 1}", sems)
                    v.tensor_scalar(mrowS, PB[0][0:1, :], INV_D, None, ALU.mult)
                    i = v.tensor_scalar(vrowS, PB[1][0:1, :], FFN_A, FFN_B,
                                        ALU.mult, ALU.add)
                    nv = C.inc(i, "V", sems, 1)
                    v.wait_ge(sems["V"], nv)
                    i = v.scalar_tensor_tensor(qrowS, mrowS, FFN_M, mrowS,
                                               ALU.mult, ALU.mult)
                    nv = C.inc(i, "V", sems, 1)
                    v.wait_ge(sems["V"], nv)
                    i = v.tensor_tensor(vrowS, vrowS, qrowS, ALU.subtract)
                    C.inc(i, "V", sems, 1, label=f"ssm:math1:{cc}")
                if c <= NCH - 1:
                    for e in range(NS):
                        C.wait(v, f"ssm:umm:{c}:{e}", sems)
                        if e >= 1:
                            C.wait(v, f"ssm:hcast:{c}:{e - 1}", sems)
                        elif c >= 1:
                            C.wait(v, f"ssm:hcast:{c - 1}:{NS - 1}", sems)
                        v.scalar_tensor_tensor(b_sb[:], PB[4 + e % 2][:],
                                               col(vc["bvec"], e), s_rep[:],
                                               ALU.mult, ALU.mult)
                        i = v.tensor_scalar(b_sb[:], b_sb[:], col(bb_cols, e),
                                            None, ALU.add)
                        C.inc(i, "V", sems, 1, label=f"ssm:b:{c}:{e}")
                        a_bc = bass.AP(vct, NS + e, [[6 * NS, 128], [0, TC]])
                        v.tensor_tensor_scan(h_sb[:], a_bc, b_sb[:],
                                             col(carry, e), ALU.mult, ALU.add)
                        i = v.tensor_copy(col(carry, e), h_sb[:, TC - 1:TC])
                        nv = C.inc(i, "V", sems, 1, label=f"ssm:scan:{c}:{e}")
                        v.wait_ge(sems["V"], nv)
                if 2 <= c <= NCH + 1:
                    cc = c - 2
                    C.wait(v, f"ssm:rep3:{cc}", sems)
                    for s in range(NS):
                        k = cc * NS + s
                        if k >= 2:
                            C.wait(v, f"ssm:red:{(k - 2) // NS}:{(k - 2) % NS}",
                                   sems)
                        C.wait(v, f"ssm:xt4:{cc}:{s}", sems)
                        v.tensor_tensor(tn1[s % 2][:], chunk(x_sb[s], cc),
                                        PB[1][:], ALU.subtract)
                        i = v.tensor_tensor(tn1[s % 2][:], tn1[s % 2][:],
                                            PB[0][:], ALU.mult)
                        C.inc(i, "V", sems, 1, label=f"ssm:xn2:{cc}:{s}")
                if c <= NCH - 1:
                    for f in range(NS):
                        C.wait(v, f"ssm:ymm:{c}:{f}", sems)
                        if f >= 2:
                            C.wait(v, f"ssm:xt4:{c}:{f - 2}", sems)
                        elif c >= 1:
                            C.wait(v, f"ssm:xt4:{c - 1}:{f + NS - 2}", sems)
                        t4buf = scr1 if f % 2 == 0 else scr3
                        v.scalar_tensor_tensor(t4buf[:], chunk(x_sb[f], c),
                                               col(vc["dd"], f), s_rep[:],
                                               ALU.mult, ALU.mult)
                        i = v.tensor_tensor(chunk(x_sb[f], c),
                                            chunk(x_sb[f], c),
                                            PB[6 + f % 2][:], ALU.add)
                        C.inc(i, "V", sems, 1, label=f"ssm:xy:{c}:{f}")
            C.wait(v, f"ssm:red:{NCH - 1}:{NS - 1}", sems)
            i = v.tensor_reduce(mx1[0:1, :], gsl1[0:1, 0:64], AXX, ALU.max)
            nv = C.inc(i, "V", sems, 1)
            v.wait_ge(sems["V"], nv)
            i = v.tensor_scalar(stg_a, mx1[0:1, :], THR_C, None, ALU.mult)
            nv = C.inc(i, "V", sems, 1)
            v.wait_ge(sems["V"], nv)
            i = v.tensor_scalar(stg_b, stg_a, -1.0, None, ALU.mult)
            nv = C.inc(i, "V", sems, 1, label="thr2v")
            v.wait_ge(sems["V"], nv)

        @blk3.scalar
        def _(sc):
            barrier(sc, "B1")
            C.wait(sc, "p:vecd", sems)
            for e in range(NS):
                C.wait(sc, f"ssm:wsc:{e}", sems)
                sc.dma_start(out=nwrow[e][:], in_=col(ws_cols, e),
                             allow_slow_non_contiguous=True
                             ).then_inc(sems["RS"], 16)
                C.inc(None, "RS", sems, 16)
                sc.dma_start(out=ndrow[e][:], in_=col(nd_colsb, e),
                             allow_slow_non_contiguous=True
                             ).then_inc(sems["RS"], 16)
                C.inc(None, "RS", sems, 16, label=f"ssm:rowdma:{e}")
            sc.wait_ge(sems["RS"], C.cnt["RS"])
            i = sc.nop()
            C.inc(i, "S", sems, 1, label="p:vecws")
            for s in range(NS):
                i = sc.copy(bufX[0][s][:], chunk(x_sb[s], 0))
                C.inc(i, "S", sems, 1, label=f"ssm:cast1:0:{s}")
            for c in range(NCH + 2):
                if 1 <= c <= NCH:
                    cc = c - 1
                    for s in range(NS):
                        C.wait(sc, f"ssm:st0:{cc}:{s}", sems)
                        i = sc.activation(bufX[cc % 2][s][:],
                                          bufX[cc % 2][s][:], ACT.Square)
                        C.inc(i, "S", sems, 1, label=f"ssm:cast2:{cc}:{s}")
                if c + 1 <= NCH - 1:
                    for s in range(NS):
                        if c + 1 >= 2:
                            C.wait(sc, f"ssm:statmm:{c - 1}:{s}", sems)
                        i = sc.copy(bufX[(c + 1) % 2][s][:],
                                    chunk(x_sb[s], c + 1))
                        C.inc(i, "S", sems, 1, label=f"ssm:cast1:{c + 1}:{s}")
                if 2 <= c <= NCH + 1:
                    cc = c - 2
                    C.wait(sc, f"ssm:rc:{cc}", sems)
                    if cc + 2 <= NCH - 1:
                        C.wait(sc, f"ssm:repl:{cc + 2}", sems)
                    elif cc >= 1:
                        C.wait(sc, f"ssm:rep3:{cc - 1}", sems)
                    sc.dma_start(out=srow[:], in_=dvout,
                                 allow_slow_non_contiguous=True
                                 ).then_inc(sems["RS"], 16)
                    rs = C.inc(None, "RS", sems, 16)
                    sc.wait_ge(sems["RS"], rs)
                    sc.dma_start(out=cmp_s["x2"][:, 4 * cc:4 * cc + 4],
                                 in_=dvout, allow_slow_non_contiguous=True
                                 ).then_inc(sems["RS"], 16)
                    rs = C.inc(None, "RS", sems, 16)
                    sc.wait_ge(sems["RS"], rs)
                    i = sc.nop()
                    C.inc(i, "S", sems, 1, label=f"ssm:sr:{cc}")
                if 1 <= c <= NCH:
                    cc = c - 1
                    C.wait(sc, f"ssm:math1:{cc}", sems)
                    i = sc.activation(vrowS, vrowS, ACT.Sqrt)
                    ns = C.inc(i, "S", sems, 1)
                    sc.wait_ge(sems["S"], ns)
                    sc.dma_start(out=dvin, in_=vrowS,
                                 allow_slow_non_contiguous=True
                                 ).then_inc(sems["RS"], 16)
                    rs = C.inc(None, "RS", sems, 16)
                    sc.wait_ge(sems["RS"], rs)
                    i = sc.nop()
                    C.inc(i, "S", sems, 1, label=f"ssm:sg:{cc}")
                if c <= NCH - 1:
                    for e in range(NS):
                        C.wait(sc, f"ssm:scan:{c}:{e}", sems)
                        if c >= 1 and e == 0:
                            C.wait(sc, f"ssm:ymm:{c - 1}:{NS - 1}", sems)
                        i = sc.copy(bufH[e][:], h_sb[:])
                        C.inc(i, "S", sems, 1, label=f"ssm:hhat:{c}:{e}")
                        C.mark("S", f"ssm:hcast:{c}:{e}")
                if 2 <= c <= NCH + 1:
                    cc = c - 2
                    for s in range(NS):
                        C.wait(sc, f"ssm:xn2:{cc}:{s}", sems)
                        i = sc.activation(tn1[s % 2][:], tn1[s % 2][:], ACT.Abs,
                                          bias=col(vc["bet"], s),
                                          scale=col(vc["gam"], s))
                        C.inc(i, "S", sems, 1, label=f"ssm:ab:{cc}:{s}")
                if c <= NCH - 1:
                    for s in range(NS):
                        C.wait(sc, f"ssm:xt4:{c}:{s}", sems)
                        C.wait(sc, f"ssm:umm:{c}:{NS - 1}", sems)
                        i = sc.copy(bufX[c % 2][s][:], chunk(x_sb[s], c))
                        C.inc(i, "S", sems, 1, label=f"ssm:cpy:{c}:{s}")

        for e in ("V", "S", "T", "G"):
            C.mark(e, f"end:SSM:{e}")

        blk3.__exit__(None, None, None)
        blk4 = nc.Block(); blk4.__enter__()

        @blk4.sync
        def _(sy):
            if DEBUG:
                for e in ("V", "S", "T", "G"):
                    C.wait(sy, f"end:SSM:{e}", sems)
                for s in range(NS):
                    sy.dma_start(out=dbg2[s * 128:(s + 1) * 128, :],
                                 in_=x_sb[s][:]).then_inc(sems["RQ"], 16)
                    C.inc(None, "RQ", sems, 16)
            def _b2_store(c):
                for e in range(NS):
                    C.wait(sy, f"b2:evac:{c}:{e}", sems)
                    sy.dma_start(out=outT[e * 128:(e + 1) * 128,
                                          c * TC:(c + 1) * TC],
                                 in_=chunk(x_sb[e], c)).then_inc(sems["ST"], 16)
                    C.inc(None, "ST", sems, 16, label=f"st:{c}:{e}")

            for c in range(NCH):
                if c == 0:
                    C.wait(sy, f"ssm:sr:{NCH - 1}", sems)
                    C.wait(sy, f"ssm:mst:{NCH - 1}", sems)
                    C.wait(sy, f"ssm:rep3:{NCH - 1}", sems)
                else:
                    C.wait(sy, f"b2:repl:{c - 1}", sems)
                sy.dma_start(out=srow[:], in_=cmp_s["x2"][:, 4 * c:4 * c + 4]
                             ).then_inc(sems["RQ"], 16)
                C.inc(None, "RQ", sems, 16)
                sy.dma_start(out=m2row[:], in_=cmp_m2["x2"][:, 4 * c:4 * c + 4]
                             ).then_inc(sems["RQ"], 16)
                C.inc(None, "RQ", sems, 16, label=f"b2:rows:{c}")
                if c >= 1:
                    _b2_store(c - 1)
            _b2_store(NCH - 1)

        def _b2_big(t, c):
            C.wait(t, f"b2:xq:{c}", sems)
            i = None
            for e in range(NS):
                k = c * NS + e
                if k >= 2:
                    C.wait(t, f"b2:evac:{(k - 2) // NS}:{(k - 2) % NS}", sems)
                for s in range(NS):
                    i = t.matmul(PB[6 + e % 2][:], w2[s][:, e * 128:(e + 1) * 128],
                                 bufX[c % 2][s][:], start=(s == 0),
                                 stop=(s == NS - 1))
                C.inc(i, "T", sems, 1, label=f"b2:mm:{c}:{e}")

        @blk4.tensor
        def _(t):
            barrier(t, "SSM")
            C.wait(t, "thr2v", sems)
            i = t.matmul(PB[6][:, 0:2], ones1, cpk[0:1, 13:15],
                         start=True, stop=True)
            C.inc(i, "T", sems, 1, label="thr2mm")
            C.wait(t, "lwq2f", sems)
            for c in range(NCH):
                C.wait(t, f"b2:rows:{c}", sems)
                if c >= 2:
                    C.wait(t, f"b2:xq:{c - 2}", sems)
                r1, r2 = PB[2 + (c % 2) * 2], PB[3 + (c % 2) * 2]
                t.matmul(r1[:], ones1, srow[:], start=True, stop=True)
                i = t.matmul(r2[:], ones1, m2row[:], start=True, stop=True)
                C.inc(i, "T", sems, 1, label=f"b2:repl:{c}")
                C.mark("T", f"b2:replq:{c}")
                if c >= 1:
                    _b2_big(t, c - 1)
            _b2_big(t, NCH - 1)

        @blk4.vector
        def _(v):
            barrier(v, "SSM")
            C.wait(v, "thr2mm", sems)
            i = v.tensor_copy(cpk[:, 11:13], PB[6][:, 0:2])
            nv = C.inc(i, "V", sems, 1)
            v.wait_ge(sems["V"], nv)
            for c in range(NCH + 1):
                cc = c - 1
                if c <= NCH - 1:
                    C.wait(v, f"b2:repl:{c}", sems)
                    if c >= 2:
                        C.wait(v, f"b2:mm:{c - 2}:{NS - 1}", sems)
                r1, r2 = PB[2 + (c % 2) * 2], PB[3 + (c % 2) * 2]
                i = None
                for s in range(NS):
                    if c <= NCH - 1:
                        v.tensor_tensor(b_sb[:], chunk(x_sb[s], c), r2[:],
                                        ALU.subtract)
                        v.tensor_tensor(s_rep[:], b_sb[:], r1[:], ALU.mult)
                        v.tensor_scalar(b_sb[:], s_rep[:], col(vc["gam"], s),
                                        col(vc["bet"], s), ALU.mult, ALU.add)
                        i = v.tensor_scalar(bufX[c % 2][s][:], b_sb[:],
                                            col(thr2, 0), col(nthr2, 0),
                                            ALU.min, ALU.max)
                        if s == NS - 1:
                            C.inc(i, "V", sems, 1, label=f"b2:xq:{c}")
                    if c >= 1:
                        C.wait(v, f"b2:mm:{cc}:{s}", sems)
                        i = v.scalar_tensor_tensor(chunk(x_sb[s], cc),
                                                   PB[6 + s % 2][:],
                                                   col(hbeta_c, 0),
                                                   chunk(x_sb[s], cc),
                                                   ALU.mult, ALU.add)
                        C.inc(i, "V", sems, 1, label=f"b2:evac:{cc}:{s}")

        @blk4.gpsimd
        def _(gp):
            barrier(gp, "SSM")
            gp.wait_ge(sems["ST"], 16 * NCH * NS)
            if DEBUG:
                w = NCH * 4
                for j, tt in enumerate([cmp_s["x"], cmp_m2["x"], cmp_s["x1"],
                                        cmp_m2["x1"], cmp_mu, cmp_s["x2"],
                                        cmp_m2["x2"]]):
                    gp.dma_start(out=dbgs[:, j * w:(j + 1) * w], in_=tt[:]
                                 ).then_inc(sems["GQ"], 16)
                    C.inc(None, "GQ", sems, 16)
                gp.dma_start(out=dbgs[:, 7 * w:7 * w + 16], in_=cpk[:]
                             ).then_inc(sems["GQ"], 16)
                C.inc(None, "GQ", sems, 16)
                gp.dma_start(out=dbgs[:, 7 * w + 16:7 * w + 16 + NS],
                             in_=carry[:]).then_inc(sems["GQ"], 16)
                C.inc(None, "GQ", sems, 16)
                gp.wait_ge(sems["GQ"], C.cnt["GQ"])

        blk4.__exit__(None, None, None)

    return C.out


def _build():
    nc1 = bass.Bass("TRN2", target_bir_lowering=False, detect_race_conditions=False)
    labels = _emit(nc1, {}, strict=False)
    nc = bass.Bass("TRN2", target_bir_lowering=False, detect_race_conditions=False)
    labels2 = _emit(nc, labels, strict=True)
    return nc


_BUILT = {}


def kernel(L0_attn_x, L0_attn_W_in, L0_attn_b_in, L0_attn_log_A, L0_attn_B_vec,
           L0_attn_C, L0_attn_D, L0_ffn_weight, L0_ffn_ln_gamma, L0_ffn_ln_beta):
    if "nc" not in _BUILT:
        _BUILT["nc"] = _build()
    nc = _BUILT["nc"]
    vecs = np.stack([
        np.asarray(L0_attn_b_in), np.asarray(L0_attn_log_A),
        np.asarray(L0_attn_B_vec), np.asarray(L0_attn_D),
        np.asarray(L0_ffn_ln_gamma), np.asarray(L0_ffn_ln_beta)
    ]).astype(np.float32, copy=False)
    vecs = np.ascontiguousarray(
        vecs.reshape(6, 8, 128).transpose(2, 0, 1).reshape(128, 48))
    wffn = np.ascontiguousarray(np.asarray(L0_ffn_weight, dtype=np.float32))
    win = np.ascontiguousarray(np.asarray(L0_attn_W_in, dtype=np.float32))
    cmat = np.ascontiguousarray(np.asarray(L0_attn_C, dtype=np.float32))
    x = np.asarray(L0_attn_x)
    in_maps = [{
        "xT": np.ascontiguousarray(x[i].T),
        "wffn": wffn, "win": win, "cmat": cmat, "vecs": vecs,
    } for i in range(x.shape[0])]
    res = run_bass_kernel_spmd(nc, in_maps, core_ids=list(range(len(in_maps))))
    return np.stack([np.ascontiguousarray(r["outT"].T) for r in res.results])
